# version 02
# baseline (speedup 1.0000x reference)
import contextlib
import numpy as np
import ml_dtypes

import concourse.bass as bass
import concourse.mybir as mybir
from concourse.bass_utils import run_bass_kernel_spmd

F32 = mybir.dt.float32
BF16 = mybir.dt.bfloat16
I32 = mybir.dt.int32
AF = mybir.ActivationFunctionType
ALU = mybir.AluOpType
AX = mybir.AxisListType

S = 8192
D = 4096
H = 16
HD = 256
ROT = 64
INTER = 11008
NCORE = 8
HPC = H // NCORE
TOK = S // NCORE
NCH = D // 128
NG = D // 128
NGD = INTER // 128
KP8 = D // 8
KP8D = INTER // 8
EPS = 1e-5


class SemCnt:
    def __init__(self, handle):
        self.h = handle
        self.v = 0


class Emitter:
    ENGINES = ("sync", "gpsimd", "vector", "scalar", "tensor")

    def __init__(self, own_sems=None):
        self.streams = {e: [] for e in self.ENGINES}
        self.last = {e: None for e in self.ENGINES}
        self.own = own_sems or {}
        self.all_sems = []

    def emit(self, eng, fn, waits=(), inc=None, inc_n=1):
        rec = [(s.h, v) for (s, v) in waits if v > 0]
        own = self.own.get(eng)
        if own is not None and own.v > 0 and fn is not None:
            rec.append((own.h, own.v))
        inc_h = None
        if inc is not None:
            inc.v += inc_n
            inc_h = inc.h
            self.last[eng] = (inc, inc.v)
        self.streams[eng].append((fn, rec, inc_h, inc_n))

    def mark_last(self, eng, inc, inc_n=1):
        fn, waits, inc_h, n = self.streams[eng][-1]
        assert inc_h is None and fn is not None
        inc.v += inc_n
        self.last[eng] = (inc, inc.v)
        self.streams[eng][-1] = (fn, waits, inc.h, inc_n)

    def barrier(self):
        marks = [(s, s.v) for s in self.all_sems if s.v > 0]
        for eng in self.ENGINES:
            if marks:
                self.emit(eng, None, waits=marks)

    def replay(self, eng_name, eng_obj):
        for fn, waits, inc_h, inc_n in self.streams[eng_name]:
            for (h, v) in waits:
                eng_obj.wait_ge(h, v)
            ins = eng_obj.nop() if fn is None else fn(eng_obj)
            if inc_h is not None:
                ins.then_inc(inc_h, inc_n)


def _rearr_cpt(ap, p=128):
    return ap.rearrange("(c p) t -> p c t", p=p)


def _down_segments(c2):
    segs = []
    p = 0
    while p < 128:
        kpp = 128 * c2 + p
        j = kpp // KP8D
        k8 = kpp % KP8D
        cnt = min(128 - p, KP8D - k8)
        segs.append((p, cnt, j, k8))
        p += cnt
    return segs



def build_launch_a(upto=99):
    nc = bass.Bass("TRN2", target_bir_lowering=False, debug=False)

    xT = nc.dram_tensor("xT", [D, S], F32, kind="ExternalInput").ap()
    x_c = nc.dram_tensor("x_c", [TOK, D], F32, kind="ExternalInput").ap()
    wq_s = nc.dram_tensor("wq_s", [D, HPC * HD], F32, kind="ExternalInput").ap()
    wk_s = nc.dram_tensor("wk_s", [D, HPC * HD], F32, kind="ExternalInput").ap()
    wv_s = nc.dram_tensor("wv_s", [D, HPC * HD], F32, kind="ExternalInput").ap()
    rope_cos = nc.dram_tensor("rope_cos", [S, ROT], F32, kind="ExternalInput").ap()
    rope_sin = nc.dram_tensor("rope_sin", [S, ROT], F32, kind="ExternalInput").ap()
    g_qw = nc.dram_tensor("g_qw", [KP8, INTER], I32, kind="ExternalInput").ap()
    u_qw = nc.dram_tensor("u_qw", [KP8, INTER], I32, kind="ExternalInput").ap()
    d_qw = nc.dram_tensor("d_qw", [KP8D, D], I32, kind="ExternalInput").ap()
    g_sc = nc.dram_tensor("g_sc", [NG, INTER], F32, kind="ExternalInput").ap()
    u_sc = nc.dram_tensor("u_sc", [NG, INTER], F32, kind="ExternalInput").ap()
    d_sc = nc.dram_tensor("d_sc", [NGD, D], F32, kind="ExternalInput").ap()
    g_qz = nc.dram_tensor("g_qz", [NG, INTER // 8], I32, kind="ExternalInput").ap()
    u_qz = nc.dram_tensor("u_qz", [NG, INTER // 8], I32, kind="ExternalInput").ap()
    d_qz = nc.dram_tensor("d_qz", [NGD, D // 8], I32, kind="ExternalInput").ap()
    ident = nc.dram_tensor("ident", [128, 128], BF16, kind="ExternalInput").ap()
    identf = nc.dram_tensor("identf", [128, 128], F32, kind="ExternalInput").ap()
    maskbin = nc.dram_tensor("maskbin", [128, 896], BF16, kind="ExternalInput").ap()
    e_gate = nc.dram_tensor("e_gate", [NCH * 128, NG], BF16, kind="ExternalInput").ap()
    e_down = nc.dram_tensor("e_down", [NGD * 128, NGD], BF16, kind="ExternalInput").ap()
    dshift = nc.dram_tensor("dshift", [128, NGD], I32, kind="ExternalInput").ap()
    e2b = nc.dram_tensor("e2b", [2, 256], F32, kind="ExternalInput").ap()

    o_out = nc.dram_tensor("o_out", [S, HPC * HD], BF16, kind="ExternalOutput").ap()
    g_out = nc.dram_tensor("g_out", [TOK, D], F32, kind="ExternalOutput").ap()

    qt_d = nc.dram_tensor("qt_d", [4 * 128, S], BF16, kind="Internal")
    kt_d = nc.dram_tensor("kt_d", [4 * 128, S], BF16, kind="Internal")
    v_d = nc.dram_tensor("v_d", [S, HPC * HD], BF16, kind="Internal")
    ht_d = nc.dram_tensor("ht_d", [INTER, TOK], BF16, kind="Internal")

    glob = contextlib.ExitStack()
    Tg = lambda n, s, d: glob.enter_context(nc.sbuf_tensor(n, s, d))

    ident_sb = Tg("ident_sb", [128, 128], BF16)
    identf_sb = Tg("identf_sb", [128, 128], F32)
    onescol_bf = Tg("onescol_bf", [128, 1], BF16)
    ones11_f = Tg("ones11_f", [1, 1], F32)
    eps_sb = Tg("eps_sb", [1, 1], F32)
    epscol_sb = Tg("epscol_sb", [128, 1], F32)
    negmu_sb = Tg("negmu_sb", [1, S], BF16)
    rcol_all = Tg("rcol_all", [128, 64], F32)
    cwq_sb = Tg("cwq_sb", [1, 512], BF16)
    cwk_sb = Tg("cwk_sb", [1, 512], BF16)
    cwv_sb = Tg("cwv_sb", [1, 512], BF16)
    negb_col = Tg("negb_col", [128, 2], F32)
    e2_sb = Tg("e2_sb", [2, 256], F32)

    def mksem(n):
        s = SemCnt(glob.enter_context(nc.semaphore(n)))
        _allsems.append(s)
        return s
    _allsems = []
    semPE = mksem("semPE")
    semV = mksem("semV")
    semA = mksem("semA")
    em = Emitter(own_sems={"vector": semV, "scalar": semA})
    em.all_sems = _allsems
    sCONST = mksem("sCONST")
    sWGT = mksem("sWGT")
    sXT = [mksem("sXT0"), mksem("sXT1")]
    sCOS = mksem("sCOS")
    sQST = [mksem("sQST0"), mksem("sQST1")]
    sVST = [mksem("sVST0"), mksem("sVST1")]
    sQTL = mksem("sQTL")
    sKV = [mksem("sKV0"), mksem("sKV1")]
    sOST = [mksem("sOST0"), mksem("sOST1")]
    sXC = [mksem("sXC0"), mksem("sXC1")]
    sZL = mksem("sZL")
    sEG = mksem("sEG")
    sED = mksem("sED")
    sDS = mksem("sDS")
    sQW = [mksem("sQW0"), mksem("sQW1")]
    sSE = [mksem("sSE0"), mksem("sSE1")]
    sHST = [mksem("sHST0"), mksem("sHST1")]
    sHC = [mksem("sHC0"), mksem("sHC1")]
    sQD = [mksem("sQD0"), mksem("sQD1")]
    sSD = [mksem("sSD0"), mksem("sSD1")]
    sGST = [mksem("sGST0"), mksem("sGST1")]
    store_sems = sQST + sVST + sOST + sHST + sGST
    E = em.emit

    E("sync", lambda e: e.dma_start(out=ident_sb[:], in_=ident), inc=sCONST, inc_n=16)
    E("sync", lambda e: e.dma_start(out=identf_sb[:], in_=identf), inc=sCONST, inc_n=16)
    E("sync", lambda e: e.dma_start(out=e2_sb[:], in_=e2b), inc=sCONST, inc_n=16)
    S_CONST = sCONST.v
    E("vector", lambda e: e.memset(onescol_bf[:], 1.0), inc=semV)
    E("vector", lambda e: e.memset(ones11_f[:], 1.0), inc=semV)
    E("vector", lambda e: e.memset(eps_sb[:], EPS), inc=semV)
    E("vector", lambda e: e.memset(epscol_sb[:], EPS), inc=semV)
    V_CONST = semV.v

    with contextlib.ExitStack() as ph:
        Tp = lambda n, s, d: ph.enter_context(nc.sbuf_tensor(n, s, d))
        Pp = lambda n, s, d: ph.enter_context(nc.psum_tensor(n, s, d))

        xt_sb = [Tp(f"xt{i}", [128, NCH, 512], BF16) for i in range(2)]
        wq_sb = Tp("wq_sb", [128, NCH, 512], BF16)
        wk_sb = Tp("wk_sb", [128, NCH, 512], BF16)
        sq_sb = [Tp(f"sq{i}", [128, 512], BF16) for i in range(2)]
        rowt = Tp("rowt", [1, 512], F32)
        rowt2 = Tp("rowt2", [1, 512], F32)
        rowt3 = Tp("rowt3", [1, 512], F32)
        rstd_row = Tp("rstd_row", [1, 512], F32)
        qk_f32 = Tp("qk_f32", [128, 512], F32)
        q_bf = Tp("q_bf", [128, 512], BF16)
        k_bf = Tp("k_bf", [128, 512], BF16)
        rt1 = Tp("rt1", [128, 64], F32)
        rt2 = Tp("rt2", [128, 64], F32)
        cos_sb = Tp("cos_sb", [128, ROT], F32)
        sin_sb = Tp("sin_sb", [128, ROT], F32)
        rcol_sb = Tp("rcol_sb", [128, 4], F32)
        qtw_sb = [Tp(f"qtw_sb{i}", [128, 4, 512], BF16) for i in range(2)]
        ktw_sb = [Tp(f"ktw_sb{i}", [128, 4, 512], BF16) for i in range(2)]
        nrm_sb = Tp("nrm_sb", [128, 4], F32)
        qmax_pad = Tp("qmax_pad", [128, 128], F32)
        kmax_pad = Tp("kmax_pad", [128, 128], F32)
        sqn_sb = Tp("sqn_sb", [128, 512], F32)
        maxred = Tp("maxred", [2, 128], F32)
        qmax_s = Tp("qmax_s", [2, 1], F32)
        kmax_s = Tp("kmax_s", [2, 1], F32)
        bcol_t = Tp("bcol_t", [128, 2], F32)

        ps_rows = Pp("ps_rows", [1, 1024], F32)
        ps_q = [Pp(f"ps_q{i}", [128, 512], F32) for i in range(2)]
        ps_k = [Pp(f"ps_k{i}", [128, 512], F32) for i in range(2)]
        ps_t = Pp("ps_t", [128, 1024], BF16)
        ps_rc = Pp("ps_rc", [128, 4], F32)

        E("vector", lambda e: e.memset(qmax_pad[:], 0.0), inc=semV)
        E("vector", lambda e: e.memset(kmax_pad[:], 0.0), inc=semV)
        V_PADS = semV.v

        E("gpsimd", lambda e: e.dma_start(out=wq_sb[:], in_=_rearr_cpt(wq_s)), inc=sWGT, inc_n=16)
        E("gpsimd", lambda e: e.dma_start(out=wk_sb[:], in_=_rearr_cpt(wk_s)), inc=sWGT, inc_n=16)
        LD_W = sWGT.v

        for wi, wsb in ((0, wq_sb), (1, wk_sb)):
            for ci in range(NCH):
                E("tensor", (lambda e, wsb=wsb, ci=ci, wi=wi: e.matmul(
                    ps_rows[0:1, 512 * wi:512 * wi + 512], onescol_bf[:], wsb[:, ci, :],
                    start=(ci == 0), stop=(ci == NCH - 1))),
                    waits=([(sWGT, LD_W), (semV, V_CONST)] if (ci == 0 and wi == 0) else ()))
        em.mark_last("tensor", semPE)
        PE_CW = semPE.v
        E("vector", lambda e: e.tensor_copy(cwq_sb[:], ps_rows[0:1, 0:512]),
          waits=[(semPE, PE_CW)], inc=semV)
        E("vector", lambda e: e.tensor_copy(cwk_sb[:], ps_rows[0:1, 512:1024]), inc=semV)
        v_rows_free = semV.v

        pe_xt_done = {}
        v_evac = {}
        v_trev = {}
        st_qk = {}
        pe_sq = {}

        for g in range(S // 512):
            buf = g % 2
            lw = [(semPE, pe_xt_done[g - 2])] if g >= 2 else []
            E("gpsimd", (lambda e, buf=buf, g=g: e.dma_start(
                out=xt_sb[buf][:], in_=_rearr_cpt(xT)[:, :, g * 512:(g + 1) * 512])),
                waits=lw, inc=sXT[buf], inc_n=16)
            LD_G = sXT[buf].v

            for ci in range(NCH):
                sb = ci % 2
                w = [(sXT[buf], LD_G)]
                if (g, ci - 2) in pe_sq:
                    w.append((semPE, pe_sq[(g, ci - 2)]))
                elif g > 0 and ci < 2:
                    w.append((semPE, pe_sq[(g - 1, NCH - 2 + ci)]))
                E("vector", (lambda e, buf=buf, ci=ci, sb=sb: e.tensor_mul(
                    sq_sb[sb][:], xt_sb[buf][:, ci, :], xt_sb[buf][:, ci, :])),
                    waits=w, inc=semV)
                V_SQ = semV.v
                wsum = [(sXT[buf], LD_G), (semV, v_rows_free)] if ci == 0 else ()
                E("tensor", (lambda e, buf=buf, ci=ci: e.matmul(
                    ps_rows[0:1, 0:512], onescol_bf[:], xt_sb[buf][:, ci, :],
                    start=(ci == 0), stop=(ci == NCH - 1))), waits=wsum)
                E("tensor", (lambda e, sb=sb, ci=ci: e.matmul(
                    ps_rows[0:1, 512:1024], onescol_bf[:], sq_sb[sb][:],
                    start=(ci == 0), stop=(ci == NCH - 1))),
                    waits=[(semV, V_SQ)], inc=semPE)
                pe_sq[(g, ci)] = semPE.v
            PE_ROWS = semPE.v

            E("vector", lambda e: e.tensor_scalar(rowt[:], ps_rows[0:1, 0:512], 1.0 / D, None, op0=ALU.mult),
              waits=[(semPE, PE_ROWS)], inc=semV)
            E("vector", lambda e: e.tensor_scalar(rowt2[:], ps_rows[0:1, 512:1024], 1.0 / D, None, op0=ALU.mult), inc=semV)
            v_rows_free = semV.v
            E("vector", lambda e: e.tensor_mul(rowt3[:], rowt[:], rowt[:]), inc=semV)
            E("vector", lambda e: e.tensor_sub(rowt2[:], rowt2[:], rowt3[:]), inc=semV)
            V_VAR = semV.v
            E("scalar", lambda e: e.activation(rowt2[:], rowt2[:], AF.Ln, bias=eps_sb[0:1, 0:1]),
              waits=[(semV, V_VAR)], inc=semA)
            E("scalar", lambda e: e.activation(rstd_row[:], rowt2[:], AF.Exp, scale=-0.5), inc=semA)
            E("scalar", (lambda e, g=g: e.activation(
                negmu_sb[0:1, g * 512:(g + 1) * 512], rowt[:], AF.Copy, scale=-1.0)), inc=semA)
            A_ROWS = semA.v

            for mi in range(4):
                E("tensor", (lambda e, mi=mi: e.matmul(
                    ps_rc[:, mi:mi + 1], rstd_row[0:1, mi * 128:(mi + 1) * 128], ones11_f[:],
                    start=True, stop=True)),
                    waits=([(semA, A_ROWS)] if mi == 0 else ()))
            em.mark_last("tensor", semPE)
            PE_RC = semPE.v
            E("vector", lambda e: e.tensor_copy(rcol_sb[:], ps_rc[:]), waits=[(semPE, PE_RC)], inc=semV)
            E("vector", (lambda e, g=g: e.tensor_copy(rcol_all[:, g * 4:g * 4 + 4], rcol_sb[:])), inc=semV)
            V_RC = semV.v

            for mi in range(4):
                m = g * 4 + mi
                qb = m % 2
                E("sync", (lambda e, m=m: e.dma_start(out=cos_sb[:], in_=rope_cos[m * 128:(m + 1) * 128, :])),
                  waits=([(semV, v_evac[m - 1])] if m >= 1 else []), inc=sCOS, inc_n=16)
                E("sync", (lambda e, m=m: e.dma_start(out=sin_sb[:], in_=rope_sin[m * 128:(m + 1) * 128, :])),
                  inc=sCOS, inc_n=16)
                S_ROPE = sCOS.v

                for proj, psb, wsb, cwt in ((0, ps_q[qb], wq_sb, cwq_sb), (1, ps_k[qb], wk_sb, cwk_sb)):
                    w0 = [(semV, v_evac[m - 2])] if m >= 2 else []
                    for ci in range(NCH):
                        E("tensor", (lambda e, psb=psb, buf=buf, ci=ci, mi=mi, wsb=wsb: e.matmul(
                            psb[:], xt_sb[buf][:, ci, mi * 128:(mi + 1) * 128], wsb[:, ci, :],
                            start=(ci == 0), stop=False)),
                            waits=(w0 if ci == 0 else ()))
                    E("tensor", (lambda e, psb=psb, g=g, mi=mi, cwt=cwt: e.matmul(
                        psb[:], negmu_sb[0:1, g * 512 + mi * 128:g * 512 + (mi + 1) * 128],
                        cwt[:], start=False, stop=True)), inc=semPE)
                pe_xt_done[g] = semPE.v
                PE_QK = semPE.v

                for proj, psb, obf in ((0, ps_q[qb], q_bf), (1, ps_k[qb], k_bf)):
                    E("vector", (lambda e, psb=psb, mi=mi: e.tensor_scalar(
                        qk_f32[:], psb[:], rcol_sb[:, mi:mi + 1], None, op0=ALU.mult)),
                        waits=[(semPE, PE_QK), (semV, V_RC), (sCOS, S_ROPE)]
                        + ([(semV, V_PADS)] if m == 0 and proj == 0 else []), inc=semV)
                    E("vector", (lambda e, obf=obf: e.tensor_copy(obf[:], qk_f32[:])), inc=semV)
                    for h in range(HPC):
                        b = h * HD
                        E("vector", (lambda e, b=b: e.tensor_mul(rt1[:], qk_f32[:, b:b + ROT], cos_sb[:])), inc=semV)
                        E("vector", (lambda e, b=b: e.tensor_mul(
                            rt2[:, 0:ROT:2], qk_f32[:, b + 1:b + ROT:2], sin_sb[:, 0:ROT:2])), inc=semV)
                        E("vector", (lambda e, b=b: e.tensor_mul(
                            rt2[:, 1:ROT:2], qk_f32[:, b:b + ROT:2], sin_sb[:, 1:ROT:2])), inc=semV)
                        E("vector", (lambda e, b=b, obf=obf: e.tensor_add(
                            obf[:, b:b + ROT], rt1[:], rt2[:])), inc=semV)
                    E("vector", (lambda e, obf=obf: e.tensor_mul(sqn_sb[:], obf[:], obf[:])), inc=semV)
                    for h in range(HPC):
                        E("vector", (lambda e, h=h, proj=proj: e.reduce_sum(
                            nrm_sb[:, 2 * proj + h:2 * proj + h + 1],
                            sqn_sb[:, h * HD:(h + 1) * HD], axis=AX.X)), inc=semV)
                        pad = qmax_pad if proj == 0 else kmax_pad
                        E("vector", (lambda e, pad=pad, h=h, proj=proj: e.tensor_max(
                            pad[:, h:h + 1], pad[:, h:h + 1],
                            nrm_sb[:, 2 * proj + h:2 * proj + h + 1])), inc=semV)
                v_evac[m] = semV.v
                V_EV = semV.v

                wtr = [(semV, V_EV)]
                if m == 0:
                    wtr.append((sCONST, S_CONST))
                if m >= 1:
                    wtr.append((semV, v_trev[m - 1]))
                for proj, obf, half in ((0, q_bf, 0), (1, k_bf, 1)):
                    for dc in range(4):
                        E("tensor", (lambda e, obf=obf, dc=dc, half=half: e.transpose(
                            ps_t[:, half * 512 + dc * 128:half * 512 + (dc + 1) * 128],
                            obf[:, dc * 128:(dc + 1) * 128], ident_sb[:])),
                            waits=(wtr if (dc == 0 and proj == 0) else ()))
                em.mark_last("tensor", semPE)
                PE_TR = semPE.v
                wev = [(semPE, PE_TR)]
                if g >= 2 and mi == 0:
                    wev.append((sQST[g % 2], st_qk[g - 2]))
                E("vector", (lambda e, mi=mi, buf=buf: e.tensor_copy(
                    qtw_sb[buf][:, :, mi * 128:(mi + 1) * 128],
                    ps_t[:, 0:512].rearrange("p (c n) -> p c n", c=4))),
                    waits=wev, inc=semV)
                E("vector", (lambda e, mi=mi, buf=buf: e.tensor_copy(
                    ktw_sb[buf][:, :, mi * 128:(mi + 1) * 128],
                    ps_t[:, 512:1024].rearrange("p (c n) -> p c n", c=4))), inc=semV)
                v_trev[m] = semV.v

            wst = [(semV, v_trev[g * 4 + 3])]
            E("sync", (lambda e, g=g, buf=buf: e.dma_start(
                out=_rearr_cpt(qt_d.ap())[:, :, g * 512:(g + 1) * 512],
                in_=qtw_sb[buf][:])), waits=wst, inc=sQST[buf], inc_n=16)
            E("sync", (lambda e, g=g, buf=buf: e.dma_start(
                out=_rearr_cpt(kt_d.ap())[:, :, g * 512:(g + 1) * 512],
                in_=ktw_sb[buf][:])), inc=sQST[buf], inc_n=16)
            st_qk[g] = sQST[buf].v

        V_ALL = semV.v
        E("tensor", lambda e: e.transpose(ps_q[0][:, 0:128], qmax_pad[:], identf_sb[:]),
          waits=[(semV, V_ALL)])
        E("tensor", lambda e: e.transpose(ps_k[0][:, 0:128], kmax_pad[:], identf_sb[:]),
          inc=semPE)
        PE_MXT = semPE.v
        E("vector", lambda e: e.tensor_copy(maxred[:], ps_q[0][0:2, 0:128]), waits=[(semPE, PE_MXT)], inc=semV)
        E("vector", lambda e: e.reduce_max(qmax_s[:], maxred[:], axis=AX.X), inc=semV)
        E("vector", lambda e: e.tensor_copy(maxred[:], ps_k[0][0:2, 0:128]), inc=semV)
        E("vector", lambda e: e.reduce_max(kmax_s[:], maxred[:], axis=AX.X), inc=semV)
        V_MX = semV.v
        for h in range(HPC):
            E("tensor", (lambda e, h=h: e.matmul(
                ps_rc[:, 0:1], e2_sb[:, h * 128:(h + 1) * 128], qmax_s[:], start=True, stop=True)),
                waits=[(semV, V_MX if h == 0 else semV.v)])
            E("tensor", (lambda e, h=h: e.matmul(
                ps_rc[:, 1:2], e2_sb[:, h * 128:(h + 1) * 128], kmax_s[:], start=True, stop=True)),
                inc=semPE)
            PE_B = semPE.v
            E("vector", lambda e: e.tensor_copy(bcol_t[:], ps_rc[:, 0:2]), waits=[(semPE, PE_B)], inc=semV)
            V_B = semV.v
            E("scalar", lambda e: e.activation(bcol_t[:, 0:1], bcol_t[:, 0:1], AF.Ln),
              waits=[(semV, V_B)], inc=semA)
            E("scalar", lambda e: e.activation(bcol_t[:, 1:2], bcol_t[:, 1:2], AF.Ln), inc=semA)
            A_LN = semA.v
            E("vector", lambda e: e.tensor_add(bcol_t[:, 0:1], bcol_t[:, 0:1], bcol_t[:, 1:2]),
              waits=[(semA, A_LN)], inc=semV)
            V_AD = semV.v
            E("scalar", lambda e: e.activation(bcol_t[:, 0:1], bcol_t[:, 0:1], AF.Exp, scale=0.5),
              waits=[(semV, V_AD)], inc=semA)
            A_EX = semA.v
            E("vector", (lambda e, h=h: e.tensor_scalar(
                negb_col[:, h:h + 1], bcol_t[:, 0:1], -1.0 / 16.0, None, op0=ALU.mult)),
                waits=[(semA, A_EX)], inc=semV)
        em.barrier()

    if upto <= 1:
        _finish(nc, em, store_sems)
        glob.close()
        return nc

    with contextlib.ExitStack() as ph:
        Tp = lambda n, s, d: ph.enter_context(nc.sbuf_tensor(n, s, d))
        Pp = lambda n, s, d: ph.enter_context(nc.psum_tensor(n, s, d))
        xtv_sb = [Tp(f"xtv{i}", [128, NCH, 512], BF16) for i in range(2)]
        wv_sb = Tp("wv_sb", [128, NCH, 512], BF16)
        v_bf = [Tp(f"v_bf{i}", [128, 512], BF16) for i in range(2)]
        ps_rowsv = Pp("ps_rowsvv", [1, 1024], F32)
        ps_v = [Pp(f"ps_v{i}", [128, 512], F32) for i in range(2)]

        E("gpsimd", lambda e: e.dma_start(out=wv_sb[:], in_=_rearr_cpt(wv_s)), inc=sWGT, inc_n=16)
        LD_WV = sWGT.v
        for ci in range(NCH):
            E("tensor", (lambda e, ci=ci: e.matmul(
                ps_rowsv[0:1, 0:512], onescol_bf[:], wv_sb[:, ci, :],
                start=(ci == 0), stop=(ci == NCH - 1))),
                waits=([(sWGT, LD_WV)] if ci == 0 else ()))
        em.mark_last("tensor", semPE)
        PE_CWV = semPE.v
        E("vector", lambda e: e.tensor_copy(cwv_sb[:], ps_rowsv[0:1, 0:512]),
          waits=[(semPE, PE_CWV)], inc=semV)
        V_CWV = semV.v

        v_ev = {}
        st_v = {}
        pe_g = {}
        for g in range(S // 512):
            buf = g % 2
            lw = [(semPE, pe_g[g - 2])] if g >= 2 else []
            E("gpsimd", (lambda e, buf=buf, g=g: e.dma_start(
                out=xtv_sb[buf][:], in_=_rearr_cpt(xT)[:, :, g * 512:(g + 1) * 512])),
                waits=lw, inc=sXT[buf], inc_n=16)
            LD_G = sXT[buf].v
            for mi in range(4):
                m = g * 4 + mi
                vb = m % 2
                w0 = [(sXT[buf], LD_G), (semV, V_CWV)]
                if m >= 2:
                    w0.append((semV, v_ev[m - 2]))
                for ci in range(NCH):
                    E("tensor", (lambda e, vb=vb, buf=buf, ci=ci, mi=mi: e.matmul(
                        ps_v[vb][:], xtv_sb[buf][:, ci, mi * 128:(mi + 1) * 128], wv_sb[:, ci, :],
                        start=(ci == 0), stop=False)),
                        waits=(w0 if ci == 0 else ()))
                E("tensor", (lambda e, vb=vb, g=g, mi=mi: e.matmul(
                    ps_v[vb][:], negmu_sb[0:1, g * 512 + mi * 128:g * 512 + (mi + 1) * 128],
                    cwv_sb[:], start=False, stop=True)), inc=semPE)
                pe_g[g] = semPE.v
                PE_V = semPE.v
                wv0 = [(semPE, PE_V)]
                if m >= 2:
                    wv0.append((sVST[m % 2], st_v[m - 2]))
                E("vector", (lambda e, vb=vb, m=m: e.tensor_scalar(
                    v_bf[vb][:], ps_v[vb][:], rcol_all[:, m:m + 1], None, op0=ALU.mult)),
                    waits=wv0, inc=semV)
                v_ev[m] = semV.v
                E("sync", (lambda e, m=m, vb=vb: e.dma_start(
                    out=v_d.ap()[m * 128:(m + 1) * 128, :], in_=v_bf[vb][:])),
                    waits=[(semV, v_ev[m])], inc=sVST[vb], inc_n=16)
                st_v[m] = sVST[vb].v
        em.barrier()

    if upto <= 2:
        _finish(nc, em, store_sems)
        glob.close()
        return nc

    with contextlib.ExitStack() as ph:
        Tp = lambda n, s, d: ph.enter_context(nc.sbuf_tensor(n, s, d))
        Pp = lambda n, s, d: ph.enter_context(nc.psum_tensor(n, s, d))
        qt_sb = Tp("qt_sb", [128, 4, 512], BF16)
        kt_sb = [Tp(f"kt_s{i}", [128, 4, 512], BF16) for i in range(2)]
        v_sb = [Tp(f"v_s{i}", [128, 4, 512], BF16) for i in range(2)]
        pt_sb = [Tp(f"pt_s{i}", [128, 512], BF16) for i in range(2)]
        mask_sb = Tp("mask_sb", [128, 896], BF16)
        oev_sb = Tp("oev_sb", [128, 2, 512], F32)
        lrow_pad = Tp("lrow_pad", [128, 512], F32)
        rl_sb = Tp("rl_sb", [128, 4], F32)
        o_sb = [Tp(f"o_sb{i}", [128, 4, 512], BF16) for i in range(2)]

        ps_s = [Pp(f"ps_s{i}", [128, 512], F32) for i in range(2)]
        ps_o = [Pp(f"ps_o{i}", [128, 512], F32) for i in range(4)]
        ps_lf = Pp("ps_lf", [1, 1024], F32)

        E("sync", lambda e: e.dma_start(out=mask_sb[:], in_=maskbin), inc=sCONST, inc_n=16)
        S_MASK = sCONST.v

        fin_done = 0
        st_o = {}
        for qc in range(16):
            nkc = 4 * (qc + 1)
            E("sync", (lambda e, qc=qc: e.dma_start(
                out=qt_sb[:], in_=_rearr_cpt(qt_d.ap())[:, :, qc * 512:(qc + 1) * 512])),
                waits=[(semV, fin_done), (sQST[0], sQST[0].v), (sQST[1], sQST[1].v)],
                inc=sQTL, inc_n=16)
            S_QT = sQTL.v

            units = [(kc, h) for kc in range(nkc) for h in range(HPC)]
            ld_grp = {}
            unit_pe = {}
            unit_expfree = {}

            for ui, (kc, h) in enumerate(units):
                gi = kc // 4
                gb = gi % 2
                kci = kc % 4
                ub = ui % 2
                if h == 0 and kci == 0:
                    w = [(semV, fin_done)]
                    if gi >= 2:
                        w.append((semPE, unit_pe[(4 * (gi - 1) - 1, HPC - 1)]))
                    if qc == 0 and gi == 0:
                        w += [(sVST[0], sVST[0].v), (sVST[1], sVST[1].v)]
                    E("sync", (lambda e, gi=gi, gb=gb: e.dma_start(
                        out=kt_sb[gb][:],
                        in_=_rearr_cpt(kt_d.ap())[:, :, gi * 512:(gi + 1) * 512])),
                        waits=w, inc=sKV[gb], inc_n=16)
                    E("sync", (lambda e, gi=gi, gb=gb: e.dma_start(
                        out=v_sb[gb][:],
                        in_=v_d.ap()[gi * 512:(gi + 1) * 512, :].rearrange("(g p) d -> p g d", p=128))),
                        inc=sKV[gb], inc_n=16)
                    ld_grp[gi] = sKV[gb].v

                ws = [(sKV[gb], ld_grp[gi]), (sQTL, S_QT)]
                if ui >= 2:
                    ws.append((semA, unit_expfree[ui - 2]))
                E("tensor", (lambda e, gb=gb, h=h, kci=kci, ub=ub: e.matmul(
                    ps_s[ub][:], kt_sb[gb][:, 2 * h, kci * 128:(kci + 1) * 128],
                    qt_sb[:, 2 * h, :], start=True, stop=False)), waits=ws)
                E("tensor", (lambda e, gb=gb, h=h, kci=kci, ub=ub: e.matmul(
                    ps_s[ub][:], kt_sb[gb][:, 2 * h + 1, kci * 128:(kci + 1) * 128],
                    qt_sb[:, 2 * h + 1, :], start=False, stop=True)), inc=semPE)
                PE_S = semPE.v

                wexp = [(semPE, PE_S)]
                if ui >= 2:
                    wexp.append((semPE, unit_pe[units[ui - 2]]))
                E("scalar", (lambda e, ub=ub, h=h: e.activation(
                    pt_sb[ub][:], ps_s[ub][:], AF.Exp, bias=negb_col[:, h:h + 1], scale=1.0 / 16.0)),
                    waits=wexp, inc=semA)
                unit_expfree[ui] = semA.v
                pt_sem, pt_val = semA, semA.v

                kbase, qbase = kc * 128, qc * 512
                if kbase > qbase - 128:
                    off = 384 - (kbase - qbase)
                    E("vector", (lambda e, ub=ub, off=off: e.tensor_mul(
                        pt_sb[ub][:], pt_sb[ub][:], mask_sb[:, off:off + 512])),
                        waits=[(semA, pt_val), (sCONST, S_MASK)], inc=semV)
                    pt_sem, pt_val = semV, semV.v

                E("tensor", (lambda e, gb=gb, h=h, kci=kci, ub=ub, kc=kc, nkc=nkc: e.matmul(
                    ps_o[2 * h][:], v_sb[gb][:, kci, h * HD:h * HD + 128],
                    pt_sb[ub][:], start=(kc == 0), stop=(kc == nkc - 1))),
                    waits=[(pt_sem, pt_val)])
                E("tensor", (lambda e, gb=gb, h=h, kci=kci, ub=ub, kc=kc, nkc=nkc: e.matmul(
                    ps_o[2 * h + 1][:], v_sb[gb][:, kci, h * HD + 128:h * HD + 256],
                    pt_sb[ub][:], start=(kc == 0), stop=(kc == nkc - 1))))
                E("tensor", (lambda e, h=h, ub=ub, kc=kc, nkc=nkc: e.matmul(
                    ps_lf[0:1, 512 * h:512 * h + 512], onescol_bf[:], pt_sb[ub][:],
                    start=(kc == 0), stop=(kc == nkc - 1))), inc=semPE)
                unit_pe[(kc, h)] = semPE.v

            PE_UNITS = semPE.v
            A_LASTEXP = semA.v
            V_LASTMSK = semV.v

            for h in range(HPC):
                E("vector", (lambda e, h=h: e.tensor_copy(
                    lrow_pad[0:1, :], ps_lf[0:1, 512 * h:512 * h + 512])),
                    waits=[(semPE, PE_UNITS)], inc=semV)
                V_LR = semV.v
                for qs in range(4):
                    E("tensor", (lambda e, qs=qs: e.transpose(
                        ps_s[1][:, qs * 128:(qs + 1) * 128],
                        lrow_pad[:, qs * 128:(qs + 1) * 128], identf_sb[:])),
                        waits=([(semV, V_LR), (semA, A_LASTEXP)] if qs == 0 else ()))
                em.mark_last("tensor", semPE)
                PE_LT = semPE.v
                E("vector", (lambda e: e.reciprocal(rl_sb[:], ps_s[1][:, 0:512:128])),
                  waits=[(semPE, PE_LT)], inc=semV)
                V_RL = semV.v
                E("vector", (lambda e, h=h: e.tensor_copy(oev_sb[:, 0, :], ps_o[2 * h][:])), inc=semV)
                E("vector", (lambda e, h=h: e.tensor_copy(oev_sb[:, 1, :], ps_o[2 * h + 1][:])), inc=semV)
                V_OE = semV.v
                for pair in range(2):
                    bank = ps_s[pair]
                    wtp = [(semV, V_OE)]
                    if pair == 0:
                        wtp.append((semV, V_RL))
                    else:
                        wtp.append((semV, semV.v))
                    for qi in range(2):
                        qs = pair * 2 + qi
                        for dc in range(2):
                            E("tensor", (lambda e, bank=bank, qi=qi, qs=qs, dc=dc: e.transpose(
                                bank[:, qi * 256 + dc * 128:qi * 256 + (dc + 1) * 128],
                                oev_sb[:, dc, qs * 128:(qs + 1) * 128], identf_sb[:])),
                                waits=(wtp if (qi == 0 and dc == 0) else ()))
                    em.mark_last("tensor", semPE)
                    PE_OT = semPE.v
                    for qi in range(2):
                        qs = pair * 2 + qi
                        wsc = [(semPE, PE_OT)] if qi == 0 else []
                        if h == 0 and pair == 0 and qi == 0 and qc >= 2:
                            wsc.append((sOST[qc % 2], st_o[qc - 2]))
                        E("vector", (lambda e, bank=bank, qi=qi, qs=qs, h=h, qc=qc: e.tensor_scalar(
                            o_sb[qc % 2][:, qs, h * HD:(h + 1) * HD], bank[:, qi * 256:(qi + 1) * 256],
                            rl_sb[:, qs:qs + 1], None, op0=ALU.mult)),
                            waits=wsc, inc=semV)
            fin_done = semV.v
            E("sync", (lambda e, qc=qc: e.dma_start(
                out=o_out[qc * 512:(qc + 1) * 512, :].rearrange("(q p) d -> p q d", p=128),
                in_=o_sb[qc % 2][:])), waits=[(semV, fin_done)], inc=sOST[qc % 2], inc_n=16)
            st_o[qc] = sOST[qc % 2].v
        em.barrier()

    if upto <= 3:
        _finish(nc, em, store_sems)
        glob.close()
        return nc

    xnt_sb = Tg("xnt_sb", [128, NCH, 1024], BF16)
    with contextlib.ExitStack() as ph:
        Tp = lambda n, s, d: ph.enter_context(nc.sbuf_tensor(n, s, d))
        Pp = lambda n, s, d: ph.enter_context(nc.psum_tensor(n, s, d))
        xtile = [Tp(f"xtile{i}", [128, D], F32) for i in range(2)]
        xsq = Tp("xsq", [128, D], F32)
        stat = Tp("stat", [128, 6], F32)
        xn_bf = Tp("xn_bf", [128, D], BF16)
        ps4 = Pp("ps4", [128, 1024], BF16)

        v_done = {}
        pe_tr = {}
        for m in range(8):
            buf = m % 2
            lw = [(semV, v_done[m - 2])] if m >= 2 else []
            E("sync", (lambda e, m=m, buf=buf: e.dma_start(
                out=xtile[buf][:], in_=x_c[m * 128:(m + 1) * 128, :])),
                waits=lw, inc=sXC[buf], inc_n=16)
            LD_X = sXC[buf].v
            E("vector", (lambda e, buf=buf: e.reduce_sum(stat[:, 0:1], xtile[buf][:], axis=AX.X)),
              waits=[(sXC[buf], LD_X)], inc=semV)
            E("vector", (lambda e, buf=buf: e.tensor_mul(xsq[:], xtile[buf][:], xtile[buf][:])), inc=semV)
            E("vector", lambda e: e.reduce_sum(stat[:, 1:2], xsq[:], axis=AX.X), inc=semV)
            E("vector", lambda e: e.tensor_scalar(stat[:, 2:3], stat[:, 0:1], 1.0 / D, None, op0=ALU.mult), inc=semV)
            E("vector", lambda e: e.tensor_scalar(stat[:, 3:4], stat[:, 1:2], 1.0 / D, None, op0=ALU.mult), inc=semV)
            E("vector", lambda e: e.tensor_mul(stat[:, 4:5], stat[:, 2:3], stat[:, 2:3]), inc=semV)
            E("vector", lambda e: e.tensor_sub(stat[:, 3:4], stat[:, 3:4], stat[:, 4:5]), inc=semV)
            V_ST = semV.v
            E("scalar", lambda e: e.activation(stat[:, 3:4], stat[:, 3:4], AF.Ln, bias=epscol_sb[:, 0:1]),
              waits=[(semV, V_ST)], inc=semA)
            E("scalar", lambda e: e.activation(stat[:, 5:6], stat[:, 3:4], AF.Exp, scale=-0.5), inc=semA)
            A_ST = semA.v
            E("vector", (lambda e, buf=buf: e.tensor_scalar(
                xn_bf[:], xtile[buf][:], stat[:, 2:3], stat[:, 5:6],
                op0=ALU.subtract, op1=ALU.mult)),
                waits=[(semA, A_ST)], inc=semV)
            v_done[m] = semV.v
            V_XN = semV.v

            for rnd in range(4):
                wt = [(semV, V_XN)]
                if (m, rnd - 1) in pe_tr:
                    wt.append((semV, pe_tr[(m, rnd - 1)]))
                elif m >= 1 and rnd == 0:
                    wt.append((semV, pe_tr[(m - 1, 3)]))
                for k in range(8):
                    c2 = rnd * 8 + k
                    j = c2 // 4
                    r = c2 % 4
                    E("tensor", (lambda e, j=j, r=r, k=k: e.transpose(
                        ps4[:, k * 128:(k + 1) * 128],
                        xn_bf[:, j + 8 * (r * 128): j + 8 * (r * 128 + 127) + 1: 8],
                        ident_sb[:])),
                        waits=(wt if k == 0 else ()))
                em.mark_last("tensor", semPE)
                PE_T4 = semPE.v
                E("vector", (lambda e, rnd=rnd, m=m: e.tensor_copy(
                    xnt_sb[:, rnd * 8:(rnd + 1) * 8, m * 128:(m + 1) * 128],
                    ps4.rearrange("p (c n) -> p c n", c=8))),
                    waits=[(semPE, PE_T4)], inc=semV)
                pe_tr[(m, rnd)] = semV.v
        em.barrier()

    if upto <= 4:
        _finish(nc, em, store_sems)
        glob.close()
        return nc

    bgate_sb = Tg("bgate_sb", [NG, INTER], BF16)
    bup_sb = Tg("bup_sb", [NG, INTER], BF16)
    bdown_sb = Tg("bdown_sb", [NGD, D], BF16)
    xg_sb = Tg("xg_sb", [NG, 1024], BF16)

    with contextlib.ExitStack() as ph:
        Tp = lambda n, s, d: ph.enter_context(nc.sbuf_tensor(n, s, d))
        Pp = lambda n, s, d: ph.enter_context(nc.psum_tensor(n, s, d))
        qz_sb = Tp("qz_sb", [NGD, 688], I32)
        zi_sb = Tp("zi_sb", [NGD, 5504], I32)
        zf_sb = Tp("zf_sb", [NGD, 5504], F32)
        sc5_sb = Tp("sc5_sb", [NGD, 5504], F32)
        egate_sb = Tp("egate_sb", [128, NCH, NG], BF16)
        ps_xg = Pp("ps_xg", [NG, 1024], F32)

        jobs = []
        for sp in range(2):
            jobs.append((g_qz, g_sc, bgate_sb, NG, 1376, INTER, sp, 1.0))
            jobs.append((u_qz, u_sc, bup_sb, NG, 1376, INTER, sp, 1.0))
        jobs.append((d_qz, d_sc, bdown_sb, NGD, 512, D, 0, -1.0))

        v_prev = 0
        for (qz, sc, bsb, ngr, n8, ntot, sp, sgn) in jobs:
            half8 = n8 // 2 if ntot == INTER else n8
            half = half8 * 8
            E("sync", (lambda e, qz=qz, ngr=ngr, sp=sp, half8=half8: e.dma_start(
                out=qz_sb[0:ngr, 0:half8], in_=qz[:, sp * half8:(sp + 1) * half8])),
                waits=([(semV, v_prev)] if v_prev else []), inc=sZL, inc_n=16)
            E("sync", (lambda e, sc=sc, ngr=ngr, sp=sp, half=half: e.dma_start(
                out=sc5_sb[0:ngr, 0:half], in_=sc[:, sp * half:(sp + 1) * half])),
                inc=sZL, inc_n=16)
            LD_Z = sZL.v
            for jn in range(8):
                E("vector", (lambda e, jn=jn, ngr=ngr, half8=half8: e.tensor_scalar(
                    zi_sb[0:ngr, jn:half8 * 8:8], qz_sb[0:ngr, 0:half8],
                    int(4 * jn), 15, op0=ALU.logical_shift_right, op1=ALU.bitwise_and)),
                    waits=([(sZL, LD_Z)] if jn == 0 else ()), inc=semV)
            E("vector", (lambda e, ngr=ngr, half=half: e.tensor_scalar(
                zf_sb[0:ngr, 0:half], zi_sb[0:ngr, 0:half], 1.0, None, op0=ALU.add)),
                inc=semV)
            if sgn != 1.0:
                E("vector", (lambda e, ngr=ngr, half=half, sgn=sgn: e.tensor_scalar(
                    sc5_sb[0:ngr, 0:half], sc5_sb[0:ngr, 0:half], sgn, None, op0=ALU.mult)),
                    inc=semV)
            E("vector", (lambda e, bsb=bsb, ngr=ngr, sp=sp, half=half: e.tensor_mul(
                bsb[0:ngr, sp * half:(sp + 1) * half], zf_sb[0:ngr, 0:half],
                sc5_sb[0:ngr, 0:half])), inc=semV)
            v_prev = semV.v
        em.barrier()

        E("sync", lambda e: e.dma_start(
            out=egate_sb[:], in_=e_gate.rearrange("(c p) g -> p c g", p=128)), inc=sEG, inc_n=16)
        LD_EG = sEG.v
        for half in range(2):
            for c2 in range(NCH):
                E("tensor", (lambda e, half=half, c2=c2: e.matmul(
                    ps_xg[:, half * 512:(half + 1) * 512], egate_sb[:, c2, :],
                    xnt_sb[:, c2, half * 512:(half + 1) * 512],
                    start=(c2 == 0), stop=(c2 == NCH - 1))),
                    waits=([(sEG, LD_EG)] if (half == 0 and c2 == 0) else ()))
        em.mark_last("tensor", semPE)
        PE_XG = semPE.v
        E("vector", lambda e: e.tensor_copy(xg_sb[:], ps_xg[:]), waits=[(semPE, PE_XG)], inc=semV)
        em.barrier()

    if upto <= 6:
        _finish(nc, em, store_sems)
        glob.close()
        return nc

    with contextlib.ExitStack() as ph:
        Tp = lambda n, s, d: ph.enter_context(nc.sbuf_tensor(n, s, d))
        Pp = lambda n, s, d: ph.enter_context(nc.psum_tensor(n, s, d))
        qwg_sb = [Tp(f"qwg{i}", [128, 4, 128], I32) for i in range(2)]
        qwu_sb = [Tp(f"qwu{i}", [128, 4, 128], I32) for i in range(2)]
        seg_sb = [Tp(f"seg{i}", [128, 4, 128], BF16) for i in range(2)]
        seu_sb = [Tp(f"seu{i}", [128, 4, 128], BF16) for i in range(2)]
        wg_all = [Tp(f"wg_all{i}", [128, NCH, 128], BF16) for i in range(2)]
        wu_all = [Tp(f"wu_all{i}", [128, NCH, 128], BF16) for i in range(2)]
        w4g = Tp("w4g", [128, 128], I32)
        w4u = Tp("w4u", [128, 128], I32)
        silu_t = Tp("silu_t", [128, 512], F32)
        htw = [Tp(f"htw{i}", [128, 1024], BF16) for i in range(2)]
        ps7 = [Pp(f"ps7_{i}", [128, 512], F32) for i in range(8)]

        NSP = INTER // 128
        pe_mm = {}
        v_deq = {}
        v_ht = {}
        st_ht = {}
        for sp in range(NSP):
            buf = sp % 2
            base = sp * 128
            st_ = (sp % 2) * 4
            lw = [(semV, v_deq[sp - 2])] if sp >= 2 else []
            E("sync", (lambda e, buf=buf, base=base: e.dma_start(
                out=qwg_sb[buf][:], in_=g_qw.rearrange("(r p) n -> p r n", p=128)[:, :, base:base + 128])),
                waits=lw, inc=sQW[buf], inc_n=16)
            E("sync", (lambda e, buf=buf, base=base: e.dma_start(
                out=qwu_sb[buf][:], in_=u_qw.rearrange("(r p) n -> p r n", p=128)[:, :, base:base + 128])),
                inc=sQW[buf], inc_n=16)
            LD_QW = sQW[buf].v
            for r in range(4):
                E("gpsimd", (lambda e, buf=buf, base=base, r=r: e.dma_start(
                    out=seg_sb[buf][:, r, :],
                    in_=g_sc[8 * r:8 * r + 8, base:base + 128].unsqueeze(1).broadcast_to([8, 16, 128]))),
                    waits=(lw if r == 0 else ()), inc=sSE[buf], inc_n=16)
                E("gpsimd", (lambda e, buf=buf, base=base, r=r: e.dma_start(
                    out=seu_sb[buf][:, r, :],
                    in_=u_sc[8 * r:8 * r + 8, base:base + 128].unsqueeze(1).broadcast_to([8, 16, 128]))),
                    inc=sSE[buf], inc_n=16)
            LD_SE = sSE[buf].v

            for c2 in range(NCH):
                j, r = c2 // 4, c2 % 4
                wdq = [(sQW[buf], LD_QW), (sSE[buf], LD_SE)]
                if sp >= 2 and c2 == 0:
                    wdq.append((semPE, pe_mm[sp - 2]))
                E("vector", (lambda e, buf=buf, j=j, r=r: e.tensor_scalar(
                    w4g[:], qwg_sb[buf][:, r, :], int(4 * j), 15,
                    op0=ALU.logical_shift_right, op1=ALU.bitwise_and)),
                    waits=wdq, inc=semV)
                E("vector", (lambda e, buf=buf, r=r, c2=c2: e.tensor_mul(
                    wg_all[buf][:, c2, :], w4g[:], seg_sb[buf][:, r, :])), inc=semV)
                E("vector", (lambda e, buf=buf, j=j, r=r: e.tensor_scalar(
                    w4u[:], qwu_sb[buf][:, r, :], int(4 * j), 15,
                    op0=ALU.logical_shift_right, op1=ALU.bitwise_and)), inc=semV)
                E("vector", (lambda e, buf=buf, r=r, c2=c2: e.tensor_mul(
                    wu_all[buf][:, c2, :], w4u[:], seu_sb[buf][:, r, :])), inc=semV)
            v_deq[sp] = semV.v
            V_DEQ = semV.v

            wmm0 = [(semV, V_DEQ)]
            if sp >= 2:
                wmm0.append((semV, v_ht[sp - 2]))
            for c2 in range(NCH):
                for half in range(2):
                    E("tensor", (lambda e, buf=buf, half=half, c2=c2, st_=st_: e.matmul(
                        ps7[st_ + half][:], wg_all[buf][:, c2, :],
                        xnt_sb[:, c2, half * 512:(half + 1) * 512],
                        start=(c2 == 0), stop=False)),
                        waits=(wmm0 if (c2 == 0 and half == 0) else ()))
                    E("tensor", (lambda e, buf=buf, half=half, c2=c2, st_=st_: e.matmul(
                        ps7[st_ + 2 + half][:], wu_all[buf][:, c2, :],
                        xnt_sb[:, c2, half * 512:(half + 1) * 512],
                        start=(c2 == 0), stop=False)))
            for half in range(2):
                E("tensor", (lambda e, half=half, base=base, st_=st_: e.matmul(
                    ps7[st_ + half][:], bgate_sb[:, base:base + 128],
                    xg_sb[:, half * 512:(half + 1) * 512], start=False, stop=True)))
                E("tensor", (lambda e, half=half, base=base, st_=st_: e.matmul(
                    ps7[st_ + 2 + half][:], bup_sb[:, base:base + 128],
                    xg_sb[:, half * 512:(half + 1) * 512], start=False, stop=True)))
            em.mark_last("tensor", semPE)
            pe_mm[sp] = semPE.v
            PE_NT = semPE.v

            for half in range(2):
                E("scalar", (lambda e, half=half, st_=st_: e.activation(
                    silu_t[:], ps7[st_ + half][:], AF.Silu)),
                    waits=([(semPE, PE_NT)] if half == 0 else []) + [(semV, semV.v)],
                    inc=semA)
                A_SI = semA.v
                wht = [(semA, A_SI)]
                if sp >= 2 and half == 0:
                    wht.append((sHST[sp % 2], st_ht[sp - 2]))
                E("vector", (lambda e, buf=buf, half=half, st_=st_: e.tensor_mul(
                    htw[buf][:, half * 512:(half + 1) * 512], silu_t[:],
                    ps7[st_ + 2 + half][:])), waits=wht, inc=semV)
            v_ht[sp] = semV.v
            E("sync", (lambda e, buf=buf, base=base: e.dma_start(
                out=ht_d.ap()[base:base + 128, :], in_=htw[buf][:])),
                waits=[(semV, v_ht[sp])], inc=sHST[buf], inc_n=16)
            st_ht[sp] = sHST[buf].v
        em.barrier()

    if upto <= 7:
        _finish(nc, em, store_sems)
        glob.close()
        return nc

    xgd_sb = Tg("xgd_sb", [NGD, 1024], BF16)
    with contextlib.ExitStack() as ph:
        Tp = lambda n, s, d: ph.enter_context(nc.sbuf_tensor(n, s, d))
        Pp = lambda n, s, d: ph.enter_context(nc.psum_tensor(n, s, d))
        edown_sb = Tp("edown_sb", [128, NGD, NGD], BF16)
        htc_sb = [Tp(f"htc{i}", [128, 1024], BF16) for i in range(2)]
        ps_xgd = Pp("ps_xgd", [NGD, 1024], F32)

        E("sync", lambda e: e.dma_start(
            out=edown_sb[:], in_=e_down.rearrange("(c p) g -> p c g", p=128)), inc=sED, inc_n=16)
        LD_ED = sED.v
        pe_c = {}
        for c2 in range(NGD):
            buf = c2 % 2
            segs = _down_segments(c2)
            w = [(sED, LD_ED)]
            if c2 >= 2:
                w.append((semPE, pe_c[c2 - 2]))
            else:
                w += [(sHST[0], sHST[0].v), (sHST[1], sHST[1].v)]
            for si, (p0, cnt, j, k8) in enumerate(segs):
                E("sync", (lambda e, buf=buf, p0=p0, cnt=cnt, j=j, k8=k8: e.dma_start(
                    out=htc_sb[buf][p0:p0 + cnt, :],
                    in_=ht_d.ap()[8 * k8 + j: 8 * (k8 + cnt - 1) + j + 1: 8, :])),
                    waits=(w if si == 0 else ()), inc=sHC[buf], inc_n=16)
            LD_HT = sHC[buf].v
            for half in range(2):
                E("tensor", (lambda e, buf=buf, c2=c2, half=half: e.matmul(
                    ps_xgd[:, half * 512:(half + 1) * 512], edown_sb[:, c2, :],
                    htc_sb[buf][:, half * 512:(half + 1) * 512],
                    start=(c2 == 0), stop=(c2 == NGD - 1))),
                    waits=([(sHC[buf], LD_HT)] if half == 0 else ()))
            em.mark_last("tensor", semPE)
            pe_c[c2] = semPE.v
        PE_XGD = semPE.v
        E("vector", lambda e: e.tensor_copy(xgd_sb[:], ps_xgd[:]), waits=[(semPE, PE_XGD)], inc=semV)
        em.barrier()

    if upto <= 8:
        _finish(nc, em, store_sems)
        glob.close()
        return nc

    with contextlib.ExitStack() as ph:
        Tp = lambda n, s, d: ph.enter_context(nc.sbuf_tensor(n, s, d))
        Pp = lambda n, s, d: ph.enter_context(nc.psum_tensor(n, s, d))
        dshift_sb = Tp("dshift_sb", [128, NGD], I32)
        qwd_sb = [Tp(f"qwd{i}", [128, 512], I32) for i in range(2)]
        sed_sb = [Tp(f"sed{i}", [128, 512], BF16) for i in range(2)]
        htc9_sb = [Tp(f"htc9{i}", [128, 1024], BF16) for i in range(2)]
        w4d_t = [Tp(f"w4d_t{i}", [128, 512], I32) for i in range(2)]
        wd_bf = [Tp(f"wd_bf{i}", [128, 512], BF16) for i in range(2)]
        gtile = [Tp(f"gtile{i}", [128, 512], F32) for i in range(2)]
        ps9 = [Pp(f"ps9_{i}", [128, 512], F32) for i in range(8)]

        E("sync", lambda e: e.dma_start(out=dshift_sb[:], in_=dshift), inc=sDS, inc_n=16)
        LD_DS = sDS.v

        a_g = {}
        st_g = {}
        pe_n = {}
        for ns in range(8):
            nb = ns * 512
            for c2 in range(NGD):
                buf = c2 % 2
                segs = _down_segments(c2)
                wld = []
                if c2 >= 2:
                    wld.append((semPE, pe_n[(ns, c2 - 2)]))
                elif ns >= 1 and c2 < 2:
                    wld.append((semPE, pe_n[(ns - 1, NGD - 2 + c2)]))
                if ns == 0 and c2 < 2:
                    wld += [(sHST[0], sHST[0].v), (sHST[1], sHST[1].v), (sHC[0], sHC[0].v), (sHC[1], sHC[1].v)]
                for si, (p0, cnt, j, k8) in enumerate(segs):
                    E("sync", (lambda e, buf=buf, p0=p0, cnt=cnt, j=j, k8=k8, nb=nb: e.dma_start(
                        out=qwd_sb[buf][p0:p0 + cnt, :],
                        in_=d_qw[k8:k8 + cnt, nb:nb + 512])),
                        waits=(wld if si == 0 else ()), inc=sQD[buf], inc_n=16)
                    E("sync", (lambda e, buf=buf, p0=p0, cnt=cnt, j=j, k8=k8: e.dma_start(
                        out=htc9_sb[buf][p0:p0 + cnt, :],
                        in_=ht_d.ap()[8 * k8 + j: 8 * (k8 + cnt - 1) + j + 1: 8, :])),
                        inc=sQD[buf], inc_n=16)
                LD_C = sQD[buf].v
                for si, (p0, cnt, j, k8) in enumerate(segs):
                    g0 = (8 * k8 + j) // 128
                    ngr = cnt // 16
                    E("gpsimd", (lambda e, buf=buf, p0=p0, g0=g0, ngr=ngr, nb=nb: e.dma_start(
                        out=sed_sb[buf][p0:p0 + 16 * ngr, :],
                        in_=d_sc[g0:g0 + ngr, nb:nb + 512].unsqueeze(1).broadcast_to([ngr, 16, 512]))),
                        waits=(wld if si == 0 else ()), inc=sSD[buf], inc_n=16)
                LD_SED = sSD[buf].v

                wdq = [(sQD[buf], LD_C), (sSD[buf], LD_SED), (sDS, LD_DS)]
                if (ns, c2 - 2) in pe_n:
                    wdq.append((semPE, pe_n[(ns, c2 - 2)]))
                elif ns >= 1 and c2 < 2:
                    wdq.append((semPE, pe_n[(ns - 1, NGD - 2 + c2)]))
                if len(segs) == 1:
                    E("vector", (lambda e, buf=buf, j=segs[0][2]: e.tensor_scalar(
                        w4d_t[buf][:], qwd_sb[buf][:], int(4 * j), 15,
                        op0=ALU.logical_shift_right, op1=ALU.bitwise_and)),
                        waits=wdq, inc=semV)
                else:
                    pstar = segs[1][0]
                    j2 = segs[1][2]
                    j1 = segs[0][2]
                    E("vector", (lambda e, buf=buf, j2=j2: e.tensor_scalar(
                        w4d_t[buf][64:128, :], qwd_sb[buf][64:128, :], int(4 * j2), 15,
                        op0=ALU.logical_shift_right, op1=ALU.bitwise_and)),
                        waits=wdq, inc=semV)
                    if pstar == 32:
                        E("vector", (lambda e, buf=buf, j2=j2: e.tensor_scalar(
                            w4d_t[buf][32:64, :], qwd_sb[buf][32:64, :], int(4 * j2), 15,
                            op0=ALU.logical_shift_right, op1=ALU.bitwise_and)),
                            inc=semV)
                    E("vector", (lambda e, buf=buf, pstar=pstar, j1=j1: e.tensor_scalar(
                        w4d_t[buf][0:pstar, :], qwd_sb[buf][0:pstar, :], int(4 * j1), 15,
                        op0=ALU.logical_shift_right, op1=ALU.bitwise_and)),
                        inc=semV)
                E("vector", (lambda e, buf=buf: e.tensor_mul(
                    wd_bf[buf][:], w4d_t[buf][:], sed_sb[buf][:])), inc=semV)
                V_DQ = semV.v

                for mt in range(8):
                    wmm = []
                    if mt == 0:
                        wmm.append((semV, V_DQ))
                        if ns >= 1 and c2 == 0:
                            wmm.append((semA, a_g[ns - 1]))
                    E("tensor", (lambda e, buf=buf, mt=mt, c2=c2: e.matmul(
                        ps9[mt][:], htc9_sb[buf][:, mt * 128:(mt + 1) * 128], wd_bf[buf][:],
                        start=(c2 == 0), stop=False)), waits=wmm)
                em.mark_last("tensor", semPE)
                pe_n[(ns, c2)] = semPE.v

            for mt in range(8):
                E("tensor", (lambda e, mt=mt, nb=nb: e.matmul(
                    ps9[mt][:], xgd_sb[:, mt * 128:(mt + 1) * 128],
                    bdown_sb[:, nb:nb + 512], start=False, stop=True)))
            em.mark_last("tensor", semPE)
            PE_NS = semPE.v

            for mt in range(8):
                gb = mt % 2
                wsl = [(semPE, PE_NS)]
                if mt >= 2:
                    wsl.append((sGST[mt % 2], st_g[(ns, mt - 2)]))
                elif ns >= 1:
                    wsl.append((sGST[mt % 2], st_g[(ns - 1, 6 + mt)]))
                E("scalar", (lambda e, mt=mt, gb=gb: e.activation(
                    gtile[gb][:], ps9[mt][:], AF.Silu)), waits=wsl, inc=semA)
                a_g[ns] = semA.v
                A_G = semA.v
                E("sync", (lambda e, mt=mt, gb=gb, nb=nb: e.dma_start(
                    out=g_out[mt * 128:(mt + 1) * 128, nb:nb + 512], in_=gtile[gb][:])),
                    waits=[(semA, A_G)], inc=sGST[gb], inc_n=16)
                st_g[(ns, mt)] = sGST[gb].v
        em.barrier()

    _finish(nc, em, store_sems)
    glob.close()
    return nc


def _finish(nc, em, store_sems):
    if not isinstance(store_sems, (list, tuple)):
        store_sems = [store_sems]
    with nc.Block() as block:
        @block.sync
        def _(sync):
            em.replay("sync", sync)
            for s in store_sems:
                if s.v > 0:
                    sync.wait_ge(s.h, s.v)

        @block.gpsimd
        def _(gpsimd):
            em.replay("gpsimd", gpsimd)

        @block.vector
        def _(vector):
            em.replay("vector", vector)

        @block.scalar
        def _(scalar):
            em.replay("scalar", scalar)

        @block.tensor
        def _(tensor):
            em.replay("tensor", tensor)



def build_launch_b():
    nc = bass.Bass("TRN2", target_bir_lowering=False, debug=False)
    o_in = nc.dram_tensor("o_in", [TOK, D], BF16, kind="ExternalInput").ap()
    g_in = nc.dram_tensor("g_in", [TOK, D], F32, kind="ExternalInput").ap()
    x_in = nc.dram_tensor("x_in", [TOK, D], F32, kind="ExternalInput").ap()
    wo = nc.dram_tensor("wo", [D, D], F32, kind="ExternalInput").ap()
    ident = nc.dram_tensor("ident", [128, 128], BF16, kind="ExternalInput").ap()
    out_c = nc.dram_tensor("out_c", [TOK, D], F32, kind="ExternalOutput").ap()

    st = contextlib.ExitStack()
    T = lambda n, s, d: st.enter_context(nc.sbuf_tensor(n, s, d))
    P = lambda n, s, d: st.enter_context(nc.psum_tensor(n, s, d))

    ident_sb = T("ident_sb", [128, 128], BF16)
    ot_sb = T("ot_sb", [128, NCH, 1024], BF16)
    otile = [T(f"otile{i}", [128, D], BF16) for i in range(2)]
    wo_sb = [T(f"wo_sb{i}", [128, 8, 512], BF16) for i in range(2)]
    gt_sb = [T(f"gt{i}", [128, 512], F32) for i in range(2)]
    xt_sb = [T(f"xtb{i}", [128, 512], F32) for i in range(2)]
    res_sb = [T(f"res{i}", [128, 512], F32) for i in range(2)]
    tmp_sb = T("tmp_sb", [128, 512], F32)

    _allsems = []
    def mksem(n):
        s = SemCnt(st.enter_context(nc.semaphore(n)))
        _allsems.append(s)
        return s
    semPEb_ = None
    semPE = mksem("semPEb")
    semV = mksem("semVb")
    semA = mksem("semAb")
    em = Emitter(own_sems={"vector": semV, "scalar": semA})
    em.all_sems = _allsems
    sIDB = mksem("sIDB")
    sOB = [mksem("sOB0"), mksem("sOB1")]
    sWOB = [mksem("sWOB0"), mksem("sWOB1")]
    sGXB = [mksem("sGXB0"), mksem("sGXB1")]
    sOUTB = [mksem("sOUTB0"), mksem("sOUTB1")]
    E = em.emit

    E("sync", lambda e: e.dma_start(out=ident_sb[:], in_=ident), inc=sIDB, inc_n=16)
    S_ID = sIDB.v

    with nc.psum_tensor("psb_t", [128, 1024], BF16) as ps_t:
        v_tr = {}
        for m in range(8):
            buf = m % 2
            lw = [(sIDB, S_ID)]
            if m >= 2:
                lw.append((semV, v_tr[(m - 2, 3)]))
            E("sync", (lambda e, m=m, buf=buf: e.dma_start(
                out=otile[buf][:], in_=o_in[m * 128:(m + 1) * 128, :])),
                waits=lw, inc=sOB[buf], inc_n=16)
            LD_O = sOB[buf].v
            for rnd in range(4):
                wt = [(sOB[buf], LD_O)]
                if m == 0 and rnd == 0:
                    wt.append((sIDB, S_ID))
                if (m, rnd - 1) in v_tr:
                    wt.append((semV, v_tr[(m, rnd - 1)]))
                elif m >= 1 and rnd == 0:
                    wt.append((semV, v_tr[(m - 1, 3)]))
                for k in range(8):
                    ci = rnd * 8 + k
                    E("tensor", (lambda e, buf=buf, ci=ci, k=k: e.transpose(
                        ps_t[:, k * 128:(k + 1) * 128],
                        otile[buf][:, ci * 128:(ci + 1) * 128], ident_sb[:])),
                        waits=(wt if k == 0 else ()))
                em.mark_last("tensor", semPE)
                PE_T = semPE.v
                E("vector", (lambda e, rnd=rnd, m=m: e.tensor_copy(
                    ot_sb[:, rnd * 8:(rnd + 1) * 8, m * 128:(m + 1) * 128],
                    ps_t.rearrange("p (c n) -> p c n", c=8))),
                    waits=[(semPE, PE_T)], inc=semV)
                v_tr[(m, rnd)] = semV.v
        em.barrier()

    psb = [P(f"psb{i}", [128, 512], F32) for i in range(8)]
    v_done = {}
    st_out = {}
    pe_grp = {}
    for ns in range(8):
        nb = ns * 512
        for cg in range(4):
            buf = cg % 2
            lw = []
            if (ns, cg - 2) in pe_grp:
                lw.append((semPE, pe_grp[(ns, cg - 2)]))
            elif ns >= 1 and cg < 2:
                lw.append((semPE, pe_grp[(ns - 1, 2 + cg)]))
            E("gpsimd", (lambda e, buf=buf, cg=cg, nb=nb: e.dma_start(
                out=wo_sb[buf][:],
                in_=wo[cg * 1024:(cg + 1) * 1024, nb:nb + 512].rearrange("(c p) n -> p c n", p=128))),
                waits=lw, inc=sWOB[buf], inc_n=16)
            LD_WO = sWOB[buf].v
            for ci in range(8):
                for mt in range(8):
                    w0 = []
                    if ci == 0 and mt == 0:
                        w0 = [(sWOB[buf], LD_WO)]
                        if cg == 0 and ns >= 1:
                            w0.append((semV, v_done[ns - 1]))
                    E("tensor", (lambda e, buf=buf, cg=cg, ci=ci, mt=mt: e.matmul(
                        psb[mt][:], ot_sb[:, cg * 8 + ci, mt * 128:(mt + 1) * 128],
                        wo_sb[buf][:, ci, :],
                        start=(cg == 0 and ci == 0), stop=(cg == 3 and ci == 7))),
                        waits=w0)
            em.mark_last("tensor", semPE)
            pe_grp[(ns, cg)] = semPE.v
        PE_NS = semPE.v

        for mt in range(8):
            gb = mt % 2
            wld = [(semPE, PE_NS)]
            if mt >= 2:
                wld.append((sOUTB[gb], st_out[(ns, mt - 2)]))
            elif ns >= 1:
                wld.append((sOUTB[gb], st_out[(ns - 1, 6 + mt)]))
            E("sync", (lambda e, mt=mt, gb=gb, nb=nb: e.dma_start(
                out=gt_sb[gb][:], in_=g_in[mt * 128:(mt + 1) * 128, nb:nb + 512])),
                waits=wld, inc=sGXB[gb], inc_n=16)
            E("sync", (lambda e, mt=mt, gb=gb, nb=nb: e.dma_start(
                out=xt_sb[gb][:], in_=x_in[mt * 128:(mt + 1) * 128, nb:nb + 512])),
                inc=sGXB[gb], inc_n=16)
            LD_GX = sGXB[gb].v
            E("vector", (lambda e, mt=mt, gb=gb: e.tensor_mul(tmp_sb[:], psb[mt][:], gt_sb[gb][:])),
              waits=[(sGXB[gb], LD_GX)], inc=semV)
            E("vector", (lambda e, gb=gb: e.tensor_add(res_sb[gb][:], tmp_sb[:], xt_sb[gb][:])),
              inc=semV)
            v_done[ns] = semV.v
            V_R = semV.v
            E("sync", (lambda e, mt=mt, gb=gb, nb=nb: e.dma_start(
                out=out_c[mt * 128:(mt + 1) * 128, nb:nb + 512], in_=res_sb[gb][:])),
                waits=[(semV, V_R)], inc=sOUTB[gb], inc_n=16)
            st_out[(ns, mt)] = sOUTB[gb].v
    em.barrier()

    _finish(nc, em, sOUTB)
    st.close()
    return nc



def _host_consts():
    bf = ml_dtypes.bfloat16
    ident = np.eye(128, dtype=np.float32).astype(bf)
    identf = np.eye(128, dtype=np.float32)

    iidx = np.arange(128)[:, None]
    uidx = np.arange(896)[None, :]
    maskbin = (iidx <= uidx - 384).astype(np.float32).astype(bf)

    inv = 1.0 / (10000.0 ** (np.arange(0, ROT, 2) / ROT))
    t = np.arange(S, dtype=np.float64)
    sinus = np.outer(t, inv)
    rope_cos = np.repeat(np.cos(sinus), 2, axis=1).astype(np.float32)
    rope_sin = np.repeat(np.sin(sinus), 2, axis=1).astype(np.float32)

    e_gate = np.zeros((NCH, 128, NG), dtype=np.float32)
    for c2 in range(NCH):
        j, r = c2 // 4, c2 % 4
        k8 = r * 128 + np.arange(128)
        G = (8 * k8 + j) // 128
        e_gate[c2, np.arange(128), G] = -1.0
    e_gate = e_gate.reshape(NCH * 128, NG).astype(bf)

    e_down = np.zeros((NGD, 128, NGD), dtype=np.float32)
    dshift = np.zeros((128, NGD), dtype=np.int32)
    for c2 in range(NGD):
        kpp = 128 * c2 + np.arange(128)
        j = kpp // KP8D
        k8 = kpp % KP8D
        G = (8 * k8 + j) // 128
        e_down[c2, np.arange(128), G] = 1.0
        dshift[:, c2] = 4 * j
    e_down = e_down.reshape(NGD * 128, NGD).astype(bf)

    e2b = np.zeros((2, 256), dtype=np.float32)
    e2b[0, 0:128] = 1.0
    e2b[1, 128:256] = 1.0
    return dict(ident=ident, identf=identf, maskbin=maskbin, rope_cos=rope_cos,
                rope_sin=rope_sin, e_gate=e_gate, e_down=e_down, dshift=dshift,
                e2b=e2b)



def kernel(L0_attn_hidden_states, L0_attn_Wq, L0_attn_Wk, L0_attn_Wv, L0_attn_Wo,
           L0_ffn_gate_qweight, L0_ffn_gate_scales, L0_ffn_gate_qzeros, L0_ffn_gate_g_idx,
           L0_ffn_up_qweight, L0_ffn_up_scales, L0_ffn_up_qzeros, L0_ffn_up_g_idx,
           L0_ffn_down_qweight, L0_ffn_down_scales, L0_ffn_down_qzeros, L0_ffn_down_g_idx):
    x = np.ascontiguousarray(np.asarray(L0_attn_hidden_states, dtype=np.float32).reshape(S, D))
    xT = np.ascontiguousarray(x.T)
    consts = _host_consts()

    nc_a = build_launch_a()
    in_maps_a = []
    for c in range(NCORE):
        hs = c * HPC * HD
        m = dict(
            xT=xT,
            x_c=np.ascontiguousarray(x[c * TOK:(c + 1) * TOK, :]),
            wq_s=np.ascontiguousarray(np.asarray(L0_attn_Wq, np.float32)[:, hs:hs + HPC * HD]),
            wk_s=np.ascontiguousarray(np.asarray(L0_attn_Wk, np.float32)[:, hs:hs + HPC * HD]),
            wv_s=np.ascontiguousarray(np.asarray(L0_attn_Wv, np.float32)[:, hs:hs + HPC * HD]),
            rope_cos=consts["rope_cos"], rope_sin=consts["rope_sin"],
            g_qw=np.ascontiguousarray(np.asarray(L0_ffn_gate_qweight, np.int32)),
            u_qw=np.ascontiguousarray(np.asarray(L0_ffn_up_qweight, np.int32)),
            d_qw=np.ascontiguousarray(np.asarray(L0_ffn_down_qweight, np.int32)),
            g_sc=np.ascontiguousarray(np.asarray(L0_ffn_gate_scales, np.float32)),
            u_sc=np.ascontiguousarray(np.asarray(L0_ffn_up_scales, np.float32)),
            d_sc=np.ascontiguousarray(np.asarray(L0_ffn_down_scales, np.float32)),
            g_qz=np.ascontiguousarray(np.asarray(L0_ffn_gate_qzeros, np.int32)),
            u_qz=np.ascontiguousarray(np.asarray(L0_ffn_up_qzeros, np.int32)),
            d_qz=np.ascontiguousarray(np.asarray(L0_ffn_down_qzeros, np.int32)),
            ident=consts["ident"], identf=consts["identf"], maskbin=consts["maskbin"],
            e_gate=consts["e_gate"], e_down=consts["e_down"], dshift=consts["dshift"],
            e2b=consts["e2b"],
        )
        in_maps_a.append(m)

    res_a = run_bass_kernel_spmd(nc_a, in_maps_a, core_ids=list(range(NCORE)))
    o_full = np.concatenate([np.asarray(res_a.results[c]["o_out"]) for c in range(NCORE)], axis=1)
    g_full = [np.asarray(res_a.results[c]["g_out"]) for c in range(NCORE)]

    nc_b = build_launch_b()
    in_maps_b = []
    wo = np.ascontiguousarray(np.asarray(L0_attn_Wo, np.float32))
    for c in range(NCORE):
        in_maps_b.append(dict(
            o_in=np.ascontiguousarray(o_full[c * TOK:(c + 1) * TOK, :]),
            g_in=np.ascontiguousarray(g_full[c]),
            x_in=np.ascontiguousarray(x[c * TOK:(c + 1) * TOK, :]),
            wo=wo, ident=consts["ident"],
        ))
    res_b = run_bass_kernel_spmd(nc_b, in_maps_b, core_ids=list(range(NCORE)))
    out = np.concatenate([np.asarray(res_b.results[c]["out_c"]) for c in range(NCORE)], axis=0)
    return out.reshape(1, S, D).astype(np.float32)


# version 03
# speedup vs baseline: 1.2244x; 1.2244x over previous
import contextlib
import numpy as np
import ml_dtypes

import concourse.bass as bass
import concourse.mybir as mybir
from concourse.bass_utils import run_bass_kernel_spmd

F32 = mybir.dt.float32
BF16 = mybir.dt.bfloat16
I32 = mybir.dt.int32
AF = mybir.ActivationFunctionType
ALU = mybir.AluOpType
AX = mybir.AxisListType

S = 8192
D = 4096
H = 16
HD = 256
ROT = 64
INTER = 11008
NCORE = 8
HPC = H // NCORE
TOK = S // NCORE
NCH = D // 128
NG = D // 128
NGD = INTER // 128
KP8 = D // 8
KP8D = INTER // 8
EPS = 1e-5


class SemCnt:
    def __init__(self, handle):
        self.h = handle
        self.v = 0


class Emitter:
    ENGINES = ("sync", "gpsimd", "vector", "scalar", "tensor")

    def __init__(self, own_sems=None):
        self.streams = {e: [] for e in self.ENGINES}
        self.last = {e: None for e in self.ENGINES}
        self.own = own_sems or {}
        self.all_sems = []

    def emit(self, eng, fn, waits=(), inc=None, inc_n=1):
        rec = [(s.h, v) for (s, v) in waits if v > 0]
        own = self.own.get(eng)
        if own is not None and own.v > 0 and fn is not None:
            rec.append((own.h, own.v))
        inc_h = None
        if inc is not None:
            inc.v += inc_n
            inc_h = inc.h
            self.last[eng] = (inc, inc.v)
        self.streams[eng].append((fn, rec, inc_h, inc_n))

    def mark_last(self, eng, inc, inc_n=1):
        fn, waits, inc_h, n = self.streams[eng][-1]
        assert inc_h is None and fn is not None
        inc.v += inc_n
        self.last[eng] = (inc, inc.v)
        self.streams[eng][-1] = (fn, waits, inc.h, inc_n)

    def barrier(self):
        marks = [(s, s.v) for s in self.all_sems if s.v > 0]
        for eng in self.ENGINES:
            if marks:
                self.emit(eng, None, waits=marks)

    def replay(self, eng_name, eng_obj):
        for fn, waits, inc_h, inc_n in self.streams[eng_name]:
            for (h, v) in waits:
                eng_obj.wait_ge(h, v)
            ins = eng_obj.nop() if fn is None else fn(eng_obj)
            if inc_h is not None:
                ins.then_inc(inc_h, inc_n)


def _rearr_cpt(ap, p=128):
    return ap.rearrange("(c p) t -> p c t", p=p)


def _down_segments(c2):
    segs = []
    p = 0
    while p < 128:
        kpp = 128 * c2 + p
        j = kpp // KP8D
        k8 = kpp % KP8D
        cnt = min(128 - p, KP8D - k8)
        segs.append((p, cnt, j, k8))
        p += cnt
    return segs



def build_launch_a(upto=99):
    nc = bass.Bass("TRN2", target_bir_lowering=False, debug=False)

    xT = nc.dram_tensor("xT", [D, S], F32, kind="ExternalInput").ap()
    x_c = nc.dram_tensor("x_c", [TOK, D], F32, kind="ExternalInput").ap()
    wq_s = nc.dram_tensor("wq_s", [D, HPC * HD], F32, kind="ExternalInput").ap()
    wk_s = nc.dram_tensor("wk_s", [D, HPC * HD], F32, kind="ExternalInput").ap()
    wv_s = nc.dram_tensor("wv_s", [D, HPC * HD], F32, kind="ExternalInput").ap()
    rope_cos = nc.dram_tensor("rope_cos", [S, HPC * HD], F32, kind="ExternalInput").ap()
    rope_sin = nc.dram_tensor("rope_sin", [S, HPC * HD], F32, kind="ExternalInput").ap()
    g_qw = nc.dram_tensor("g_qw", [KP8, INTER], I32, kind="ExternalInput").ap()
    u_qw = nc.dram_tensor("u_qw", [KP8, INTER], I32, kind="ExternalInput").ap()
    d_qw = nc.dram_tensor("d_qw", [KP8D, D], I32, kind="ExternalInput").ap()
    g_sc = nc.dram_tensor("g_sc", [NG, INTER], F32, kind="ExternalInput").ap()
    u_sc = nc.dram_tensor("u_sc", [NG, INTER], F32, kind="ExternalInput").ap()
    d_sc = nc.dram_tensor("d_sc", [NGD, D], F32, kind="ExternalInput").ap()
    g_qz = nc.dram_tensor("g_qz", [NG, INTER // 8], I32, kind="ExternalInput").ap()
    u_qz = nc.dram_tensor("u_qz", [NG, INTER // 8], I32, kind="ExternalInput").ap()
    d_qz = nc.dram_tensor("d_qz", [NGD, D // 8], I32, kind="ExternalInput").ap()
    ident = nc.dram_tensor("ident", [128, 128], BF16, kind="ExternalInput").ap()
    identf = nc.dram_tensor("identf", [128, 128], F32, kind="ExternalInput").ap()
    maskbin = nc.dram_tensor("maskbin", [128, 896], BF16, kind="ExternalInput").ap()
    e_gate = nc.dram_tensor("e_gate", [NCH * 128, NG], BF16, kind="ExternalInput").ap()
    e_down = nc.dram_tensor("e_down", [NGD * 128, NGD], BF16, kind="ExternalInput").ap()
    dshift = nc.dram_tensor("dshift", [128, NGD], I32, kind="ExternalInput").ap()
    e2b = nc.dram_tensor("e2b", [2, 256], F32, kind="ExternalInput").ap()

    o_out = nc.dram_tensor("o_out", [S, HPC * HD], BF16, kind="ExternalOutput").ap()
    g_out = nc.dram_tensor("g_out", [TOK, D], F32, kind="ExternalOutput").ap()

    qt_d = nc.dram_tensor("qt_d", [4 * 128, S], BF16, kind="Internal")
    kt_d = nc.dram_tensor("kt_d", [4 * 128, S], BF16, kind="Internal")
    v_d = nc.dram_tensor("v_d", [S, HPC * HD], BF16, kind="Internal")
    ht_d = nc.dram_tensor("ht_d", [INTER, TOK], BF16, kind="Internal")

    glob = contextlib.ExitStack()
    Tg = lambda n, s, d: glob.enter_context(nc.sbuf_tensor(n, s, d))

    ident_sb = Tg("ident_sb", [128, 128], BF16)
    identf_sb = Tg("identf_sb", [128, 128], F32)
    onescol_bf = Tg("onescol_bf", [128, 1], BF16)
    ones11_f = Tg("ones11_f", [1, 1], F32)
    eps_sb = Tg("eps_sb", [1, 1], F32)
    epscol_sb = Tg("epscol_sb", [128, 1], F32)
    negmu_sb = Tg("negmu_sb", [1, S], BF16)
    rcol_all = Tg("rcol_all", [128, 64], F32)
    cwq_sb = Tg("cwq_sb", [1, 512], BF16)
    cwk_sb = Tg("cwk_sb", [1, 512], BF16)
    cwv_sb = Tg("cwv_sb", [1, 512], BF16)
    negb_col = Tg("negb_col", [128, 2], F32)
    negb_min = Tg("negb_min", [128, 1], F32)
    onesrow_f = Tg("onesrow_f", [1, 128], F32)

    def mksem(n):
        s = SemCnt(glob.enter_context(nc.semaphore(n)))
        _allsems.append(s)
        return s
    _allsems = []
    semPE = mksem("semPE")
    semV = mksem("semV")
    semA = mksem("semA")
    em = Emitter(own_sems={"vector": semV, "scalar": semA})
    em.all_sems = _allsems
    sCONST = mksem("sCONST")
    sWGT = mksem("sWGT")
    sXT = [mksem("sXT0"), mksem("sXT1")]
    sCOS = mksem("sCOS")
    sQST = [mksem("sQST0"), mksem("sQST1")]
    sVST = [mksem("sVST0"), mksem("sVST1")]
    sQTL = mksem("sQTL")
    sKV = [mksem("sKV0"), mksem("sKV1")]
    sOST = [mksem("sOST0"), mksem("sOST1")]
    sXC = [mksem("sXC0"), mksem("sXC1")]
    sZL = mksem("sZL")
    sEG = mksem("sEG")
    sED = mksem("sED")
    sDS = mksem("sDS")
    sQW = [mksem("sQW0"), mksem("sQW1")]
    sSE = [mksem("sSE0"), mksem("sSE1")]
    sHST = [mksem("sHST0"), mksem("sHST1")]
    sHC = [mksem("sHC0"), mksem("sHC1")]
    sQD = [mksem("sQD0"), mksem("sQD1")]
    sSD = [mksem("sSD0"), mksem("sSD1")]
    sGST = [mksem("sGST0"), mksem("sGST1")]
    store_sems = sQST + sVST + sOST + sHST + sGST
    E = em.emit

    E("sync", lambda e: e.dma_start(out=ident_sb[:], in_=ident), inc=sCONST, inc_n=16)
    E("sync", lambda e: e.dma_start(out=identf_sb[:], in_=identf), inc=sCONST, inc_n=16)
    S_CONST = sCONST.v
    E("vector", lambda e: e.memset(onescol_bf[:], 1.0), inc=semV)
    E("vector", lambda e: e.memset(ones11_f[:], 1.0), inc=semV)
    E("vector", lambda e: e.memset(eps_sb[:], EPS), inc=semV)
    E("vector", lambda e: e.memset(onesrow_f[:], 1.0), inc=semV)
    E("vector", lambda e: e.memset(epscol_sb[:], EPS), inc=semV)
    V_CONST = semV.v

    with contextlib.ExitStack() as ph:
        Tp = lambda n, s, d: ph.enter_context(nc.sbuf_tensor(n, s, d))
        Pp = lambda n, s, d: ph.enter_context(nc.psum_tensor(n, s, d))

        xt_sb = [Tp(f"xt{i}", [128, NCH, 512], BF16) for i in range(2)]
        wq_sb = Tp("wq_sb", [128, NCH, 512], BF16)
        wk_sb = Tp("wk_sb", [128, NCH, 512], BF16)
        sq_sb = [Tp(f"sq{i}", [128, 4, 512], BF16) for i in range(2)]
        rowt = Tp("rowt", [1, 512], F32)
        rowt2 = Tp("rowt2", [1, 512], F32)
        rowt3 = Tp("rowt3", [1, 512], F32)
        rstd_row = Tp("rstd_row", [1, 512], F32)
        qk_f32 = Tp("qk_f32", [128, 512], F32)
        q_bf = Tp("q_bf", [128, 512], BF16)
        k_bf = Tp("k_bf", [128, 512], BF16)
        rt1 = Tp("rt1", [128, 512], F32)
        rt2 = Tp("rt2", [128, 512], F32)
        cos_sb = Tp("cos_sb", [128, HPC * HD], F32)
        sin_sb = Tp("sin_sb", [128, HPC * HD], F32)
        rcol_sb = Tp("rcol_sb", [128, 4], F32)
        qtw_sb = [Tp(f"qtw_sb{i}", [128, 4, 512], BF16) for i in range(2)]
        ktw_sb = [Tp(f"ktw_sb{i}", [128, 4, 512], BF16) for i in range(2)]

        ps_rows = Pp("ps_rows", [1, 1024], F32)
        ps_q = [Pp(f"ps_q{i}", [128, 512], F32) for i in range(2)]
        ps_k = [Pp(f"ps_k{i}", [128, 512], F32) for i in range(2)]
        ps_t = Pp("ps_t", [128, 1024], BF16)
        ps_rc = Pp("ps_rc", [128, 4], F32)


        E("gpsimd", lambda e: e.dma_start(out=wq_sb[:], in_=_rearr_cpt(wq_s)), inc=sWGT, inc_n=16)
        E("gpsimd", lambda e: e.dma_start(out=wk_sb[:], in_=_rearr_cpt(wk_s)), inc=sWGT, inc_n=16)
        LD_W = sWGT.v

        for wi, wsb in ((0, wq_sb), (1, wk_sb)):
            for ci in range(NCH):
                E("tensor", (lambda e, wsb=wsb, ci=ci, wi=wi: e.matmul(
                    ps_rows[0:1, 512 * wi:512 * wi + 512], onescol_bf[:], wsb[:, ci, :],
                    start=(ci == 0), stop=(ci == NCH - 1))),
                    waits=([(sWGT, LD_W), (semV, V_CONST)] if (ci == 0 and wi == 0) else ()))
        em.mark_last("tensor", semPE)
        PE_CW = semPE.v
        E("vector", lambda e: e.tensor_copy(cwq_sb[:], ps_rows[0:1, 0:512]),
          waits=[(semPE, PE_CW)], inc=semV)
        E("vector", lambda e: e.tensor_copy(cwk_sb[:], ps_rows[0:1, 512:1024]), inc=semV)
        v_rows_free = semV.v

        pe_xt_done = {}
        v_evac = {}
        v_trev = {}
        st_qk = {}
        pe_sq = {}

        for g in range(S // 512):
            buf = g % 2
            lw = [(semPE, pe_xt_done[g - 2])] if g >= 2 else []
            E("gpsimd", (lambda e, buf=buf, g=g: e.dma_start(
                out=xt_sb[buf][:], in_=_rearr_cpt(xT)[:, :, g * 512:(g + 1) * 512])),
                waits=lw, inc=sXT[buf], inc_n=16)
            LD_G = sXT[buf].v

            for cg4 in range(NCH // 4):
                sb = cg4 % 2
                w = [(sXT[buf], LD_G)]
                if (g, cg4 - 2) in pe_sq:
                    w.append((semPE, pe_sq[(g, cg4 - 2)]))
                elif g > 0 and cg4 < 2:
                    w.append((semPE, pe_sq[(g - 1, NCH // 4 - 2 + cg4)]))
                E("vector", (lambda e, buf=buf, cg4=cg4, sb=sb: e.tensor_mul(
                    sq_sb[sb][:], xt_sb[buf][:, 4 * cg4:4 * cg4 + 4, :],
                    xt_sb[buf][:, 4 * cg4:4 * cg4 + 4, :])),
                    waits=w, inc=semV)
                V_SQ = semV.v
                for cr in range(4):
                    ci = 4 * cg4 + cr
                    wsum = [(sXT[buf], LD_G), (semV, v_rows_free)] if ci == 0 else ()
                    E("tensor", (lambda e, buf=buf, ci=ci: e.matmul(
                        ps_rows[0:1, 0:512], onescol_bf[:], xt_sb[buf][:, ci, :],
                        start=(ci == 0), stop=(ci == NCH - 1))), waits=wsum)
                    E("tensor", (lambda e, sb=sb, cr=cr, ci=ci: e.matmul(
                        ps_rows[0:1, 512:1024], onescol_bf[:], sq_sb[sb][:, cr, :],
                        start=(ci == 0), stop=(ci == NCH - 1))),
                        waits=([(semV, V_SQ)] if cr == 0 else ()))
                em.mark_last("tensor", semPE)
                pe_sq[(g, cg4)] = semPE.v
            PE_ROWS = semPE.v

            E("vector", lambda e: e.tensor_scalar(rowt[:], ps_rows[0:1, 0:512], 1.0 / D, None, op0=ALU.mult),
              waits=[(semPE, PE_ROWS)], inc=semV)
            E("vector", lambda e: e.tensor_scalar(rowt2[:], ps_rows[0:1, 512:1024], 1.0 / D, None, op0=ALU.mult), inc=semV)
            v_rows_free = semV.v
            E("vector", lambda e: e.tensor_mul(rowt3[:], rowt[:], rowt[:]), inc=semV)
            E("vector", lambda e: e.tensor_sub(rowt2[:], rowt2[:], rowt3[:]), inc=semV)
            V_VAR = semV.v
            E("scalar", lambda e: e.activation(rowt2[:], rowt2[:], AF.Ln, bias=eps_sb[0:1, 0:1]),
              waits=[(semV, V_VAR)], inc=semA)
            E("scalar", lambda e: e.activation(rstd_row[:], rowt2[:], AF.Exp, scale=-0.5), inc=semA)
            E("scalar", (lambda e, g=g: e.activation(
                negmu_sb[0:1, g * 512:(g + 1) * 512], rowt[:], AF.Copy, scale=-1.0)), inc=semA)
            A_ROWS = semA.v

            for mi in range(4):
                E("tensor", (lambda e, mi=mi: e.matmul(
                    ps_rc[:, mi:mi + 1], rstd_row[0:1, mi * 128:(mi + 1) * 128], ones11_f[:],
                    start=True, stop=True)),
                    waits=([(semA, A_ROWS)] if mi == 0 else ()))
            em.mark_last("tensor", semPE)
            PE_RC = semPE.v
            E("vector", lambda e: e.tensor_copy(rcol_sb[:], ps_rc[:]), waits=[(semPE, PE_RC)], inc=semV)
            E("vector", (lambda e, g=g: e.tensor_copy(rcol_all[:, g * 4:g * 4 + 4], rcol_sb[:])), inc=semV)
            V_RC = semV.v

            for mi in range(4):
                m = g * 4 + mi
                qb = m % 2
                E("sync", (lambda e, m=m: e.dma_start(out=cos_sb[:], in_=rope_cos[m * 128:(m + 1) * 128, :])),
                  waits=([(semV, v_evac[m - 1])] if m >= 1 else []), inc=sCOS, inc_n=16)
                E("sync", (lambda e, m=m: e.dma_start(out=sin_sb[:], in_=rope_sin[m * 128:(m + 1) * 128, :])),
                  inc=sCOS, inc_n=16)
                S_ROPE = sCOS.v

                for proj, psb, wsb, cwt in ((0, ps_q[qb], wq_sb, cwq_sb), (1, ps_k[qb], wk_sb, cwk_sb)):
                    w0 = [(semV, v_evac[m - 2])] if m >= 2 else []
                    for ci in range(NCH):
                        E("tensor", (lambda e, psb=psb, buf=buf, ci=ci, mi=mi, wsb=wsb: e.matmul(
                            psb[:], xt_sb[buf][:, ci, mi * 128:(mi + 1) * 128], wsb[:, ci, :],
                            start=(ci == 0), stop=False)),
                            waits=(w0 if ci == 0 else ()))
                    E("tensor", (lambda e, psb=psb, g=g, mi=mi, cwt=cwt: e.matmul(
                        psb[:], negmu_sb[0:1, g * 512 + mi * 128:g * 512 + (mi + 1) * 128],
                        cwt[:], start=False, stop=True)), inc=semPE)
                pe_xt_done[g] = semPE.v
                PE_QK = semPE.v

                for proj, psb, obf in ((0, ps_q[qb], q_bf), (1, ps_k[qb], k_bf)):
                    E("vector", (lambda e, psb=psb, mi=mi: e.tensor_scalar(
                        qk_f32[:], psb[:], rcol_sb[:, mi:mi + 1], None, op0=ALU.mult)),
                        waits=[(semPE, PE_QK), (semV, V_RC), (sCOS, S_ROPE)], inc=semV)
                    E("vector", lambda e: e.tensor_mul(rt1[:], qk_f32[:], cos_sb[:]), inc=semV)
                    E("vector", lambda e: e.tensor_mul(
                        rt2[:, 0:512:2], qk_f32[:, 1:512:2], sin_sb[:, 0:512:2]), inc=semV)
                    E("vector", lambda e: e.tensor_mul(
                        rt2[:, 1:512:2], qk_f32[:, 0:512:2], sin_sb[:, 1:512:2]), inc=semV)
                    E("vector", (lambda e, obf=obf: e.tensor_add(obf[:], rt1[:], rt2[:])), inc=semV)
                v_evac[m] = semV.v
                V_EV = semV.v

                wtr = [(semV, V_EV)]
                if m == 0:
                    wtr.append((sCONST, S_CONST))
                if m >= 1:
                    wtr.append((semV, v_trev[m - 1]))
                for proj, obf, half in ((0, q_bf, 0), (1, k_bf, 1)):
                    for dc in range(4):
                        E("tensor", (lambda e, obf=obf, dc=dc, half=half: e.transpose(
                            ps_t[:, half * 512 + dc * 128:half * 512 + (dc + 1) * 128],
                            obf[:, dc * 128:(dc + 1) * 128], ident_sb[:])),
                            waits=(wtr if (dc == 0 and proj == 0) else ()))
                em.mark_last("tensor", semPE)
                PE_TR = semPE.v
                wev = [(semPE, PE_TR)]
                if g >= 2 and mi == 0:
                    wev.append((sQST[g % 2], st_qk[g - 2]))
                E("vector", (lambda e, mi=mi, buf=buf: e.tensor_copy(
                    qtw_sb[buf][:, :, mi * 128:(mi + 1) * 128],
                    ps_t[:, 0:512].rearrange("p (c n) -> p c n", c=4))),
                    waits=wev, inc=semV)
                E("vector", (lambda e, mi=mi, buf=buf: e.tensor_copy(
                    ktw_sb[buf][:, :, mi * 128:(mi + 1) * 128],
                    ps_t[:, 512:1024].rearrange("p (c n) -> p c n", c=4))), inc=semV)
                v_trev[m] = semV.v

            wst = [(semV, v_trev[g * 4 + 3])]
            E("sync", (lambda e, g=g, buf=buf: e.dma_start(
                out=_rearr_cpt(qt_d.ap())[:, :, g * 512:(g + 1) * 512],
                in_=qtw_sb[buf][:])), waits=wst, inc=sQST[buf], inc_n=16)
            E("sync", (lambda e, g=g, buf=buf: e.dma_start(
                out=_rearr_cpt(kt_d.ap())[:, :, g * 512:(g + 1) * 512],
                in_=ktw_sb[buf][:])), inc=sQST[buf], inc_n=16)
            st_qk[g] = sQST[buf].v

        em.barrier()

    if upto <= 1:
        _finish(nc, em, store_sems)
        glob.close()
        return nc

    with contextlib.ExitStack() as ph:
        Tp = lambda n, s, d: ph.enter_context(nc.sbuf_tensor(n, s, d))
        Pp = lambda n, s, d: ph.enter_context(nc.psum_tensor(n, s, d))
        xtv_sb = [Tp(f"xtv{i}", [128, NCH, 512], BF16) for i in range(2)]
        wv_sb = Tp("wv_sb", [128, NCH, 512], BF16)
        v_bf = [Tp(f"v_bf{i}", [128, 512], BF16) for i in range(2)]
        ps_rowsv = Pp("ps_rowsvv", [1, 1024], F32)
        ps_v = [Pp(f"ps_v{i}", [128, 512], F32) for i in range(2)]

        E("gpsimd", lambda e: e.dma_start(out=wv_sb[:], in_=_rearr_cpt(wv_s)), inc=sWGT, inc_n=16)
        LD_WV = sWGT.v
        for ci in range(NCH):
            E("tensor", (lambda e, ci=ci: e.matmul(
                ps_rowsv[0:1, 0:512], onescol_bf[:], wv_sb[:, ci, :],
                start=(ci == 0), stop=(ci == NCH - 1))),
                waits=([(sWGT, LD_WV)] if ci == 0 else ()))
        em.mark_last("tensor", semPE)
        PE_CWV = semPE.v
        E("vector", lambda e: e.tensor_copy(cwv_sb[:], ps_rowsv[0:1, 0:512]),
          waits=[(semPE, PE_CWV)], inc=semV)
        V_CWV = semV.v

        v_ev = {}
        st_v = {}
        pe_g = {}
        for g in range(S // 512):
            buf = g % 2
            lw = [(semPE, pe_g[g - 2])] if g >= 2 else []
            E("gpsimd", (lambda e, buf=buf, g=g: e.dma_start(
                out=xtv_sb[buf][:], in_=_rearr_cpt(xT)[:, :, g * 512:(g + 1) * 512])),
                waits=lw, inc=sXT[buf], inc_n=16)
            LD_G = sXT[buf].v
            for mi in range(4):
                m = g * 4 + mi
                vb = m % 2
                w0 = [(sXT[buf], LD_G), (semV, V_CWV)]
                if m >= 2:
                    w0.append((semV, v_ev[m - 2]))
                for ci in range(NCH):
                    E("tensor", (lambda e, vb=vb, buf=buf, ci=ci, mi=mi: e.matmul(
                        ps_v[vb][:], xtv_sb[buf][:, ci, mi * 128:(mi + 1) * 128], wv_sb[:, ci, :],
                        start=(ci == 0), stop=False)),
                        waits=(w0 if ci == 0 else ()))
                E("tensor", (lambda e, vb=vb, g=g, mi=mi: e.matmul(
                    ps_v[vb][:], negmu_sb[0:1, g * 512 + mi * 128:g * 512 + (mi + 1) * 128],
                    cwv_sb[:], start=False, stop=True)), inc=semPE)
                pe_g[g] = semPE.v
                PE_V = semPE.v
                wv0 = [(semPE, PE_V)]
                if m >= 2:
                    wv0.append((sVST[m % 2], st_v[m - 2]))
                E("vector", (lambda e, vb=vb, m=m: e.tensor_scalar(
                    v_bf[vb][:], ps_v[vb][:], rcol_all[:, m:m + 1], None, op0=ALU.mult)),
                    waits=wv0, inc=semV)
                v_ev[m] = semV.v
                E("sync", (lambda e, m=m, vb=vb: e.dma_start(
                    out=v_d.ap()[m * 128:(m + 1) * 128, :], in_=v_bf[vb][:])),
                    waits=[(semV, v_ev[m])], inc=sVST[vb], inc_n=16)
                st_v[m] = sVST[vb].v
        em.barrier()

    with contextlib.ExitStack() as ph:
        Tp = lambda n, s, d: ph.enter_context(nc.sbuf_tensor(n, s, d))
        Pp = lambda n, s, d: ph.enter_context(nc.psum_tensor(n, s, d))
        qkt_sb = [Tp(f"qkt2b{i}", [128, S], BF16) for i in range(2)]
        sq2b = [Tp(f"sq2b{i}", [128, S], BF16) for i in range(2)]
        nrow = Tp("nrow", [1, S], F32)
        scal2 = Tp("scal2", [1, 8], F32)
        ps2b = Pp("ps2b", [1, 1024], F32)
        ps2c = Pp("ps2c", [128, 2], F32)

        for pi, tdram in ((0, qt_d), (1, kt_d)):
            for h in range(HPC):
                v_sq2 = []
                for dc in range(2):
                    ch = 2 * h + dc
                    wl = []
                    if pi + h + dc > 0:
                        wl.append((semV, semV.v))
                    E("sync", (lambda e, tdram=tdram, ch=ch, dc=dc: e.dma_start(
                        out=qkt_sb[dc][:], in_=tdram.ap()[128 * ch:128 * (ch + 1), :])),
                        waits=wl, inc=sQTL, inc_n=16)
                    LD2B = sQTL.v
                    E("vector", (lambda e, dc=dc: e.tensor_mul(
                        sq2b[dc][:], qkt_sb[dc][:], qkt_sb[dc][:])),
                        waits=[(sQTL, LD2B)], inc=semV)
                    v_sq2.append(semV.v)
                ev_done = 0
                for span in range(S // 512):
                    sl = (span % 2) * 512
                    wm = [(semV, v_sq2[1])]
                    if span >= 2:
                        wm.append((semV, ev_done))
                    E("tensor", (lambda e, span=span, sl=sl: e.matmul(
                        ps2b[0:1, sl:sl + 512], onescol_bf[:],
                        sq2b[0][:, span * 512:(span + 1) * 512], start=True, stop=False)),
                        waits=wm)
                    E("tensor", (lambda e, span=span, sl=sl: e.matmul(
                        ps2b[0:1, sl:sl + 512], onescol_bf[:],
                        sq2b[1][:, span * 512:(span + 1) * 512], start=False, stop=True)))
                    em.mark_last("tensor", semPE)
                    PE2B = semPE.v
                    E("vector", (lambda e, span=span, sl=sl: e.tensor_copy(
                        nrow[0:1, span * 512:(span + 1) * 512], ps2b[0:1, sl:sl + 512])),
                        waits=[(semPE, PE2B)], inc=semV)
                    ev_done = semV.v
                E("vector", (lambda e, pi=pi, h=h: e.reduce_max(
                    scal2[0:1, 2 * pi + h:2 * pi + h + 1], nrow[:], axis=AX.X)), inc=semV)
        V_NORMS = semV.v
        for h in range(HPC):
            E("scalar", (lambda e, h=h: e.activation(
                scal2[0:1, 4 + h:5 + h], scal2[0:1, h:h + 1], AF.Ln)),
                waits=[(semV, V_NORMS)], inc=semA)
            E("scalar", (lambda e, h=h: e.activation(
                scal2[0:1, 6 + h:7 + h], scal2[0:1, 2 + h:3 + h], AF.Ln)), inc=semA)
            A_LN2 = semA.v
            E("vector", (lambda e, h=h: e.tensor_add(
                scal2[0:1, 4 + h:5 + h], scal2[0:1, 4 + h:5 + h], scal2[0:1, 6 + h:7 + h])),
                waits=[(semA, A_LN2)], inc=semV)
            V_AD2 = semV.v
            E("scalar", (lambda e, h=h: e.activation(
                scal2[0:1, 4 + h:5 + h], scal2[0:1, 4 + h:5 + h], AF.Exp, scale=0.5)),
                waits=[(semV, V_AD2)], inc=semA)
            A_EX2 = semA.v
            E("vector", (lambda e, h=h: e.tensor_scalar(
                scal2[0:1, 4 + h:5 + h], scal2[0:1, 4 + h:5 + h], -1.0 / 16.0, None,
                op0=ALU.mult)), waits=[(semA, A_EX2)], inc=semV)
            V_NB = semV.v
            E("tensor", (lambda e, h=h: e.matmul(
                ps2c[:, h:h + 1], onesrow_f[:], scal2[0:1, 4 + h:5 + h],
                start=True, stop=True)), waits=[(semV, V_NB)])
            em.mark_last("tensor", semPE)
            PE2C = semPE.v
            E("vector", (lambda e, h=h: e.tensor_copy(negb_col[:, h:h + 1], ps2c[:, h:h + 1])),
              waits=[(semPE, PE2C)], inc=semV)
        E("vector", lambda e: e.tensor_tensor(
            negb_min[:], negb_col[:, 0:1], negb_col[:, 1:2], op=ALU.min), inc=semV)
        em.barrier()

    if upto <= 2:
        _finish(nc, em, store_sems)
        glob.close()
        return nc

    with contextlib.ExitStack() as ph:
        Tp = lambda n, s, d: ph.enter_context(nc.sbuf_tensor(n, s, d))
        Pp = lambda n, s, d: ph.enter_context(nc.psum_tensor(n, s, d))
        qt_sb = Tp("qt_sb", [128, 4, 512], BF16)
        kt_sb = [Tp(f"kt_s{i}", [128, 4, 512], BF16) for i in range(2)]
        v_sb = [Tp(f"v_s{i}", [128, 4, 512], BF16) for i in range(2)]
        pt_sb = [Tp(f"pt_s{i}", [128, 1024], BF16) for i in range(2)]
        mask_sb = Tp("mask_sb", [128, 896], BF16)
        oev_sb = Tp("oev_sb", [128, 2, 512], F32)
        lrow_pad = Tp("lrow_pad", [128, 512], F32)
        rl_sb = Tp("rl_sb", [128, 4], F32)
        o_sb = [Tp(f"o_sb{i}", [128, 4, 512], BF16) for i in range(2)]

        ps_s2 = Pp("ps_s2", [128, 1024], F32)
        ps_o = [Pp(f"ps_o{i}", [128, 512], F32) for i in range(4)]
        ps_lf = Pp("ps_lf", [1, 1024], F32)

        E("sync", lambda e: e.dma_start(out=mask_sb[:], in_=maskbin), inc=sCONST, inc_n=16)
        S_MASK = sCONST.v

        fin_done = 0
        st_o = {}
        for qc in range(16):
            nkc = 4 * (qc + 1)
            E("sync", (lambda e, qc=qc: e.dma_start(
                out=qt_sb[:], in_=_rearr_cpt(qt_d.ap())[:, :, qc * 512:(qc + 1) * 512])),
                waits=[(semV, fin_done), (sQST[0], sQST[0].v), (sQST[1], sQST[1].v)],
                inc=sQTL, inc_n=16)
            S_QT = sQTL.v

            ld_grp = {}
            kc_pe = {}
            kc_exp = {}
            for kc in range(nkc):
                gi = kc // 4
                gb = gi % 2
                kci = kc % 4
                ub = kc % 2
                if kci == 0:
                    w = [(semV, fin_done)]
                    if gi >= 2:
                        w.append((semPE, kc_pe[4 * (gi - 1) - 1]))
                    if qc == 0 and gi == 0:
                        w += [(sVST[0], sVST[0].v), (sVST[1], sVST[1].v)]
                    E("sync", (lambda e, gi=gi, gb=gb: e.dma_start(
                        out=kt_sb[gb][:],
                        in_=_rearr_cpt(kt_d.ap())[:, :, gi * 512:(gi + 1) * 512])),
                        waits=w, inc=sKV[gb], inc_n=16)
                    E("sync", (lambda e, gi=gi, gb=gb: e.dma_start(
                        out=v_sb[gb][:],
                        in_=v_d.ap()[gi * 512:(gi + 1) * 512, :].rearrange("(g p) d -> p g d", p=128))),
                        inc=sKV[gb], inc_n=16)
                    ld_grp[gi] = sKV[gb].v

                ws = [(sKV[gb], ld_grp[gi]), (sQTL, S_QT)]
                if kc >= 1:
                    ws.append((semA, kc_exp[kc - 1]))
                else:
                    ws.append((semV, fin_done))
                for h in range(HPC):
                    for dc in range(2):
                        E("tensor", (lambda e, gb=gb, h=h, dc=dc, kci=kci: e.matmul(
                            ps_s2[:, 512 * h:512 * h + 512],
                            kt_sb[gb][:, 2 * h + dc, kci * 128:(kci + 1) * 128],
                            qt_sb[:, 2 * h + dc, :], start=(dc == 0), stop=(dc == 1))),
                            waits=(ws if (h == 0 and dc == 0) else ()))
                em.mark_last("tensor", semPE)
                PE_S = semPE.v

                wexp = [(semPE, PE_S)]
                if kc >= 2:
                    wexp.append((semPE, kc_pe[kc - 2]))
                E("scalar", (lambda e, ub=ub: e.activation(
                    pt_sb[ub][:], ps_s2[:], AF.Exp, bias=negb_min[:, 0:1], scale=1.0 / 16.0)),
                    waits=wexp, inc=semA)
                kc_exp[kc] = semA.v
                pt_sem, pt_val = semA, semA.v

                kbase, qbase = kc * 128, qc * 512
                if kbase > qbase - 128:
                    off = 384 - (kbase - qbase)
                    E("vector", (lambda e, ub=ub, off=off: e.tensor_mul(
                        pt_sb[ub][:, 0:512], pt_sb[ub][:, 0:512], mask_sb[:, off:off + 512])),
                        waits=[(semA, pt_val), (sCONST, S_MASK)], inc=semV)
                    E("vector", (lambda e, ub=ub, off=off: e.tensor_mul(
                        pt_sb[ub][:, 512:1024], pt_sb[ub][:, 512:1024], mask_sb[:, off:off + 512])),
                        inc=semV)
                    pt_sem, pt_val = semV, semV.v

                for h in range(HPC):
                    E("tensor", (lambda e, gb=gb, h=h, kci=kci, ub=ub, kc=kc, nkc=nkc: e.matmul(
                        ps_o[2 * h][:], v_sb[gb][:, kci, h * HD:h * HD + 128],
                        pt_sb[ub][:, 512 * h:512 * h + 512],
                        start=(kc == 0), stop=(kc == nkc - 1))),
                        waits=([(pt_sem, pt_val)] if h == 0 else ()))
                    E("tensor", (lambda e, gb=gb, h=h, kci=kci, ub=ub, kc=kc, nkc=nkc: e.matmul(
                        ps_o[2 * h + 1][:], v_sb[gb][:, kci, h * HD + 128:h * HD + 256],
                        pt_sb[ub][:, 512 * h:512 * h + 512],
                        start=(kc == 0), stop=(kc == nkc - 1))))
                    E("tensor", (lambda e, h=h, ub=ub, kc=kc, nkc=nkc: e.matmul(
                        ps_lf[0:1, 512 * h:512 * h + 512], onescol_bf[:],
                        pt_sb[ub][:, 512 * h:512 * h + 512],
                        start=(kc == 0), stop=(kc == nkc - 1))))
                em.mark_last("tensor", semPE)
                kc_pe[kc] = semPE.v

            PE_UNITS = semPE.v
            A_LASTEXP = semA.v
            V_LASTMSK = semV.v

            for h in range(HPC):
                E("vector", (lambda e, h=h: e.tensor_copy(
                    lrow_pad[0:1, :], ps_lf[0:1, 512 * h:512 * h + 512])),
                    waits=[(semPE, PE_UNITS)], inc=semV)
                V_LR = semV.v
                for qs in range(4):
                    E("tensor", (lambda e, qs=qs: e.transpose(
                        ps_s2[:, qs * 128:(qs + 1) * 128],
                        lrow_pad[:, qs * 128:(qs + 1) * 128], identf_sb[:])),
                        waits=([(semV, V_LR), (semA, A_LASTEXP)] if qs == 0 else ()))
                em.mark_last("tensor", semPE)
                PE_LT = semPE.v
                E("vector", (lambda e: e.reciprocal(rl_sb[:], ps_s2[:, 0:512:128])),
                  waits=[(semPE, PE_LT)], inc=semV)
                V_RL = semV.v
                E("vector", (lambda e, h=h: e.tensor_copy(oev_sb[:, 0, :], ps_o[2 * h][:])), inc=semV)
                E("vector", (lambda e, h=h: e.tensor_copy(oev_sb[:, 1, :], ps_o[2 * h + 1][:])), inc=semV)
                V_OE = semV.v
                for pair in range(2):
                    bank = ps_s2[:, 512:1024] if pair == 0 else ps_s2[:, 0:512]
                    wtp = [(semV, V_OE)]
                    if pair == 0:
                        wtp.append((semV, V_RL))
                    else:
                        wtp.append((semV, semV.v))
                    for qi in range(2):
                        qs = pair * 2 + qi
                        for dc in range(2):
                            E("tensor", (lambda e, bank=bank, qi=qi, qs=qs, dc=dc: e.transpose(
                                bank[:, qi * 256 + dc * 128:qi * 256 + (dc + 1) * 128],
                                oev_sb[:, dc, qs * 128:(qs + 1) * 128], identf_sb[:])),
                                waits=(wtp if (qi == 0 and dc == 0) else ()))
                    em.mark_last("tensor", semPE)
                    PE_OT = semPE.v
                    for qi in range(2):
                        qs = pair * 2 + qi
                        wsc = [(semPE, PE_OT)] if qi == 0 else []
                        if h == 0 and pair == 0 and qi == 0 and qc >= 2:
                            wsc.append((sOST[qc % 2], st_o[qc - 2]))
                        E("vector", (lambda e, bank=bank, qi=qi, qs=qs, h=h, qc=qc: e.tensor_scalar(
                            o_sb[qc % 2][:, qs, h * HD:(h + 1) * HD], bank[:, qi * 256:(qi + 1) * 256],
                            rl_sb[:, qs:qs + 1], None, op0=ALU.mult)),
                            waits=wsc, inc=semV)
            fin_done = semV.v
            E("sync", (lambda e, qc=qc: e.dma_start(
                out=o_out[qc * 512:(qc + 1) * 512, :].rearrange("(q p) d -> p q d", p=128),
                in_=o_sb[qc % 2][:])), waits=[(semV, fin_done)], inc=sOST[qc % 2], inc_n=16)
            st_o[qc] = sOST[qc % 2].v
        em.barrier()

    if upto <= 3:
        _finish(nc, em, store_sems)
        glob.close()
        return nc

    xnt_sb = Tg("xnt_sb", [128, NCH, 1024], BF16)
    with contextlib.ExitStack() as ph:
        Tp = lambda n, s, d: ph.enter_context(nc.sbuf_tensor(n, s, d))
        Pp = lambda n, s, d: ph.enter_context(nc.psum_tensor(n, s, d))
        xtile = [Tp(f"xtile{i}", [128, D], F32) for i in range(2)]
        xsq = Tp("xsq", [128, D], F32)
        stat = Tp("stat", [128, 6], F32)
        xn_bf = Tp("xn_bf", [128, D], BF16)
        ps4 = Pp("ps4", [128, 1024], BF16)

        v_done = {}
        pe_tr = {}
        for m in range(8):
            buf = m % 2
            lw = [(semV, v_done[m - 2])] if m >= 2 else []
            E("sync", (lambda e, m=m, buf=buf: e.dma_start(
                out=xtile[buf][:], in_=x_c[m * 128:(m + 1) * 128, :])),
                waits=lw, inc=sXC[buf], inc_n=16)
            LD_X = sXC[buf].v
            E("vector", (lambda e, buf=buf: e.reduce_sum(stat[:, 0:1], xtile[buf][:], axis=AX.X)),
              waits=[(sXC[buf], LD_X)], inc=semV)
            E("vector", (lambda e, buf=buf: e.tensor_mul(xsq[:], xtile[buf][:], xtile[buf][:])), inc=semV)
            E("vector", lambda e: e.reduce_sum(stat[:, 1:2], xsq[:], axis=AX.X), inc=semV)
            E("vector", lambda e: e.tensor_scalar(stat[:, 2:3], stat[:, 0:1], 1.0 / D, None, op0=ALU.mult), inc=semV)
            E("vector", lambda e: e.tensor_scalar(stat[:, 3:4], stat[:, 1:2], 1.0 / D, None, op0=ALU.mult), inc=semV)
            E("vector", lambda e: e.tensor_mul(stat[:, 4:5], stat[:, 2:3], stat[:, 2:3]), inc=semV)
            E("vector", lambda e: e.tensor_sub(stat[:, 3:4], stat[:, 3:4], stat[:, 4:5]), inc=semV)
            V_ST = semV.v
            E("scalar", lambda e: e.activation(stat[:, 3:4], stat[:, 3:4], AF.Ln, bias=epscol_sb[:, 0:1]),
              waits=[(semV, V_ST)], inc=semA)
            E("scalar", lambda e: e.activation(stat[:, 5:6], stat[:, 3:4], AF.Exp, scale=-0.5), inc=semA)
            A_ST = semA.v
            E("vector", (lambda e, buf=buf: e.tensor_scalar(
                xn_bf[:], xtile[buf][:], stat[:, 2:3], stat[:, 5:6],
                op0=ALU.subtract, op1=ALU.mult)),
                waits=[(semA, A_ST)], inc=semV)
            v_done[m] = semV.v
            V_XN = semV.v

            for rnd in range(4):
                wt = [(semV, V_XN)]
                if (m, rnd - 1) in pe_tr:
                    wt.append((semV, pe_tr[(m, rnd - 1)]))
                elif m >= 1 and rnd == 0:
                    wt.append((semV, pe_tr[(m - 1, 3)]))
                for k in range(8):
                    c2 = rnd * 8 + k
                    j = c2 // 4
                    r = c2 % 4
                    E("tensor", (lambda e, j=j, r=r, k=k: e.transpose(
                        ps4[:, k * 128:(k + 1) * 128],
                        xn_bf[:, j + 8 * (r * 128): j + 8 * (r * 128 + 127) + 1: 8],
                        ident_sb[:])),
                        waits=(wt if k == 0 else ()))
                em.mark_last("tensor", semPE)
                PE_T4 = semPE.v
                E("vector", (lambda e, rnd=rnd, m=m: e.tensor_copy(
                    xnt_sb[:, rnd * 8:(rnd + 1) * 8, m * 128:(m + 1) * 128],
                    ps4.rearrange("p (c n) -> p c n", c=8))),
                    waits=[(semPE, PE_T4)], inc=semV)
                pe_tr[(m, rnd)] = semV.v
        em.barrier()

    if upto <= 4:
        _finish(nc, em, store_sems)
        glob.close()
        return nc

    bgate_sb = Tg("bgate_sb", [NG, INTER], BF16)
    bup_sb = Tg("bup_sb", [NG, INTER], BF16)
    bdown_sb = Tg("bdown_sb", [NGD, D], BF16)
    xg_sb = Tg("xg_sb", [NG, 1024], BF16)

    with contextlib.ExitStack() as ph:
        Tp = lambda n, s, d: ph.enter_context(nc.sbuf_tensor(n, s, d))
        Pp = lambda n, s, d: ph.enter_context(nc.psum_tensor(n, s, d))
        qz_sb = Tp("qz_sb", [NGD, 688], I32)
        zi_sb = Tp("zi_sb", [NGD, 5504], I32)
        zf_sb = Tp("zf_sb", [NGD, 5504], F32)
        sc5_sb = Tp("sc5_sb", [NGD, 5504], F32)
        egate_sb = Tp("egate_sb", [128, NCH, NG], BF16)
        ps_xg = Pp("ps_xg", [NG, 1024], F32)

        jobs = []
        for sp in range(2):
            jobs.append((g_qz, g_sc, bgate_sb, NG, 1376, INTER, sp, 1.0))
            jobs.append((u_qz, u_sc, bup_sb, NG, 1376, INTER, sp, 1.0))
        jobs.append((d_qz, d_sc, bdown_sb, NGD, 512, D, 0, -1.0))

        v_prev = 0
        for (qz, sc, bsb, ngr, n8, ntot, sp, sgn) in jobs:
            half8 = n8 // 2 if ntot == INTER else n8
            half = half8 * 8
            E("sync", (lambda e, qz=qz, ngr=ngr, sp=sp, half8=half8: e.dma_start(
                out=qz_sb[0:ngr, 0:half8], in_=qz[:, sp * half8:(sp + 1) * half8])),
                waits=([(semV, v_prev)] if v_prev else []), inc=sZL, inc_n=16)
            E("sync", (lambda e, sc=sc, ngr=ngr, sp=sp, half=half: e.dma_start(
                out=sc5_sb[0:ngr, 0:half], in_=sc[:, sp * half:(sp + 1) * half])),
                inc=sZL, inc_n=16)
            LD_Z = sZL.v
            for jn in range(8):
                E("vector", (lambda e, jn=jn, ngr=ngr, half8=half8: e.tensor_scalar(
                    zi_sb[0:ngr, jn:half8 * 8:8], qz_sb[0:ngr, 0:half8],
                    int(4 * jn), 15, op0=ALU.logical_shift_right, op1=ALU.bitwise_and)),
                    waits=([(sZL, LD_Z)] if jn == 0 else ()), inc=semV)
            E("vector", (lambda e, ngr=ngr, half=half: e.tensor_scalar(
                zf_sb[0:ngr, 0:half], zi_sb[0:ngr, 0:half], 1.0, None, op0=ALU.add)),
                inc=semV)
            if sgn != 1.0:
                E("vector", (lambda e, ngr=ngr, half=half, sgn=sgn: e.tensor_scalar(
                    sc5_sb[0:ngr, 0:half], sc5_sb[0:ngr, 0:half], sgn, None, op0=ALU.mult)),
                    inc=semV)
            E("vector", (lambda e, bsb=bsb, ngr=ngr, sp=sp, half=half: e.tensor_mul(
                bsb[0:ngr, sp * half:(sp + 1) * half], zf_sb[0:ngr, 0:half],
                sc5_sb[0:ngr, 0:half])), inc=semV)
            v_prev = semV.v
        em.barrier()

        E("sync", lambda e: e.dma_start(
            out=egate_sb[:], in_=e_gate.rearrange("(c p) g -> p c g", p=128)), inc=sEG, inc_n=16)
        LD_EG = sEG.v
        for half in range(2):
            for c2 in range(NCH):
                E("tensor", (lambda e, half=half, c2=c2: e.matmul(
                    ps_xg[:, half * 512:(half + 1) * 512], egate_sb[:, c2, :],
                    xnt_sb[:, c2, half * 512:(half + 1) * 512],
                    start=(c2 == 0), stop=(c2 == NCH - 1))),
                    waits=([(sEG, LD_EG)] if (half == 0 and c2 == 0) else ()))
        em.mark_last("tensor", semPE)
        PE_XG = semPE.v
        E("vector", lambda e: e.tensor_copy(xg_sb[:], ps_xg[:]), waits=[(semPE, PE_XG)], inc=semV)
        em.barrier()

    if upto <= 6:
        _finish(nc, em, store_sems)
        glob.close()
        return nc

    with contextlib.ExitStack() as ph:
        Tp = lambda n, s, d: ph.enter_context(nc.sbuf_tensor(n, s, d))
        Pp = lambda n, s, d: ph.enter_context(nc.psum_tensor(n, s, d))
        qwg_sb = [Tp(f"qwg{i}", [128, 4, 128], I32) for i in range(2)]
        qwu_sb = [Tp(f"qwu{i}", [128, 4, 128], I32) for i in range(2)]
        seg_sb = [Tp(f"seg{i}", [128, 4, 128], BF16) for i in range(2)]
        seu_sb = [Tp(f"seu{i}", [128, 4, 128], BF16) for i in range(2)]
        wg_all = [Tp(f"wg_all{i}", [128, NCH, 128], BF16) for i in range(2)]
        wu_all = [Tp(f"wu_all{i}", [128, NCH, 128], BF16) for i in range(2)]
        w4g = Tp("w4g", [128, 4, 128], I32)
        w4u = Tp("w4u", [128, 4, 128], I32)
        silu_t = Tp("silu_t", [128, 512], F32)
        htw = [Tp(f"htw{i}", [128, 1024], BF16) for i in range(2)]
        ps7 = [Pp(f"ps7_{i}", [128, 512], F32) for i in range(8)]

        NSP = INTER // 128
        pe_mm = {}
        v_deq = {}
        v_ht = {}
        st_ht = {}
        for sp in range(NSP):
            buf = sp % 2
            base = sp * 128
            st_ = (sp % 2) * 4
            lw = [(semV, v_deq[sp - 2])] if sp >= 2 else []
            E("sync", (lambda e, buf=buf, base=base: e.dma_start(
                out=qwg_sb[buf][:], in_=g_qw.rearrange("(r p) n -> p r n", p=128)[:, :, base:base + 128])),
                waits=lw, inc=sQW[buf], inc_n=16)
            E("sync", (lambda e, buf=buf, base=base: e.dma_start(
                out=qwu_sb[buf][:], in_=u_qw.rearrange("(r p) n -> p r n", p=128)[:, :, base:base + 128])),
                inc=sQW[buf], inc_n=16)
            LD_QW = sQW[buf].v
            for r in range(4):
                E("gpsimd", (lambda e, buf=buf, base=base, r=r: e.dma_start(
                    out=seg_sb[buf][:, r, :],
                    in_=g_sc[8 * r:8 * r + 8, base:base + 128].unsqueeze(1).broadcast_to([8, 16, 128]))),
                    waits=(lw if r == 0 else ()), inc=sSE[buf], inc_n=16)
                E("gpsimd", (lambda e, buf=buf, base=base, r=r: e.dma_start(
                    out=seu_sb[buf][:, r, :],
                    in_=u_sc[8 * r:8 * r + 8, base:base + 128].unsqueeze(1).broadcast_to([8, 16, 128]))),
                    inc=sSE[buf], inc_n=16)
            LD_SE = sSE[buf].v

            for j in range(8):
                wdq = [(sQW[buf], LD_QW), (sSE[buf], LD_SE)]
                if sp >= 2 and j == 0:
                    wdq.append((semPE, pe_mm[sp - 2]))
                E("vector", (lambda e, buf=buf, j=j: e.tensor_scalar(
                    w4g[:], qwg_sb[buf][:], int(4 * j), 15,
                    op0=ALU.logical_shift_right, op1=ALU.bitwise_and)),
                    waits=wdq, inc=semV)
                E("vector", (lambda e, buf=buf, j=j: e.tensor_mul(
                    wg_all[buf][:, 4 * j:4 * j + 4, :], w4g[:], seg_sb[buf][:])), inc=semV)
                E("vector", (lambda e, buf=buf, j=j: e.tensor_scalar(
                    w4u[:], qwu_sb[buf][:], int(4 * j), 15,
                    op0=ALU.logical_shift_right, op1=ALU.bitwise_and)), inc=semV)
                E("vector", (lambda e, buf=buf, j=j: e.tensor_mul(
                    wu_all[buf][:, 4 * j:4 * j + 4, :], w4u[:], seu_sb[buf][:])), inc=semV)
            v_deq[sp] = semV.v
            V_DEQ = semV.v

            wmm0 = [(semV, V_DEQ)]
            if sp >= 2:
                wmm0.append((semV, v_ht[sp - 2]))
            for c2 in range(NCH):
                for half in range(2):
                    E("tensor", (lambda e, buf=buf, half=half, c2=c2, st_=st_: e.matmul(
                        ps7[st_ + half][:], wg_all[buf][:, c2, :],
                        xnt_sb[:, c2, half * 512:(half + 1) * 512],
                        start=(c2 == 0), stop=False)),
                        waits=(wmm0 if (c2 == 0 and half == 0) else ()))
                    E("tensor", (lambda e, buf=buf, half=half, c2=c2, st_=st_: e.matmul(
                        ps7[st_ + 2 + half][:], wu_all[buf][:, c2, :],
                        xnt_sb[:, c2, half * 512:(half + 1) * 512],
                        start=(c2 == 0), stop=False)))
            for half in range(2):
                E("tensor", (lambda e, half=half, base=base, st_=st_: e.matmul(
                    ps7[st_ + half][:], bgate_sb[:, base:base + 128],
                    xg_sb[:, half * 512:(half + 1) * 512], start=False, stop=True)))
                E("tensor", (lambda e, half=half, base=base, st_=st_: e.matmul(
                    ps7[st_ + 2 + half][:], bup_sb[:, base:base + 128],
                    xg_sb[:, half * 512:(half + 1) * 512], start=False, stop=True)))
            em.mark_last("tensor", semPE)
            pe_mm[sp] = semPE.v
            PE_NT = semPE.v

            for half in range(2):
                E("scalar", (lambda e, half=half, st_=st_: e.activation(
                    silu_t[:], ps7[st_ + half][:], AF.Silu)),
                    waits=([(semPE, PE_NT)] if half == 0 else []) + [(semV, semV.v)],
                    inc=semA)
                A_SI = semA.v
                wht = [(semA, A_SI)]
                if sp >= 2 and half == 0:
                    wht.append((sHST[sp % 2], st_ht[sp - 2]))
                E("vector", (lambda e, buf=buf, half=half, st_=st_: e.tensor_mul(
                    htw[buf][:, half * 512:(half + 1) * 512], silu_t[:],
                    ps7[st_ + 2 + half][:])), waits=wht, inc=semV)
            v_ht[sp] = semV.v
            E("sync", (lambda e, buf=buf, base=base: e.dma_start(
                out=ht_d.ap()[base:base + 128, :], in_=htw[buf][:])),
                waits=[(semV, v_ht[sp])], inc=sHST[buf], inc_n=16)
            st_ht[sp] = sHST[buf].v
        em.barrier()

    if upto <= 7:
        _finish(nc, em, store_sems)
        glob.close()
        return nc

    xgd_sb = Tg("xgd_sb", [NGD, 1024], BF16)
    with contextlib.ExitStack() as ph:
        Tp = lambda n, s, d: ph.enter_context(nc.sbuf_tensor(n, s, d))
        Pp = lambda n, s, d: ph.enter_context(nc.psum_tensor(n, s, d))
        edown_sb = Tp("edown_sb", [128, NGD, NGD], BF16)
        htc_sb = [Tp(f"htc{i}", [128, 1024], BF16) for i in range(2)]
        ps_xgd = Pp("ps_xgd", [NGD, 1024], F32)

        E("sync", lambda e: e.dma_start(
            out=edown_sb[:], in_=e_down.rearrange("(c p) g -> p c g", p=128)), inc=sED, inc_n=16)
        LD_ED = sED.v
        pe_c = {}
        for c2 in range(NGD):
            buf = c2 % 2
            segs = _down_segments(c2)
            w = [(sED, LD_ED)]
            if c2 >= 2:
                w.append((semPE, pe_c[c2 - 2]))
            else:
                w += [(sHST[0], sHST[0].v), (sHST[1], sHST[1].v)]
            for si, (p0, cnt, j, k8) in enumerate(segs):
                E("sync", (lambda e, buf=buf, p0=p0, cnt=cnt, j=j, k8=k8: e.dma_start(
                    out=htc_sb[buf][p0:p0 + cnt, :],
                    in_=ht_d.ap()[8 * k8 + j: 8 * (k8 + cnt - 1) + j + 1: 8, :])),
                    waits=(w if si == 0 else ()), inc=sHC[buf], inc_n=16)
            LD_HT = sHC[buf].v
            for half in range(2):
                E("tensor", (lambda e, buf=buf, c2=c2, half=half: e.matmul(
                    ps_xgd[:, half * 512:(half + 1) * 512], edown_sb[:, c2, :],
                    htc_sb[buf][:, half * 512:(half + 1) * 512],
                    start=(c2 == 0), stop=(c2 == NGD - 1))),
                    waits=([(sHC[buf], LD_HT)] if half == 0 else ()))
            em.mark_last("tensor", semPE)
            pe_c[c2] = semPE.v
        PE_XGD = semPE.v
        E("vector", lambda e: e.tensor_copy(xgd_sb[:], ps_xgd[:]), waits=[(semPE, PE_XGD)], inc=semV)
        em.barrier()

    if upto <= 8:
        _finish(nc, em, store_sems)
        glob.close()
        return nc

    with contextlib.ExitStack() as ph:
        Tp = lambda n, s, d: ph.enter_context(nc.sbuf_tensor(n, s, d))
        Pp = lambda n, s, d: ph.enter_context(nc.psum_tensor(n, s, d))
        dshift_sb = Tp("dshift_sb", [128, NGD], I32)
        qwd_sb = [Tp(f"qwd{i}", [128, 512], I32) for i in range(2)]
        sed_sb = [Tp(f"sed{i}", [128, 512], BF16) for i in range(2)]
        htc9_sb = [Tp(f"htc9{i}", [128, 1024], BF16) for i in range(2)]
        w4d_t = [Tp(f"w4d_t{i}", [128, 512], I32) for i in range(2)]
        wd_bf = [Tp(f"wd_bf{i}", [128, 512], BF16) for i in range(2)]
        gtile = [Tp(f"gtile{i}", [128, 512], F32) for i in range(2)]
        ps9 = [Pp(f"ps9_{i}", [128, 512], F32) for i in range(8)]

        E("sync", lambda e: e.dma_start(out=dshift_sb[:], in_=dshift), inc=sDS, inc_n=16)
        LD_DS = sDS.v

        a_g = {}
        st_g = {}
        pe_n = {}
        for ns in range(8):
            nb = ns * 512
            for c2 in range(NGD):
                buf = c2 % 2
                segs = _down_segments(c2)
                wld = []
                if c2 >= 2:
                    wld.append((semPE, pe_n[(ns, c2 - 2)]))
                elif ns >= 1 and c2 < 2:
                    wld.append((semPE, pe_n[(ns - 1, NGD - 2 + c2)]))
                if ns == 0 and c2 < 2:
                    wld += [(sHST[0], sHST[0].v), (sHST[1], sHST[1].v), (sHC[0], sHC[0].v), (sHC[1], sHC[1].v)]
                for si, (p0, cnt, j, k8) in enumerate(segs):
                    E("sync", (lambda e, buf=buf, p0=p0, cnt=cnt, j=j, k8=k8, nb=nb: e.dma_start(
                        out=qwd_sb[buf][p0:p0 + cnt, :],
                        in_=d_qw[k8:k8 + cnt, nb:nb + 512])),
                        waits=(wld if si == 0 else ()), inc=sQD[buf], inc_n=16)
                    E("sync", (lambda e, buf=buf, p0=p0, cnt=cnt, j=j, k8=k8: e.dma_start(
                        out=htc9_sb[buf][p0:p0 + cnt, :],
                        in_=ht_d.ap()[8 * k8 + j: 8 * (k8 + cnt - 1) + j + 1: 8, :])),
                        inc=sQD[buf], inc_n=16)
                LD_C = sQD[buf].v
                for si, (p0, cnt, j, k8) in enumerate(segs):
                    g0 = (8 * k8 + j) // 128
                    ngr = cnt // 16
                    E("gpsimd", (lambda e, buf=buf, p0=p0, g0=g0, ngr=ngr, nb=nb: e.dma_start(
                        out=sed_sb[buf][p0:p0 + 16 * ngr, :],
                        in_=d_sc[g0:g0 + ngr, nb:nb + 512].unsqueeze(1).broadcast_to([ngr, 16, 512]))),
                        waits=(wld if si == 0 else ()), inc=sSD[buf], inc_n=16)
                LD_SED = sSD[buf].v

                wdq = [(sQD[buf], LD_C), (sSD[buf], LD_SED), (sDS, LD_DS)]
                if (ns, c2 - 2) in pe_n:
                    wdq.append((semPE, pe_n[(ns, c2 - 2)]))
                elif ns >= 1 and c2 < 2:
                    wdq.append((semPE, pe_n[(ns - 1, NGD - 2 + c2)]))
                if len(segs) == 1:
                    E("vector", (lambda e, buf=buf, j=segs[0][2]: e.tensor_scalar(
                        w4d_t[buf][:], qwd_sb[buf][:], int(4 * j), 15,
                        op0=ALU.logical_shift_right, op1=ALU.bitwise_and)),
                        waits=wdq, inc=semV)
                else:
                    pstar = segs[1][0]
                    j2 = segs[1][2]
                    j1 = segs[0][2]
                    E("vector", (lambda e, buf=buf, j2=j2: e.tensor_scalar(
                        w4d_t[buf][64:128, :], qwd_sb[buf][64:128, :], int(4 * j2), 15,
                        op0=ALU.logical_shift_right, op1=ALU.bitwise_and)),
                        waits=wdq, inc=semV)
                    if pstar == 32:
                        E("vector", (lambda e, buf=buf, j2=j2: e.tensor_scalar(
                            w4d_t[buf][32:64, :], qwd_sb[buf][32:64, :], int(4 * j2), 15,
                            op0=ALU.logical_shift_right, op1=ALU.bitwise_and)),
                            inc=semV)
                    E("vector", (lambda e, buf=buf, pstar=pstar, j1=j1: e.tensor_scalar(
                        w4d_t[buf][0:pstar, :], qwd_sb[buf][0:pstar, :], int(4 * j1), 15,
                        op0=ALU.logical_shift_right, op1=ALU.bitwise_and)),
                        inc=semV)
                E("vector", (lambda e, buf=buf: e.tensor_mul(
                    wd_bf[buf][:], w4d_t[buf][:], sed_sb[buf][:])), inc=semV)
                V_DQ = semV.v

                for mt in range(8):
                    wmm = []
                    if mt == 0:
                        wmm.append((semV, V_DQ))
                        if ns >= 1 and c2 == 0:
                            wmm.append((semA, a_g[ns - 1]))
                    E("tensor", (lambda e, buf=buf, mt=mt, c2=c2: e.matmul(
                        ps9[mt][:], htc9_sb[buf][:, mt * 128:(mt + 1) * 128], wd_bf[buf][:],
                        start=(c2 == 0), stop=False)), waits=wmm)
                em.mark_last("tensor", semPE)
                pe_n[(ns, c2)] = semPE.v

            for mt in range(8):
                E("tensor", (lambda e, mt=mt, nb=nb: e.matmul(
                    ps9[mt][:], xgd_sb[:, mt * 128:(mt + 1) * 128],
                    bdown_sb[:, nb:nb + 512], start=False, stop=True)))
            em.mark_last("tensor", semPE)
            PE_NS = semPE.v

            for mt in range(8):
                gb = mt % 2
                wsl = [(semPE, PE_NS)]
                if mt >= 2:
                    wsl.append((sGST[mt % 2], st_g[(ns, mt - 2)]))
                elif ns >= 1:
                    wsl.append((sGST[mt % 2], st_g[(ns - 1, 6 + mt)]))
                E("scalar", (lambda e, mt=mt, gb=gb: e.activation(
                    gtile[gb][:], ps9[mt][:], AF.Silu)), waits=wsl, inc=semA)
                a_g[ns] = semA.v
                A_G = semA.v
                E("sync", (lambda e, mt=mt, gb=gb, nb=nb: e.dma_start(
                    out=g_out[mt * 128:(mt + 1) * 128, nb:nb + 512], in_=gtile[gb][:])),
                    waits=[(semA, A_G)], inc=sGST[gb], inc_n=16)
                st_g[(ns, mt)] = sGST[gb].v
        em.barrier()

    _finish(nc, em, store_sems)
    glob.close()
    return nc


def _finish(nc, em, store_sems):
    if not isinstance(store_sems, (list, tuple)):
        store_sems = [store_sems]
    with nc.Block() as block:
        @block.sync
        def _(sync):
            em.replay("sync", sync)
            for s in store_sems:
                if s.v > 0:
                    sync.wait_ge(s.h, s.v)

        @block.gpsimd
        def _(gpsimd):
            em.replay("gpsimd", gpsimd)

        @block.vector
        def _(vector):
            em.replay("vector", vector)

        @block.scalar
        def _(scalar):
            em.replay("scalar", scalar)

        @block.tensor
        def _(tensor):
            em.replay("tensor", tensor)



def build_launch_b():
    nc = bass.Bass("TRN2", target_bir_lowering=False, debug=False)
    o_in = nc.dram_tensor("o_in", [TOK, D], BF16, kind="ExternalInput").ap()
    g_in = nc.dram_tensor("g_in", [TOK, D], F32, kind="ExternalInput").ap()
    x_in = nc.dram_tensor("x_in", [TOK, D], F32, kind="ExternalInput").ap()
    wo = nc.dram_tensor("wo", [D, D], F32, kind="ExternalInput").ap()
    ident = nc.dram_tensor("ident", [128, 128], BF16, kind="ExternalInput").ap()
    out_c = nc.dram_tensor("out_c", [TOK, D], F32, kind="ExternalOutput").ap()

    st = contextlib.ExitStack()
    T = lambda n, s, d: st.enter_context(nc.sbuf_tensor(n, s, d))
    P = lambda n, s, d: st.enter_context(nc.psum_tensor(n, s, d))

    ident_sb = T("ident_sb", [128, 128], BF16)
    ot_sb = T("ot_sb", [128, NCH, 1024], BF16)
    otile = [T(f"otile{i}", [128, D], BF16) for i in range(2)]
    wo_sb = [T(f"wo_sb{i}", [128, 8, 512], BF16) for i in range(2)]
    gt_sb = [T(f"gt{i}", [128, 512], F32) for i in range(2)]
    xt_sb = [T(f"xtb{i}", [128, 512], F32) for i in range(2)]
    res_sb = [T(f"res{i}", [128, 512], F32) for i in range(2)]
    tmp_sb = T("tmp_sb", [128, 512], F32)

    _allsems = []
    def mksem(n):
        s = SemCnt(st.enter_context(nc.semaphore(n)))
        _allsems.append(s)
        return s
    semPEb_ = None
    semPE = mksem("semPEb")
    semV = mksem("semVb")
    semA = mksem("semAb")
    em = Emitter(own_sems={"vector": semV, "scalar": semA})
    em.all_sems = _allsems
    sIDB = mksem("sIDB")
    sOB = [mksem("sOB0"), mksem("sOB1")]
    sWOB = [mksem("sWOB0"), mksem("sWOB1")]
    sGXB = [mksem("sGXB0"), mksem("sGXB1")]
    sOUTB = [mksem("sOUTB0"), mksem("sOUTB1")]
    E = em.emit

    E("sync", lambda e: e.dma_start(out=ident_sb[:], in_=ident), inc=sIDB, inc_n=16)
    S_ID = sIDB.v

    with nc.psum_tensor("psb_t", [128, 1024], BF16) as ps_t:
        v_tr = {}
        for m in range(8):
            buf = m % 2
            lw = [(sIDB, S_ID)]
            if m >= 2:
                lw.append((semV, v_tr[(m - 2, 3)]))
            E("sync", (lambda e, m=m, buf=buf: e.dma_start(
                out=otile[buf][:], in_=o_in[m * 128:(m + 1) * 128, :])),
                waits=lw, inc=sOB[buf], inc_n=16)
            LD_O = sOB[buf].v
            for rnd in range(4):
                wt = [(sOB[buf], LD_O)]
                if m == 0 and rnd == 0:
                    wt.append((sIDB, S_ID))
                if (m, rnd - 1) in v_tr:
                    wt.append((semV, v_tr[(m, rnd - 1)]))
                elif m >= 1 and rnd == 0:
                    wt.append((semV, v_tr[(m - 1, 3)]))
                for k in range(8):
                    ci = rnd * 8 + k
                    E("tensor", (lambda e, buf=buf, ci=ci, k=k: e.transpose(
                        ps_t[:, k * 128:(k + 1) * 128],
                        otile[buf][:, ci * 128:(ci + 1) * 128], ident_sb[:])),
                        waits=(wt if k == 0 else ()))
                em.mark_last("tensor", semPE)
                PE_T = semPE.v
                E("vector", (lambda e, rnd=rnd, m=m: e.tensor_copy(
                    ot_sb[:, rnd * 8:(rnd + 1) * 8, m * 128:(m + 1) * 128],
                    ps_t.rearrange("p (c n) -> p c n", c=8))),
                    waits=[(semPE, PE_T)], inc=semV)
                v_tr[(m, rnd)] = semV.v
        em.barrier()

    psb = [P(f"psb{i}", [128, 512], F32) for i in range(8)]
    v_done = {}
    st_out = {}
    pe_grp = {}
    for ns in range(8):
        nb = ns * 512
        for cg in range(4):
            buf = cg % 2
            lw = []
            if (ns, cg - 2) in pe_grp:
                lw.append((semPE, pe_grp[(ns, cg - 2)]))
            elif ns >= 1 and cg < 2:
                lw.append((semPE, pe_grp[(ns - 1, 2 + cg)]))
            E("gpsimd", (lambda e, buf=buf, cg=cg, nb=nb: e.dma_start(
                out=wo_sb[buf][:],
                in_=wo[cg * 1024:(cg + 1) * 1024, nb:nb + 512].rearrange("(c p) n -> p c n", p=128))),
                waits=lw, inc=sWOB[buf], inc_n=16)
            LD_WO = sWOB[buf].v
            for ci in range(8):
                for mt in range(8):
                    w0 = []
                    if ci == 0 and mt == 0:
                        w0 = [(sWOB[buf], LD_WO)]
                        if cg == 0 and ns >= 1:
                            w0.append((semV, v_done[ns - 1]))
                    E("tensor", (lambda e, buf=buf, cg=cg, ci=ci, mt=mt: e.matmul(
                        psb[mt][:], ot_sb[:, cg * 8 + ci, mt * 128:(mt + 1) * 128],
                        wo_sb[buf][:, ci, :],
                        start=(cg == 0 and ci == 0), stop=(cg == 3 and ci == 7))),
                        waits=w0)
            em.mark_last("tensor", semPE)
            pe_grp[(ns, cg)] = semPE.v
        PE_NS = semPE.v

        for mt in range(8):
            gb = mt % 2
            wld = [(semPE, PE_NS)]
            if mt >= 2:
                wld.append((sOUTB[gb], st_out[(ns, mt - 2)]))
            elif ns >= 1:
                wld.append((sOUTB[gb], st_out[(ns - 1, 6 + mt)]))
            E("sync", (lambda e, mt=mt, gb=gb, nb=nb: e.dma_start(
                out=gt_sb[gb][:], in_=g_in[mt * 128:(mt + 1) * 128, nb:nb + 512])),
                waits=wld, inc=sGXB[gb], inc_n=16)
            E("sync", (lambda e, mt=mt, gb=gb, nb=nb: e.dma_start(
                out=xt_sb[gb][:], in_=x_in[mt * 128:(mt + 1) * 128, nb:nb + 512])),
                inc=sGXB[gb], inc_n=16)
            LD_GX = sGXB[gb].v
            E("vector", (lambda e, mt=mt, gb=gb: e.tensor_mul(tmp_sb[:], psb[mt][:], gt_sb[gb][:])),
              waits=[(sGXB[gb], LD_GX)], inc=semV)
            E("vector", (lambda e, gb=gb: e.tensor_add(res_sb[gb][:], tmp_sb[:], xt_sb[gb][:])),
              inc=semV)
            v_done[ns] = semV.v
            V_R = semV.v
            E("sync", (lambda e, mt=mt, gb=gb, nb=nb: e.dma_start(
                out=out_c[mt * 128:(mt + 1) * 128, nb:nb + 512], in_=res_sb[gb][:])),
                waits=[(semV, V_R)], inc=sOUTB[gb], inc_n=16)
            st_out[(ns, mt)] = sOUTB[gb].v
    em.barrier()

    _finish(nc, em, sOUTB)
    st.close()
    return nc



def _host_consts():
    bf = ml_dtypes.bfloat16
    ident = np.eye(128, dtype=np.float32).astype(bf)
    identf = np.eye(128, dtype=np.float32)

    iidx = np.arange(128)[:, None]
    uidx = np.arange(896)[None, :]
    maskbin = (iidx <= uidx - 384).astype(np.float32).astype(bf)

    inv = 1.0 / (10000.0 ** (np.arange(0, ROT, 2) / ROT))
    t = np.arange(S, dtype=np.float64)
    sinus = np.outer(t, inv)
    cos64 = np.repeat(np.cos(sinus), 2, axis=1).astype(np.float32)
    sin64 = np.repeat(np.sin(sinus), 2, axis=1).astype(np.float32)
    rope_cos = np.ones((S, HPC * HD), np.float32)
    rope_sin = np.zeros((S, HPC * HD), np.float32)
    for h in range(HPC):
        rope_cos[:, h * HD:h * HD + ROT] = cos64
        rope_sin[:, h * HD:h * HD + ROT] = sin64

    e_gate = np.zeros((NCH, 128, NG), dtype=np.float32)
    for c2 in range(NCH):
        j, r = c2 // 4, c2 % 4
        k8 = r * 128 + np.arange(128)
        G = (8 * k8 + j) // 128
        e_gate[c2, np.arange(128), G] = -1.0
    e_gate = e_gate.reshape(NCH * 128, NG).astype(bf)

    e_down = np.zeros((NGD, 128, NGD), dtype=np.float32)
    dshift = np.zeros((128, NGD), dtype=np.int32)
    for c2 in range(NGD):
        kpp = 128 * c2 + np.arange(128)
        j = kpp // KP8D
        k8 = kpp % KP8D
        G = (8 * k8 + j) // 128
        e_down[c2, np.arange(128), G] = 1.0
        dshift[:, c2] = 4 * j
    e_down = e_down.reshape(NGD * 128, NGD).astype(bf)

    e2b = np.zeros((2, 256), dtype=np.float32)
    e2b[0, 0:128] = 1.0
    e2b[1, 128:256] = 1.0
    return dict(ident=ident, identf=identf, maskbin=maskbin, rope_cos=rope_cos,
                rope_sin=rope_sin, e_gate=e_gate, e_down=e_down, dshift=dshift,
                e2b=e2b)



def kernel(L0_attn_hidden_states, L0_attn_Wq, L0_attn_Wk, L0_attn_Wv, L0_attn_Wo,
           L0_ffn_gate_qweight, L0_ffn_gate_scales, L0_ffn_gate_qzeros, L0_ffn_gate_g_idx,
           L0_ffn_up_qweight, L0_ffn_up_scales, L0_ffn_up_qzeros, L0_ffn_up_g_idx,
           L0_ffn_down_qweight, L0_ffn_down_scales, L0_ffn_down_qzeros, L0_ffn_down_g_idx):
    x = np.ascontiguousarray(np.asarray(L0_attn_hidden_states, dtype=np.float32).reshape(S, D))
    xT = np.ascontiguousarray(x.T)
    consts = _host_consts()

    nc_a = build_launch_a()
    in_maps_a = []
    for c in range(NCORE):
        hs = c * HPC * HD
        m = dict(
            xT=xT,
            x_c=np.ascontiguousarray(x[c * TOK:(c + 1) * TOK, :]),
            wq_s=np.ascontiguousarray(np.asarray(L0_attn_Wq, np.float32)[:, hs:hs + HPC * HD]),
            wk_s=np.ascontiguousarray(np.asarray(L0_attn_Wk, np.float32)[:, hs:hs + HPC * HD]),
            wv_s=np.ascontiguousarray(np.asarray(L0_attn_Wv, np.float32)[:, hs:hs + HPC * HD]),
            rope_cos=consts["rope_cos"], rope_sin=consts["rope_sin"],
            g_qw=np.ascontiguousarray(np.asarray(L0_ffn_gate_qweight, np.int32)),
            u_qw=np.ascontiguousarray(np.asarray(L0_ffn_up_qweight, np.int32)),
            d_qw=np.ascontiguousarray(np.asarray(L0_ffn_down_qweight, np.int32)),
            g_sc=np.ascontiguousarray(np.asarray(L0_ffn_gate_scales, np.float32)),
            u_sc=np.ascontiguousarray(np.asarray(L0_ffn_up_scales, np.float32)),
            d_sc=np.ascontiguousarray(np.asarray(L0_ffn_down_scales, np.float32)),
            g_qz=np.ascontiguousarray(np.asarray(L0_ffn_gate_qzeros, np.int32)),
            u_qz=np.ascontiguousarray(np.asarray(L0_ffn_up_qzeros, np.int32)),
            d_qz=np.ascontiguousarray(np.asarray(L0_ffn_down_qzeros, np.int32)),
            ident=consts["ident"], identf=consts["identf"], maskbin=consts["maskbin"],
            e_gate=consts["e_gate"], e_down=consts["e_down"], dshift=consts["dshift"],
            e2b=consts["e2b"],
        )
        in_maps_a.append(m)

    res_a = run_bass_kernel_spmd(nc_a, in_maps_a, core_ids=list(range(NCORE)))
    o_full = np.concatenate([np.asarray(res_a.results[c]["o_out"]) for c in range(NCORE)], axis=1)
    g_full = [np.asarray(res_a.results[c]["g_out"]) for c in range(NCORE)]

    nc_b = build_launch_b()
    in_maps_b = []
    wo = np.ascontiguousarray(np.asarray(L0_attn_Wo, np.float32))
    for c in range(NCORE):
        in_maps_b.append(dict(
            o_in=np.ascontiguousarray(o_full[c * TOK:(c + 1) * TOK, :]),
            g_in=np.ascontiguousarray(g_full[c]),
            x_in=np.ascontiguousarray(x[c * TOK:(c + 1) * TOK, :]),
            wo=wo, ident=consts["ident"],
        ))
    res_b = run_bass_kernel_spmd(nc_b, in_maps_b, core_ids=list(range(NCORE)))
    out = np.concatenate([np.asarray(res_b.results[c]["out_c"]) for c in range(NCORE)], axis=0)
    return out.reshape(1, S, D).astype(np.float32)


# version 04
# speedup vs baseline: 1.5191x; 1.2407x over previous
import contextlib
import numpy as np
import ml_dtypes

import concourse.bass as bass
import concourse.mybir as mybir
from concourse.bass_utils import run_bass_kernel_spmd

F32 = mybir.dt.float32
BF16 = mybir.dt.bfloat16
I32 = mybir.dt.int32
AF = mybir.ActivationFunctionType
ALU = mybir.AluOpType
AX = mybir.AxisListType

S = 8192
D = 4096
H = 16
HD = 256
ROT = 64
INTER = 11008
NCORE = 8
HPC = H // NCORE
TOK = S // NCORE
NCH = D // 128
NG = D // 128
NGD = INTER // 128
KP8 = D // 8
KP8D = INTER // 8
EPS = 1e-5


class SemCnt:
    def __init__(self, handle):
        self.h = handle
        self.v = 0


class Emitter:
    ENGINES = ("sync", "gpsimd", "vector", "scalar", "tensor")

    def __init__(self, own_sems=None):
        self.streams = {e: [] for e in self.ENGINES}
        self.last = {e: None for e in self.ENGINES}
        self.own = own_sems or {}
        self.all_sems = []

    def emit(self, eng, fn, waits=(), inc=None, inc_n=1):
        rec = [(s.h, v) for (s, v) in waits if v > 0]
        own = self.own.get(eng)
        if own is not None and own.v > 0 and fn is not None:
            rec.append((own.h, own.v))
        inc_h = None
        if inc is not None:
            inc.v += inc_n
            inc_h = inc.h
            self.last[eng] = (inc, inc.v)
        self.streams[eng].append((fn, rec, inc_h, inc_n))

    def mark_last(self, eng, inc, inc_n=1):
        fn, waits, inc_h, n = self.streams[eng][-1]
        assert inc_h is None and fn is not None
        inc.v += inc_n
        self.last[eng] = (inc, inc.v)
        self.streams[eng][-1] = (fn, waits, inc.h, inc_n)

    def barrier(self):
        marks = [(s, s.v) for s in self.all_sems if s.v > 0]
        for eng in self.ENGINES:
            if marks:
                self.emit(eng, None, waits=marks)

    def replay(self, eng_name, eng_obj):
        for fn, waits, inc_h, inc_n in self.streams[eng_name]:
            for (h, v) in waits:
                eng_obj.wait_ge(h, v)
            ins = eng_obj.nop() if fn is None else fn(eng_obj)
            if inc_h is not None:
                ins.then_inc(inc_h, inc_n)


def _rearr_cpt(ap, p=128):
    return ap.rearrange("(c p) t -> p c t", p=p)


def _down_segments(c2):
    segs = []
    p = 0
    while p < 128:
        kpp = 128 * c2 + p
        j = kpp // KP8D
        k8 = kpp % KP8D
        cnt = min(128 - p, KP8D - k8)
        segs.append((p, cnt, j, k8))
        p += cnt
    return segs



def build_launch_a(upto=99):
    nc = bass.Bass("TRN2", target_bir_lowering=False, debug=False)

    xT = nc.dram_tensor("xT", [D, S], F32, kind="ExternalInput").ap()
    x_c = nc.dram_tensor("x_c", [TOK, D], F32, kind="ExternalInput").ap()
    wq_s = nc.dram_tensor("wq_s", [D, HPC * HD], F32, kind="ExternalInput").ap()
    wk_s = nc.dram_tensor("wk_s", [D, HPC * HD], F32, kind="ExternalInput").ap()
    wv_s = nc.dram_tensor("wv_s", [D, HPC * HD], F32, kind="ExternalInput").ap()
    rope_cos = nc.dram_tensor("rope_cos", [S, HPC * HD], F32, kind="ExternalInput").ap()
    rope_sin = nc.dram_tensor("rope_sin", [S, HPC * HD], F32, kind="ExternalInput").ap()
    g_qw = nc.dram_tensor("g_qw", [KP8, INTER], I32, kind="ExternalInput").ap()
    u_qw = nc.dram_tensor("u_qw", [KP8, INTER], I32, kind="ExternalInput").ap()
    d_qw = nc.dram_tensor("d_qw", [KP8D, D], I32, kind="ExternalInput").ap()
    g_sc = nc.dram_tensor("g_sc", [NG, INTER], F32, kind="ExternalInput").ap()
    u_sc = nc.dram_tensor("u_sc", [NG, INTER], F32, kind="ExternalInput").ap()
    d_sc = nc.dram_tensor("d_sc", [NGD, D], F32, kind="ExternalInput").ap()
    g_qz = nc.dram_tensor("g_qz", [NG, INTER // 8], I32, kind="ExternalInput").ap()
    u_qz = nc.dram_tensor("u_qz", [NG, INTER // 8], I32, kind="ExternalInput").ap()
    d_qz = nc.dram_tensor("d_qz", [NGD, D // 8], I32, kind="ExternalInput").ap()
    ident = nc.dram_tensor("ident", [128, 128], BF16, kind="ExternalInput").ap()
    identf = nc.dram_tensor("identf", [128, 128], F32, kind="ExternalInput").ap()
    maskbin = nc.dram_tensor("maskbin", [128, 896], BF16, kind="ExternalInput").ap()
    e_gate = nc.dram_tensor("e_gate", [NCH * 128, NG], BF16, kind="ExternalInput").ap()
    e_down = nc.dram_tensor("e_down", [NGD * 128, NGD], BF16, kind="ExternalInput").ap()
    dshift = nc.dram_tensor("dshift", [128, NGD], I32, kind="ExternalInput").ap()
    e2b = nc.dram_tensor("e2b", [2, 256], F32, kind="ExternalInput").ap()

    o_out = nc.dram_tensor("o_out", [S, HPC * HD], BF16, kind="ExternalOutput").ap()
    g_out = nc.dram_tensor("g_out", [TOK, D], F32, kind="ExternalOutput").ap()

    qt_d = nc.dram_tensor("qt_d", [4 * 128, S], BF16, kind="Internal")
    kt_d = nc.dram_tensor("kt_d", [4 * 128, S], BF16, kind="Internal")
    v_d = nc.dram_tensor("v_d", [S, HPC * HD], BF16, kind="Internal")
    ht_d = nc.dram_tensor("ht_d", [INTER, TOK], BF16, kind="Internal")

    glob = contextlib.ExitStack()
    Tg = lambda n, s, d: glob.enter_context(nc.sbuf_tensor(n, s, d))

    ident_sb = Tg("ident_sb", [128, 128], BF16)
    identf_sb = Tg("identf_sb", [128, 128], F32)
    onescol_bf = Tg("onescol_bf", [128, 1], BF16)
    ones11_f = Tg("ones11_f", [1, 1], F32)
    eps_sb = Tg("eps_sb", [1, 1], F32)
    epscol_sb = Tg("epscol_sb", [128, 1], F32)
    negmu_sb = Tg("negmu_sb", [1, S], BF16)
    rcol_all = Tg("rcol_all", [128, 64], F32)
    cwq_sb = Tg("cwq_sb", [1, 512], BF16)
    cwk_sb = Tg("cwk_sb", [1, 512], BF16)
    cwv_sb = Tg("cwv_sb", [1, 512], BF16)
    negb_col = Tg("negb_col", [128, 2], F32)
    negb_min = Tg("negb_min", [128, 1], F32)
    onesrow_f = Tg("onesrow_f", [1, 128], F32)

    def mksem(n):
        s = SemCnt(glob.enter_context(nc.semaphore(n)))
        _allsems.append(s)
        return s
    _allsems = []
    semPE = mksem("semPE")
    semV = mksem("semV")
    semA = mksem("semA")
    em = Emitter(own_sems={"vector": semV, "scalar": semA})
    em.all_sems = _allsems
    sCONST = mksem("sCONST")
    sWGT = mksem("sWGT")
    sXT = [mksem("sXT0"), mksem("sXT1")]
    sCOS = mksem("sCOS")
    sQST = [mksem("sQST0"), mksem("sQST1")]
    sVST = [mksem("sVST0"), mksem("sVST1")]
    sQTL = mksem("sQTL")
    sKV = [mksem("sKV0"), mksem("sKV1")]
    sOST = [mksem("sOST0"), mksem("sOST1")]
    sXC = [mksem("sXC0"), mksem("sXC1")]
    sZL = mksem("sZL")
    sEG = mksem("sEG")
    sED = mksem("sED")
    sDS = mksem("sDS")
    sQW = [mksem("sQW0"), mksem("sQW1")]
    sSE = [mksem("sSE0"), mksem("sSE1")]
    sHST = [mksem("sHST0"), mksem("sHST1")]
    sHC = [mksem("sHC0"), mksem("sHC1")]
    sQD = [mksem("sQD0"), mksem("sQD1")]
    sSD = [mksem("sSD0"), mksem("sSD1")]
    sGST = [mksem("sGST0"), mksem("sGST1")]
    store_sems = sQST + sVST + sOST + sHST + sGST
    E = em.emit

    E("sync", lambda e: e.dma_start(out=ident_sb[:], in_=ident), inc=sCONST, inc_n=16)
    E("sync", lambda e: e.dma_start(out=identf_sb[:], in_=identf), inc=sCONST, inc_n=16)
    S_CONST = sCONST.v
    E("vector", lambda e: e.memset(onescol_bf[:], 1.0), inc=semV)
    E("vector", lambda e: e.memset(ones11_f[:], 1.0), inc=semV)
    E("vector", lambda e: e.memset(eps_sb[:], EPS), inc=semV)
    E("vector", lambda e: e.memset(onesrow_f[:], 1.0), inc=semV)
    E("vector", lambda e: e.memset(epscol_sb[:], EPS), inc=semV)
    V_CONST = semV.v

    with contextlib.ExitStack() as ph:
        Tp = lambda n, s, d: ph.enter_context(nc.sbuf_tensor(n, s, d))
        Pp = lambda n, s, d: ph.enter_context(nc.psum_tensor(n, s, d))

        xt_sb = [Tp(f"xt{i}", [128, NCH, 512], BF16) for i in range(2)]
        wq_sb = Tp("wq_sb", [128, NCH, 512], BF16)
        wk_sb = Tp("wk_sb", [128, NCH, 512], BF16)
        sq_sb = [Tp(f"sq{i}", [128, 4, 512], BF16) for i in range(2)]
        rowt = Tp("rowt", [1, 512], F32)
        rowt2 = Tp("rowt2", [1, 512], F32)
        rowt3 = Tp("rowt3", [1, 512], F32)
        rstd_row = Tp("rstd_row", [1, 512], F32)
        qk_f32 = Tp("qk_f32", [128, 512], F32)
        q_bf = Tp("q_bf", [128, 512], BF16)
        k_bf = Tp("k_bf", [128, 512], BF16)
        rt1 = Tp("rt1", [128, 512], F32)
        rt2 = Tp("rt2", [128, 512], F32)
        cos_sb = Tp("cos_sb", [128, HPC * HD], F32)
        sin_sb = Tp("sin_sb", [128, HPC * HD], F32)
        rcol_sb = Tp("rcol_sb", [128, 4], F32)
        qtw_sb = [Tp(f"qtw_sb{i}", [128, 4, 512], BF16) for i in range(2)]
        ktw_sb = [Tp(f"ktw_sb{i}", [128, 4, 512], BF16) for i in range(2)]

        ps_rows = Pp("ps_rows", [1, 1024], F32)
        ps_q = [Pp(f"ps_q{i}", [128, 512], F32) for i in range(2)]
        ps_k = [Pp(f"ps_k{i}", [128, 512], F32) for i in range(2)]
        ps_t = Pp("ps_t", [128, 1024], BF16)
        ps_rc = Pp("ps_rc", [128, 4], F32)


        E("gpsimd", lambda e: e.dma_start(out=wq_sb[:], in_=_rearr_cpt(wq_s)), inc=sWGT, inc_n=16)
        E("gpsimd", lambda e: e.dma_start(out=wk_sb[:], in_=_rearr_cpt(wk_s)), inc=sWGT, inc_n=16)
        LD_W = sWGT.v

        for wi, wsb in ((0, wq_sb), (1, wk_sb)):
            for ci in range(NCH):
                E("tensor", (lambda e, wsb=wsb, ci=ci, wi=wi: e.matmul(
                    ps_rows[0:1, 512 * wi:512 * wi + 512], onescol_bf[:], wsb[:, ci, :],
                    start=(ci == 0), stop=(ci == NCH - 1))),
                    waits=([(sWGT, LD_W), (semV, V_CONST)] if (ci == 0 and wi == 0) else ()))
        em.mark_last("tensor", semPE)
        PE_CW = semPE.v
        E("vector", lambda e: e.tensor_copy(cwq_sb[:], ps_rows[0:1, 0:512]),
          waits=[(semPE, PE_CW)], inc=semV)
        E("vector", lambda e: e.tensor_copy(cwk_sb[:], ps_rows[0:1, 512:1024]), inc=semV)
        v_rows_free = semV.v

        pe_xt_done = {}
        v_evac = {}
        v_trev = {}
        st_qk = {}
        pe_sq = {}

        for g in range(S // 512):
            buf = g % 2
            lw = [(semPE, pe_xt_done[g - 2])] if g >= 2 else []
            E("gpsimd", (lambda e, buf=buf, g=g: e.dma_start(
                out=xt_sb[buf][:], in_=_rearr_cpt(xT)[:, :, g * 512:(g + 1) * 512])),
                waits=lw, inc=sXT[buf], inc_n=16)
            LD_G = sXT[buf].v

            for cg4 in range(NCH // 4):
                sb = cg4 % 2
                w = [(sXT[buf], LD_G)]
                if (g, cg4 - 2) in pe_sq:
                    w.append((semPE, pe_sq[(g, cg4 - 2)]))
                elif g > 0 and cg4 < 2:
                    w.append((semPE, pe_sq[(g - 1, NCH // 4 - 2 + cg4)]))
                E("vector", (lambda e, buf=buf, cg4=cg4, sb=sb: e.tensor_mul(
                    sq_sb[sb][:], xt_sb[buf][:, 4 * cg4:4 * cg4 + 4, :],
                    xt_sb[buf][:, 4 * cg4:4 * cg4 + 4, :])),
                    waits=w, inc=semV)
                V_SQ = semV.v
                for cr in range(4):
                    ci = 4 * cg4 + cr
                    wsum = [(sXT[buf], LD_G), (semV, v_rows_free)] if ci == 0 else ()
                    E("tensor", (lambda e, buf=buf, ci=ci: e.matmul(
                        ps_rows[0:1, 0:512], onescol_bf[:], xt_sb[buf][:, ci, :],
                        start=(ci == 0), stop=(ci == NCH - 1))), waits=wsum)
                    E("tensor", (lambda e, sb=sb, cr=cr, ci=ci: e.matmul(
                        ps_rows[0:1, 512:1024], onescol_bf[:], sq_sb[sb][:, cr, :],
                        start=(ci == 0), stop=(ci == NCH - 1))),
                        waits=([(semV, V_SQ)] if cr == 0 else ()))
                em.mark_last("tensor", semPE)
                pe_sq[(g, cg4)] = semPE.v
            PE_ROWS = semPE.v

            E("vector", lambda e: e.tensor_scalar(rowt[:], ps_rows[0:1, 0:512], 1.0 / D, None, op0=ALU.mult),
              waits=[(semPE, PE_ROWS)], inc=semV)
            E("vector", lambda e: e.tensor_scalar(rowt2[:], ps_rows[0:1, 512:1024], 1.0 / D, None, op0=ALU.mult), inc=semV)
            v_rows_free = semV.v
            E("vector", lambda e: e.tensor_mul(rowt3[:], rowt[:], rowt[:]), inc=semV)
            E("vector", lambda e: e.tensor_sub(rowt2[:], rowt2[:], rowt3[:]), inc=semV)
            V_VAR = semV.v
            E("scalar", lambda e: e.activation(rowt2[:], rowt2[:], AF.Ln, bias=eps_sb[0:1, 0:1]),
              waits=[(semV, V_VAR)], inc=semA)
            E("scalar", lambda e: e.activation(rstd_row[:], rowt2[:], AF.Exp, scale=-0.5), inc=semA)
            E("scalar", (lambda e, g=g: e.activation(
                negmu_sb[0:1, g * 512:(g + 1) * 512], rowt[:], AF.Copy, scale=-1.0)), inc=semA)
            A_ROWS = semA.v

            for mi in range(4):
                E("tensor", (lambda e, mi=mi: e.matmul(
                    ps_rc[:, mi:mi + 1], rstd_row[0:1, mi * 128:(mi + 1) * 128], ones11_f[:],
                    start=True, stop=True)),
                    waits=([(semA, A_ROWS)] if mi == 0 else ()))
            em.mark_last("tensor", semPE)
            PE_RC = semPE.v
            E("vector", lambda e: e.tensor_copy(rcol_sb[:], ps_rc[:]), waits=[(semPE, PE_RC)], inc=semV)
            E("vector", (lambda e, g=g: e.tensor_copy(rcol_all[:, g * 4:g * 4 + 4], rcol_sb[:])), inc=semV)
            V_RC = semV.v

            for mi in range(4):
                m = g * 4 + mi
                qb = m % 2
                E("sync", (lambda e, m=m: e.dma_start(out=cos_sb[:], in_=rope_cos[m * 128:(m + 1) * 128, :])),
                  waits=([(semV, v_evac[m - 1])] if m >= 1 else []), inc=sCOS, inc_n=16)
                E("sync", (lambda e, m=m: e.dma_start(out=sin_sb[:], in_=rope_sin[m * 128:(m + 1) * 128, :])),
                  inc=sCOS, inc_n=16)
                S_ROPE = sCOS.v

                for proj, psb, wsb, cwt in ((0, ps_q[qb], wq_sb, cwq_sb), (1, ps_k[qb], wk_sb, cwk_sb)):
                    w0 = [(semV, v_evac[m - 2])] if m >= 2 else []
                    for ci in range(NCH):
                        E("tensor", (lambda e, psb=psb, buf=buf, ci=ci, mi=mi, wsb=wsb: e.matmul(
                            psb[:], xt_sb[buf][:, ci, mi * 128:(mi + 1) * 128], wsb[:, ci, :],
                            start=(ci == 0), stop=False)),
                            waits=(w0 if ci == 0 else ()))
                    E("tensor", (lambda e, psb=psb, g=g, mi=mi, cwt=cwt: e.matmul(
                        psb[:], negmu_sb[0:1, g * 512 + mi * 128:g * 512 + (mi + 1) * 128],
                        cwt[:], start=False, stop=True)), inc=semPE)
                pe_xt_done[g] = semPE.v
                PE_QK = semPE.v

                for proj, psb, obf in ((0, ps_q[qb], q_bf), (1, ps_k[qb], k_bf)):
                    E("vector", (lambda e, psb=psb, mi=mi: e.tensor_scalar(
                        qk_f32[:], psb[:], rcol_sb[:, mi:mi + 1], None, op0=ALU.mult)),
                        waits=[(semPE, PE_QK), (semV, V_RC), (sCOS, S_ROPE)], inc=semV)
                    E("vector", lambda e: e.tensor_mul(rt1[:], qk_f32[:], cos_sb[:]), inc=semV)
                    E("vector", lambda e: e.tensor_mul(
                        rt2[:, 0:512:2], qk_f32[:, 1:512:2], sin_sb[:, 0:512:2]), inc=semV)
                    E("vector", lambda e: e.tensor_mul(
                        rt2[:, 1:512:2], qk_f32[:, 0:512:2], sin_sb[:, 1:512:2]), inc=semV)
                    E("vector", (lambda e, obf=obf: e.tensor_add(obf[:], rt1[:], rt2[:])), inc=semV)
                v_evac[m] = semV.v
                V_EV = semV.v

                wtr = [(semV, V_EV)]
                if m == 0:
                    wtr.append((sCONST, S_CONST))
                if m >= 1:
                    wtr.append((semV, v_trev[m - 1]))
                for proj, obf, half in ((0, q_bf, 0), (1, k_bf, 1)):
                    for dc in range(4):
                        E("tensor", (lambda e, obf=obf, dc=dc, half=half: e.transpose(
                            ps_t[:, half * 512 + dc * 128:half * 512 + (dc + 1) * 128],
                            obf[:, dc * 128:(dc + 1) * 128], ident_sb[:])),
                            waits=(wtr if (dc == 0 and proj == 0) else ()))
                em.mark_last("tensor", semPE)
                PE_TR = semPE.v
                wev = [(semPE, PE_TR)]
                if g >= 2 and mi == 0:
                    wev.append((sQST[g % 2], st_qk[g - 2]))
                E("vector", (lambda e, mi=mi, buf=buf: e.tensor_copy(
                    qtw_sb[buf][:, :, mi * 128:(mi + 1) * 128],
                    ps_t[:, 0:512].rearrange("p (c n) -> p c n", c=4))),
                    waits=wev, inc=semV)
                E("vector", (lambda e, mi=mi, buf=buf: e.tensor_copy(
                    ktw_sb[buf][:, :, mi * 128:(mi + 1) * 128],
                    ps_t[:, 512:1024].rearrange("p (c n) -> p c n", c=4))), inc=semV)
                v_trev[m] = semV.v

            wst = [(semV, v_trev[g * 4 + 3])]
            E("sync", (lambda e, g=g, buf=buf: e.dma_start(
                out=_rearr_cpt(qt_d.ap())[:, :, g * 512:(g + 1) * 512],
                in_=qtw_sb[buf][:])), waits=wst, inc=sQST[buf], inc_n=16)
            E("sync", (lambda e, g=g, buf=buf: e.dma_start(
                out=_rearr_cpt(kt_d.ap())[:, :, g * 512:(g + 1) * 512],
                in_=ktw_sb[buf][:])), inc=sQST[buf], inc_n=16)
            st_qk[g] = sQST[buf].v

        em.barrier()

    if upto <= 1:
        _finish(nc, em, store_sems)
        glob.close()
        return nc

    with contextlib.ExitStack() as ph:
        Tp = lambda n, s, d: ph.enter_context(nc.sbuf_tensor(n, s, d))
        Pp = lambda n, s, d: ph.enter_context(nc.psum_tensor(n, s, d))
        xtv_sb = [Tp(f"xtv{i}", [128, NCH, 512], BF16) for i in range(2)]
        wv_sb = Tp("wv_sb", [128, NCH, 512], BF16)
        v_bf = [Tp(f"v_bf{i}", [128, 512], BF16) for i in range(2)]
        ps_rowsv = Pp("ps_rowsvv", [1, 1024], F32)
        ps_v = [Pp(f"ps_v{i}", [128, 512], F32) for i in range(2)]

        E("gpsimd", lambda e: e.dma_start(out=wv_sb[:], in_=_rearr_cpt(wv_s)), inc=sWGT, inc_n=16)
        LD_WV = sWGT.v
        for ci in range(NCH):
            E("tensor", (lambda e, ci=ci: e.matmul(
                ps_rowsv[0:1, 0:512], onescol_bf[:], wv_sb[:, ci, :],
                start=(ci == 0), stop=(ci == NCH - 1))),
                waits=([(sWGT, LD_WV)] if ci == 0 else ()))
        em.mark_last("tensor", semPE)
        PE_CWV = semPE.v
        E("vector", lambda e: e.tensor_copy(cwv_sb[:], ps_rowsv[0:1, 0:512]),
          waits=[(semPE, PE_CWV)], inc=semV)
        V_CWV = semV.v

        v_ev = {}
        st_v = {}
        pe_g = {}
        for g in range(S // 512):
            buf = g % 2
            lw = [(semPE, pe_g[g - 2])] if g >= 2 else []
            E("gpsimd", (lambda e, buf=buf, g=g: e.dma_start(
                out=xtv_sb[buf][:], in_=_rearr_cpt(xT)[:, :, g * 512:(g + 1) * 512])),
                waits=lw, inc=sXT[buf], inc_n=16)
            LD_G = sXT[buf].v
            for mi in range(4):
                m = g * 4 + mi
                vb = m % 2
                w0 = [(sXT[buf], LD_G), (semV, V_CWV)]
                if m >= 2:
                    w0.append((semV, v_ev[m - 2]))
                for ci in range(NCH):
                    E("tensor", (lambda e, vb=vb, buf=buf, ci=ci, mi=mi: e.matmul(
                        ps_v[vb][:], xtv_sb[buf][:, ci, mi * 128:(mi + 1) * 128], wv_sb[:, ci, :],
                        start=(ci == 0), stop=False)),
                        waits=(w0 if ci == 0 else ()))
                E("tensor", (lambda e, vb=vb, g=g, mi=mi: e.matmul(
                    ps_v[vb][:], negmu_sb[0:1, g * 512 + mi * 128:g * 512 + (mi + 1) * 128],
                    cwv_sb[:], start=False, stop=True)), inc=semPE)
                pe_g[g] = semPE.v
                PE_V = semPE.v
                wv0 = [(semPE, PE_V)]
                if m >= 2:
                    wv0.append((sVST[m % 2], st_v[m - 2]))
                E("vector", (lambda e, vb=vb, m=m: e.tensor_scalar(
                    v_bf[vb][:], ps_v[vb][:], rcol_all[:, m:m + 1], None, op0=ALU.mult)),
                    waits=wv0, inc=semV)
                v_ev[m] = semV.v
                E("sync", (lambda e, m=m, vb=vb: e.dma_start(
                    out=v_d.ap()[m * 128:(m + 1) * 128, :], in_=v_bf[vb][:])),
                    waits=[(semV, v_ev[m])], inc=sVST[vb], inc_n=16)
                st_v[m] = sVST[vb].v
        em.barrier()

    with contextlib.ExitStack() as ph:
        Tp = lambda n, s, d: ph.enter_context(nc.sbuf_tensor(n, s, d))
        Pp = lambda n, s, d: ph.enter_context(nc.psum_tensor(n, s, d))
        qkt_sb = [Tp(f"qkt2b{i}", [128, S], BF16) for i in range(2)]
        sq2b = [Tp(f"sq2b{i}", [128, S], BF16) for i in range(2)]
        nrow = Tp("nrow", [1, S], F32)
        scal2 = Tp("scal2", [1, 8], F32)
        ps2b = Pp("ps2b", [1, 1024], F32)
        ps2c = Pp("ps2c", [128, 2], F32)

        for pi, tdram in ((0, qt_d), (1, kt_d)):
            for h in range(HPC):
                v_sq2 = []
                for dc in range(2):
                    ch = 2 * h + dc
                    wl = []
                    if pi + h + dc > 0:
                        wl.append((semV, semV.v))
                    E("sync", (lambda e, tdram=tdram, ch=ch, dc=dc: e.dma_start(
                        out=qkt_sb[dc][:], in_=tdram.ap()[128 * ch:128 * (ch + 1), :])),
                        waits=wl, inc=sQTL, inc_n=16)
                    LD2B = sQTL.v
                    E("vector", (lambda e, dc=dc: e.tensor_mul(
                        sq2b[dc][:], qkt_sb[dc][:], qkt_sb[dc][:])),
                        waits=[(sQTL, LD2B)], inc=semV)
                    v_sq2.append(semV.v)
                ev_done = 0
                for span in range(S // 512):
                    sl = (span % 2) * 512
                    wm = [(semV, v_sq2[1])]
                    if span >= 2:
                        wm.append((semV, ev_done))
                    E("tensor", (lambda e, span=span, sl=sl: e.matmul(
                        ps2b[0:1, sl:sl + 512], onescol_bf[:],
                        sq2b[0][:, span * 512:(span + 1) * 512], start=True, stop=False)),
                        waits=wm)
                    E("tensor", (lambda e, span=span, sl=sl: e.matmul(
                        ps2b[0:1, sl:sl + 512], onescol_bf[:],
                        sq2b[1][:, span * 512:(span + 1) * 512], start=False, stop=True)))
                    em.mark_last("tensor", semPE)
                    PE2B = semPE.v
                    E("vector", (lambda e, span=span, sl=sl: e.tensor_copy(
                        nrow[0:1, span * 512:(span + 1) * 512], ps2b[0:1, sl:sl + 512])),
                        waits=[(semPE, PE2B)], inc=semV)
                    ev_done = semV.v
                E("vector", (lambda e, pi=pi, h=h: e.reduce_max(
                    scal2[0:1, 2 * pi + h:2 * pi + h + 1], nrow[:], axis=AX.X)), inc=semV)
        V_NORMS = semV.v
        for h in range(HPC):
            E("scalar", (lambda e, h=h: e.activation(
                scal2[0:1, 4 + h:5 + h], scal2[0:1, h:h + 1], AF.Ln)),
                waits=[(semV, V_NORMS)], inc=semA)
            E("scalar", (lambda e, h=h: e.activation(
                scal2[0:1, 6 + h:7 + h], scal2[0:1, 2 + h:3 + h], AF.Ln)), inc=semA)
            A_LN2 = semA.v
            E("vector", (lambda e, h=h: e.tensor_add(
                scal2[0:1, 4 + h:5 + h], scal2[0:1, 4 + h:5 + h], scal2[0:1, 6 + h:7 + h])),
                waits=[(semA, A_LN2)], inc=semV)
            V_AD2 = semV.v
            E("scalar", (lambda e, h=h: e.activation(
                scal2[0:1, 4 + h:5 + h], scal2[0:1, 4 + h:5 + h], AF.Exp, scale=0.5)),
                waits=[(semV, V_AD2)], inc=semA)
            A_EX2 = semA.v
            E("vector", (lambda e, h=h: e.tensor_scalar(
                scal2[0:1, 4 + h:5 + h], scal2[0:1, 4 + h:5 + h], -1.0 / 16.0, None,
                op0=ALU.mult)), waits=[(semA, A_EX2)], inc=semV)
            V_NB = semV.v
            E("tensor", (lambda e, h=h: e.matmul(
                ps2c[:, h:h + 1], onesrow_f[:], scal2[0:1, 4 + h:5 + h],
                start=True, stop=True)), waits=[(semV, V_NB)])
            em.mark_last("tensor", semPE)
            PE2C = semPE.v
            E("vector", (lambda e, h=h: e.tensor_copy(negb_col[:, h:h + 1], ps2c[:, h:h + 1])),
              waits=[(semPE, PE2C)], inc=semV)
        E("vector", lambda e: e.tensor_tensor(
            negb_min[:], negb_col[:, 0:1], negb_col[:, 1:2], op=ALU.min), inc=semV)
        em.barrier()

    if upto <= 2:
        _finish(nc, em, store_sems)
        glob.close()
        return nc

    with contextlib.ExitStack() as ph:
        Tp = lambda n, s, d: ph.enter_context(nc.sbuf_tensor(n, s, d))
        Pp = lambda n, s, d: ph.enter_context(nc.psum_tensor(n, s, d))
        qt_sb = Tp("qt_sb", [128, 4, 512], BF16)
        kt_sb = [Tp(f"kt_s{i}", [128, 4, 512], BF16) for i in range(2)]
        v_sb = [Tp(f"v_s{i}", [128, 4, 512], BF16) for i in range(2)]
        pt_sb = [Tp(f"pt_s{i}", [128, 1024], BF16) for i in range(2)]
        mask_sb = Tp("mask_sb", [128, 896], BF16)
        oev_sb = Tp("oev_sb", [128, 2, 512], F32)
        lrow_pad = Tp("lrow_pad", [128, 512], F32)
        rl_sb = Tp("rl_sb", [128, 4], F32)
        o_sb = [Tp(f"o_sb{i}", [128, 4, 512], BF16) for i in range(2)]

        ps_s2 = Pp("ps_s2", [128, 1024], F32)
        ps_o = [Pp(f"ps_o{i}", [128, 512], F32) for i in range(4)]
        ps_lf = Pp("ps_lf", [1, 1024], F32)

        E("sync", lambda e: e.dma_start(out=mask_sb[:], in_=maskbin), inc=sCONST, inc_n=16)
        S_MASK = sCONST.v

        fin_done = 0
        st_o = {}
        for qc in range(16):
            nkc = 4 * (qc + 1)
            E("sync", (lambda e, qc=qc: e.dma_start(
                out=qt_sb[:], in_=_rearr_cpt(qt_d.ap())[:, :, qc * 512:(qc + 1) * 512])),
                waits=[(semV, fin_done), (sQST[0], sQST[0].v), (sQST[1], sQST[1].v)],
                inc=sQTL, inc_n=16)
            S_QT = sQTL.v

            ld_grp = {}
            kc_pe = {}
            kc_exp = {}
            for kc in range(nkc):
                gi = kc // 4
                gb = gi % 2
                kci = kc % 4
                ub = kc % 2
                if kci == 0:
                    w = [(semV, fin_done)]
                    if gi >= 2:
                        w.append((semPE, kc_pe[4 * (gi - 1) - 1]))
                    if qc == 0 and gi == 0:
                        w += [(sVST[0], sVST[0].v), (sVST[1], sVST[1].v)]
                    E("sync", (lambda e, gi=gi, gb=gb: e.dma_start(
                        out=kt_sb[gb][:],
                        in_=_rearr_cpt(kt_d.ap())[:, :, gi * 512:(gi + 1) * 512])),
                        waits=w, inc=sKV[gb], inc_n=16)
                    E("sync", (lambda e, gi=gi, gb=gb: e.dma_start(
                        out=v_sb[gb][:],
                        in_=v_d.ap()[gi * 512:(gi + 1) * 512, :].rearrange("(g p) d -> p g d", p=128))),
                        inc=sKV[gb], inc_n=16)
                    ld_grp[gi] = sKV[gb].v

                ws = [(sKV[gb], ld_grp[gi]), (sQTL, S_QT)]
                if kc >= 1:
                    ws.append((semA, kc_exp[kc - 1]))
                else:
                    ws.append((semV, fin_done))
                for h in range(HPC):
                    for dc in range(2):
                        E("tensor", (lambda e, gb=gb, h=h, dc=dc, kci=kci: e.matmul(
                            ps_s2[:, 512 * h:512 * h + 512],
                            kt_sb[gb][:, 2 * h + dc, kci * 128:(kci + 1) * 128],
                            qt_sb[:, 2 * h + dc, :], start=(dc == 0), stop=(dc == 1))),
                            waits=(ws if (h == 0 and dc == 0) else ()))
                em.mark_last("tensor", semPE)
                PE_S = semPE.v

                wexp = [(semPE, PE_S)]
                if kc >= 2:
                    wexp.append((semPE, kc_pe[kc - 2]))
                E("scalar", (lambda e, ub=ub: e.activation(
                    pt_sb[ub][:], ps_s2[:], AF.Exp, bias=negb_min[:, 0:1], scale=1.0 / 16.0)),
                    waits=wexp, inc=semA)
                kc_exp[kc] = semA.v
                pt_sem, pt_val = semA, semA.v

                kbase, qbase = kc * 128, qc * 512
                if kbase > qbase - 128:
                    off = 384 - (kbase - qbase)
                    E("vector", (lambda e, ub=ub, off=off: e.tensor_mul(
                        pt_sb[ub][:, 0:512], pt_sb[ub][:, 0:512], mask_sb[:, off:off + 512])),
                        waits=[(semA, pt_val), (sCONST, S_MASK)], inc=semV)
                    E("vector", (lambda e, ub=ub, off=off: e.tensor_mul(
                        pt_sb[ub][:, 512:1024], pt_sb[ub][:, 512:1024], mask_sb[:, off:off + 512])),
                        inc=semV)
                    pt_sem, pt_val = semV, semV.v

                for h in range(HPC):
                    E("tensor", (lambda e, gb=gb, h=h, kci=kci, ub=ub, kc=kc, nkc=nkc: e.matmul(
                        ps_o[2 * h][:], v_sb[gb][:, kci, h * HD:h * HD + 128],
                        pt_sb[ub][:, 512 * h:512 * h + 512],
                        start=(kc == 0), stop=(kc == nkc - 1))),
                        waits=([(pt_sem, pt_val)] if h == 0 else ()))
                    E("tensor", (lambda e, gb=gb, h=h, kci=kci, ub=ub, kc=kc, nkc=nkc: e.matmul(
                        ps_o[2 * h + 1][:], v_sb[gb][:, kci, h * HD + 128:h * HD + 256],
                        pt_sb[ub][:, 512 * h:512 * h + 512],
                        start=(kc == 0), stop=(kc == nkc - 1))))
                    E("tensor", (lambda e, h=h, ub=ub, kc=kc, nkc=nkc: e.matmul(
                        ps_lf[0:1, 512 * h:512 * h + 512], onescol_bf[:],
                        pt_sb[ub][:, 512 * h:512 * h + 512],
                        start=(kc == 0), stop=(kc == nkc - 1))))
                em.mark_last("tensor", semPE)
                kc_pe[kc] = semPE.v

            PE_UNITS = semPE.v
            A_LASTEXP = semA.v
            V_LASTMSK = semV.v

            for h in range(HPC):
                E("vector", (lambda e, h=h: e.tensor_copy(
                    lrow_pad[0:1, :], ps_lf[0:1, 512 * h:512 * h + 512])),
                    waits=[(semPE, PE_UNITS)], inc=semV)
                V_LR = semV.v
                for qs in range(4):
                    E("tensor", (lambda e, qs=qs: e.transpose(
                        ps_s2[:, qs * 128:(qs + 1) * 128],
                        lrow_pad[:, qs * 128:(qs + 1) * 128], identf_sb[:])),
                        waits=([(semV, V_LR), (semA, A_LASTEXP)] if qs == 0 else ()))
                em.mark_last("tensor", semPE)
                PE_LT = semPE.v
                E("vector", (lambda e: e.reciprocal(rl_sb[:], ps_s2[:, 0:512:128])),
                  waits=[(semPE, PE_LT)], inc=semV)
                V_RL = semV.v
                E("vector", (lambda e, h=h: e.tensor_copy(oev_sb[:, 0, :], ps_o[2 * h][:])), inc=semV)
                E("vector", (lambda e, h=h: e.tensor_copy(oev_sb[:, 1, :], ps_o[2 * h + 1][:])), inc=semV)
                V_OE = semV.v
                for pair in range(2):
                    bank = ps_s2[:, 512:1024] if pair == 0 else ps_s2[:, 0:512]
                    wtp = [(semV, V_OE)]
                    if pair == 0:
                        wtp.append((semV, V_RL))
                    else:
                        wtp.append((semV, semV.v))
                    for qi in range(2):
                        qs = pair * 2 + qi
                        for dc in range(2):
                            E("tensor", (lambda e, bank=bank, qi=qi, qs=qs, dc=dc: e.transpose(
                                bank[:, qi * 256 + dc * 128:qi * 256 + (dc + 1) * 128],
                                oev_sb[:, dc, qs * 128:(qs + 1) * 128], identf_sb[:])),
                                waits=(wtp if (qi == 0 and dc == 0) else ()))
                    em.mark_last("tensor", semPE)
                    PE_OT = semPE.v
                    for qi in range(2):
                        qs = pair * 2 + qi
                        wsc = [(semPE, PE_OT)] if qi == 0 else []
                        if h == 0 and pair == 0 and qi == 0 and qc >= 2:
                            wsc.append((sOST[qc % 2], st_o[qc - 2]))
                        E("vector", (lambda e, bank=bank, qi=qi, qs=qs, h=h, qc=qc: e.tensor_scalar(
                            o_sb[qc % 2][:, qs, h * HD:(h + 1) * HD], bank[:, qi * 256:(qi + 1) * 256],
                            rl_sb[:, qs:qs + 1], None, op0=ALU.mult)),
                            waits=wsc, inc=semV)
            fin_done = semV.v
            E("sync", (lambda e, qc=qc: e.dma_start(
                out=o_out[qc * 512:(qc + 1) * 512, :].rearrange("(q p) d -> p q d", p=128),
                in_=o_sb[qc % 2][:])), waits=[(semV, fin_done)], inc=sOST[qc % 2], inc_n=16)
            st_o[qc] = sOST[qc % 2].v
        em.barrier()

    if upto <= 3:
        _finish(nc, em, store_sems)
        glob.close()
        return nc

    xnt_sb = Tg("xnt_sb", [128, NCH, 1024], BF16)
    with contextlib.ExitStack() as ph:
        Tp = lambda n, s, d: ph.enter_context(nc.sbuf_tensor(n, s, d))
        Pp = lambda n, s, d: ph.enter_context(nc.psum_tensor(n, s, d))
        xtile = [Tp(f"xtile{i}", [128, D], F32) for i in range(2)]
        xsq = Tp("xsq", [128, D], F32)
        stat = Tp("stat", [128, 6], F32)
        xn_bf = Tp("xn_bf", [128, D], BF16)
        ps4 = Pp("ps4", [128, 1024], BF16)

        v_done = {}
        pe_tr = {}
        for m in range(8):
            buf = m % 2
            lw = [(semV, v_done[m - 2])] if m >= 2 else []
            E("sync", (lambda e, m=m, buf=buf: e.dma_start(
                out=xtile[buf][:], in_=x_c[m * 128:(m + 1) * 128, :])),
                waits=lw, inc=sXC[buf], inc_n=16)
            LD_X = sXC[buf].v
            E("vector", (lambda e, buf=buf: e.reduce_sum(stat[:, 0:1], xtile[buf][:], axis=AX.X)),
              waits=[(sXC[buf], LD_X)], inc=semV)
            E("vector", (lambda e, buf=buf: e.tensor_mul(xsq[:], xtile[buf][:], xtile[buf][:])), inc=semV)
            E("vector", lambda e: e.reduce_sum(stat[:, 1:2], xsq[:], axis=AX.X), inc=semV)
            E("vector", lambda e: e.tensor_scalar(stat[:, 2:3], stat[:, 0:1], 1.0 / D, None, op0=ALU.mult), inc=semV)
            E("vector", lambda e: e.tensor_scalar(stat[:, 3:4], stat[:, 1:2], 1.0 / D, None, op0=ALU.mult), inc=semV)
            E("vector", lambda e: e.tensor_mul(stat[:, 4:5], stat[:, 2:3], stat[:, 2:3]), inc=semV)
            E("vector", lambda e: e.tensor_sub(stat[:, 3:4], stat[:, 3:4], stat[:, 4:5]), inc=semV)
            V_ST = semV.v
            E("scalar", lambda e: e.activation(stat[:, 3:4], stat[:, 3:4], AF.Ln, bias=epscol_sb[:, 0:1]),
              waits=[(semV, V_ST)], inc=semA)
            E("scalar", lambda e: e.activation(stat[:, 5:6], stat[:, 3:4], AF.Exp, scale=-0.5), inc=semA)
            A_ST = semA.v
            E("vector", (lambda e, buf=buf: e.tensor_scalar(
                xn_bf[:], xtile[buf][:], stat[:, 2:3], stat[:, 5:6],
                op0=ALU.subtract, op1=ALU.mult)),
                waits=[(semA, A_ST)], inc=semV)
            v_done[m] = semV.v
            V_XN = semV.v

            for rnd in range(4):
                wt = [(semV, V_XN)]
                if (m, rnd - 1) in pe_tr:
                    wt.append((semV, pe_tr[(m, rnd - 1)]))
                elif m >= 1 and rnd == 0:
                    wt.append((semV, pe_tr[(m - 1, 3)]))
                for k in range(8):
                    c2 = rnd * 8 + k
                    j = c2 // 4
                    r = c2 % 4
                    E("tensor", (lambda e, j=j, r=r, k=k: e.transpose(
                        ps4[:, k * 128:(k + 1) * 128],
                        xn_bf[:, j + 8 * (r * 128): j + 8 * (r * 128 + 127) + 1: 8],
                        ident_sb[:])),
                        waits=(wt if k == 0 else ()))
                em.mark_last("tensor", semPE)
                PE_T4 = semPE.v
                E("vector", (lambda e, rnd=rnd, m=m: e.tensor_copy(
                    xnt_sb[:, rnd * 8:(rnd + 1) * 8, m * 128:(m + 1) * 128],
                    ps4.rearrange("p (c n) -> p c n", c=8))),
                    waits=[(semPE, PE_T4)], inc=semV)
                pe_tr[(m, rnd)] = semV.v
        em.barrier()

    if upto <= 4:
        _finish(nc, em, store_sems)
        glob.close()
        return nc

    bgate_sb = Tg("bgate_sb", [NG, INTER], BF16)
    bup_sb = Tg("bup_sb", [NG, INTER], BF16)
    bdown_sb = Tg("bdown_sb", [NGD, D], BF16)
    xg_sb = Tg("xg_sb", [NG, 1024], BF16)

    with contextlib.ExitStack() as ph:
        Tp = lambda n, s, d: ph.enter_context(nc.sbuf_tensor(n, s, d))
        Pp = lambda n, s, d: ph.enter_context(nc.psum_tensor(n, s, d))
        qz_sb = Tp("qz_sb", [NGD, 688], I32)
        zi_sb = Tp("zi_sb", [NGD, 5504], I32)
        zf_sb = Tp("zf_sb", [NGD, 5504], F32)
        sc5_sb = Tp("sc5_sb", [NGD, 5504], F32)
        egate_sb = Tp("egate_sb", [128, NCH, NG], BF16)
        ps_xg = Pp("ps_xg", [NG, 1024], F32)

        jobs = []
        for sp in range(2):
            jobs.append((g_qz, g_sc, bgate_sb, NG, 1376, INTER, sp, 1.0))
            jobs.append((u_qz, u_sc, bup_sb, NG, 1376, INTER, sp, 1.0))
        jobs.append((d_qz, d_sc, bdown_sb, NGD, 512, D, 0, -1.0))

        v_prev = 0
        for (qz, sc, bsb, ngr, n8, ntot, sp, sgn) in jobs:
            half8 = n8 // 2 if ntot == INTER else n8
            half = half8 * 8
            E("sync", (lambda e, qz=qz, ngr=ngr, sp=sp, half8=half8: e.dma_start(
                out=qz_sb[0:ngr, 0:half8], in_=qz[:, sp * half8:(sp + 1) * half8])),
                waits=([(semV, v_prev)] if v_prev else []), inc=sZL, inc_n=16)
            E("sync", (lambda e, sc=sc, ngr=ngr, sp=sp, half=half: e.dma_start(
                out=sc5_sb[0:ngr, 0:half], in_=sc[:, sp * half:(sp + 1) * half])),
                inc=sZL, inc_n=16)
            LD_Z = sZL.v
            for jn in range(8):
                E("vector", (lambda e, jn=jn, ngr=ngr, half8=half8: e.tensor_scalar(
                    zi_sb[0:ngr, jn:half8 * 8:8], qz_sb[0:ngr, 0:half8],
                    int(4 * jn), 15, op0=ALU.logical_shift_right, op1=ALU.bitwise_and)),
                    waits=([(sZL, LD_Z)] if jn == 0 else ()), inc=semV)
            E("vector", (lambda e, ngr=ngr, half=half: e.tensor_scalar(
                zf_sb[0:ngr, 0:half], zi_sb[0:ngr, 0:half], 1.0, None, op0=ALU.add)),
                inc=semV)
            if sgn != 1.0:
                E("vector", (lambda e, ngr=ngr, half=half, sgn=sgn: e.tensor_scalar(
                    sc5_sb[0:ngr, 0:half], sc5_sb[0:ngr, 0:half], sgn, None, op0=ALU.mult)),
                    inc=semV)
            E("vector", (lambda e, bsb=bsb, ngr=ngr, sp=sp, half=half: e.tensor_mul(
                bsb[0:ngr, sp * half:(sp + 1) * half], zf_sb[0:ngr, 0:half],
                sc5_sb[0:ngr, 0:half])), inc=semV)
            v_prev = semV.v
        em.barrier()

        E("sync", lambda e: e.dma_start(
            out=egate_sb[:], in_=e_gate.rearrange("(c p) g -> p c g", p=128)), inc=sEG, inc_n=16)
        LD_EG = sEG.v
        for half in range(2):
            for c2 in range(NCH):
                E("tensor", (lambda e, half=half, c2=c2: e.matmul(
                    ps_xg[:, half * 512:(half + 1) * 512], egate_sb[:, c2, :],
                    xnt_sb[:, c2, half * 512:(half + 1) * 512],
                    start=(c2 == 0), stop=(c2 == NCH - 1))),
                    waits=([(sEG, LD_EG)] if (half == 0 and c2 == 0) else ()))
        em.mark_last("tensor", semPE)
        PE_XG = semPE.v
        E("vector", lambda e: e.tensor_copy(xg_sb[:], ps_xg[:]), waits=[(semPE, PE_XG)], inc=semV)
        em.barrier()

    if upto <= 6:
        _finish(nc, em, store_sems)
        glob.close()
        return nc

    with contextlib.ExitStack() as ph:
        Tp = lambda n, s, d: ph.enter_context(nc.sbuf_tensor(n, s, d))
        Pp = lambda n, s, d: ph.enter_context(nc.psum_tensor(n, s, d))
        qwg_sb = [Tp(f"qwg{i}", [128, 4, 128], I32) for i in range(2)]
        qwu_sb = [Tp(f"qwu{i}", [128, 4, 128], I32) for i in range(2)]
        seg_sb = [Tp(f"seg{i}", [128, 4, 128], BF16) for i in range(2)]
        seu_sb = [Tp(f"seu{i}", [128, 4, 128], BF16) for i in range(2)]
        wg_all = [Tp(f"wg_all{i}", [128, NCH, 128], BF16) for i in range(2)]
        wu_all = [Tp(f"wu_all{i}", [128, NCH, 128], BF16) for i in range(2)]
        w4g = Tp("w4g", [128, 4, 128], I32)
        w4u = Tp("w4u", [128, 4, 128], I32)
        silu_t = Tp("silu_t", [128, 512], F32)
        htw = [Tp(f"htw{i}", [128, 1024], BF16) for i in range(2)]
        ps7 = [Pp(f"ps7_{i}", [128, 512], F32) for i in range(8)]

        NSP = INTER // 128
        pe_mm = {}
        v_deq = {}
        v_ht = {}
        st_ht = {}
        a_si = {}

        def emit_loads_deq(sp):
            buf = sp % 2
            base = sp * 128
            lw = [(semV, v_deq[sp - 2])] if sp >= 2 else []
            E("sync", (lambda e, buf=buf, base=base: e.dma_start(
                out=qwg_sb[buf][:], in_=g_qw.rearrange("(r p) n -> p r n", p=128)[:, :, base:base + 128])),
                waits=lw, inc=sQW[buf], inc_n=16)
            E("sync", (lambda e, buf=buf, base=base: e.dma_start(
                out=qwu_sb[buf][:], in_=u_qw.rearrange("(r p) n -> p r n", p=128)[:, :, base:base + 128])),
                inc=sQW[buf], inc_n=16)
            LD_QW = sQW[buf].v
            for r in range(4):
                E("gpsimd", (lambda e, buf=buf, base=base, r=r: e.dma_start(
                    out=seg_sb[buf][:, r, :],
                    in_=g_sc[8 * r:8 * r + 8, base:base + 128].unsqueeze(1).broadcast_to([8, 16, 128]))),
                    waits=(lw if r == 0 else ()), inc=sSE[buf], inc_n=16)
                E("gpsimd", (lambda e, buf=buf, base=base, r=r: e.dma_start(
                    out=seu_sb[buf][:, r, :],
                    in_=u_sc[8 * r:8 * r + 8, base:base + 128].unsqueeze(1).broadcast_to([8, 16, 128]))),
                    inc=sSE[buf], inc_n=16)
            LD_SE = sSE[buf].v
            for j in range(8):
                wdq = [(sQW[buf], LD_QW), (sSE[buf], LD_SE)]
                if sp >= 2 and j == 0:
                    wdq.append((semPE, pe_mm[sp - 2]))
                E("vector", (lambda e, buf=buf, j=j: e.tensor_scalar(
                    w4g[:], qwg_sb[buf][:], int(4 * j), 15,
                    op0=ALU.logical_shift_right, op1=ALU.bitwise_and)),
                    waits=wdq, inc=semV)
                E("vector", (lambda e, buf=buf, j=j: e.tensor_mul(
                    wg_all[buf][:, 4 * j:4 * j + 4, :], w4g[:], seg_sb[buf][:])), inc=semV)
                E("vector", (lambda e, buf=buf, j=j: e.tensor_scalar(
                    w4u[:], qwu_sb[buf][:], int(4 * j), 15,
                    op0=ALU.logical_shift_right, op1=ALU.bitwise_and)), inc=semV)
                E("vector", (lambda e, buf=buf, j=j: e.tensor_mul(
                    wu_all[buf][:, 4 * j:4 * j + 4, :], w4u[:], seu_sb[buf][:])), inc=semV)
            v_deq[sp] = semV.v

        def emit_mms(sp):
            buf = sp % 2
            base = sp * 128
            st_ = (sp % 2) * 4
            wmm0 = [(semV, v_deq[sp])]
            if sp >= 2:
                wmm0.append((semV, v_ht[sp - 2]))
            for c2 in range(NCH):
                for half in range(2):
                    E("tensor", (lambda e, buf=buf, half=half, c2=c2, st_=st_: e.matmul(
                        ps7[st_ + half][:], wg_all[buf][:, c2, :],
                        xnt_sb[:, c2, half * 512:(half + 1) * 512],
                        start=(c2 == 0), stop=False)),
                        waits=(wmm0 if (c2 == 0 and half == 0) else ()))
                    E("tensor", (lambda e, buf=buf, half=half, c2=c2, st_=st_: e.matmul(
                        ps7[st_ + 2 + half][:], wu_all[buf][:, c2, :],
                        xnt_sb[:, c2, half * 512:(half + 1) * 512],
                        start=(c2 == 0), stop=False)))
            for half in range(2):
                E("tensor", (lambda e, half=half, base=base, st_=st_: e.matmul(
                    ps7[st_ + half][:], bgate_sb[:, base:base + 128],
                    xg_sb[:, half * 512:(half + 1) * 512], start=False, stop=True)))
                E("tensor", (lambda e, half=half, base=base, st_=st_: e.matmul(
                    ps7[st_ + 2 + half][:], bup_sb[:, base:base + 128],
                    xg_sb[:, half * 512:(half + 1) * 512], start=False, stop=True)))
            em.mark_last("tensor", semPE)
            pe_mm[sp] = semPE.v

        def emit_evac(sp):
            buf = sp % 2
            base = sp * 128
            st_ = (sp % 2) * 4
            for half in range(2):
                wsi = [(semPE, pe_mm[sp]), (semV, semV.v)]
                E("scalar", (lambda e, half=half, st_=st_: e.activation(
                    silu_t[:], ps7[st_ + half][:], AF.Silu)), waits=wsi, inc=semA)
                A_SI = semA.v
                wht = [(semA, A_SI)]
                if sp >= 2 and half == 0:
                    wht.append((sHST[sp % 2], st_ht[sp - 2]))
                E("vector", (lambda e, buf=buf, half=half, st_=st_: e.tensor_mul(
                    htw[buf][:, half * 512:(half + 1) * 512], silu_t[:],
                    ps7[st_ + 2 + half][:])), waits=wht, inc=semV)
            v_ht[sp] = semV.v
            E("sync", (lambda e, buf=buf, base=base: e.dma_start(
                out=ht_d.ap()[base:base + 128, :], in_=htw[buf][:])),
                waits=[(semV, v_ht[sp])], inc=sHST[buf], inc_n=16)
            st_ht[sp] = sHST[buf].v

        emit_loads_deq(0)
        for sp in range(NSP):
            if sp + 1 < NSP:
                emit_loads_deq(sp + 1)
            emit_mms(sp)
            emit_evac(sp)
        em.barrier()


    if upto <= 7:
        _finish(nc, em, store_sems)
        glob.close()
        return nc

    xgd_sb = Tg("xgd_sb", [NGD, 1024], BF16)
    with contextlib.ExitStack() as ph:
        Tp = lambda n, s, d: ph.enter_context(nc.sbuf_tensor(n, s, d))
        Pp = lambda n, s, d: ph.enter_context(nc.psum_tensor(n, s, d))
        edown_sb = Tp("edown_sb", [128, NGD, NGD], BF16)
        htc_sb = [Tp(f"htc{i}", [128, 1024], BF16) for i in range(2)]
        ps_xgd = Pp("ps_xgd", [NGD, 1024], F32)

        E("sync", lambda e: e.dma_start(
            out=edown_sb[:], in_=e_down.rearrange("(c p) g -> p c g", p=128)), inc=sED, inc_n=16)
        LD_ED = sED.v
        pe_c = {}
        for c2 in range(NGD):
            buf = c2 % 2
            segs = _down_segments(c2)
            w = [(sED, LD_ED)]
            if c2 >= 2:
                w.append((semPE, pe_c[c2 - 2]))
            else:
                w += [(sHST[0], sHST[0].v), (sHST[1], sHST[1].v)]
            for si, (p0, cnt, j, k8) in enumerate(segs):
                E("sync", (lambda e, buf=buf, p0=p0, cnt=cnt, j=j, k8=k8: e.dma_start(
                    out=htc_sb[buf][p0:p0 + cnt, :],
                    in_=ht_d.ap()[8 * k8 + j: 8 * (k8 + cnt - 1) + j + 1: 8, :])),
                    waits=(w if si == 0 else ()), inc=sHC[buf], inc_n=16)
            LD_HT = sHC[buf].v
            for half in range(2):
                E("tensor", (lambda e, buf=buf, c2=c2, half=half: e.matmul(
                    ps_xgd[:, half * 512:(half + 1) * 512], edown_sb[:, c2, :],
                    htc_sb[buf][:, half * 512:(half + 1) * 512],
                    start=(c2 == 0), stop=(c2 == NGD - 1))),
                    waits=([(sHC[buf], LD_HT)] if half == 0 else ()))
            em.mark_last("tensor", semPE)
            pe_c[c2] = semPE.v
        PE_XGD = semPE.v
        E("vector", lambda e: e.tensor_copy(xgd_sb[:], ps_xgd[:]), waits=[(semPE, PE_XGD)], inc=semV)
        em.barrier()

    if upto <= 8:
        _finish(nc, em, store_sems)
        glob.close()
        return nc

    with contextlib.ExitStack() as ph:
        Tp = lambda n, s, d: ph.enter_context(nc.sbuf_tensor(n, s, d))
        Pp = lambda n, s, d: ph.enter_context(nc.psum_tensor(n, s, d))
        qwd4 = [Tp(f"qwd4_{i}", [128, 4, 512], I32) for i in range(2)]
        sed4 = [Tp(f"sed4_{i}", [128, 4, 512], BF16) for i in range(2)]
        htc4 = [Tp(f"htc4_{i}", [128, 4, 1024], BF16) for i in range(2)]
        w4d4 = [Tp(f"w4d4_{i}", [128, 4, 512], I32) for i in range(2)]
        wd4 = [Tp(f"wd4_{i}", [128, 4, 512], BF16) for i in range(2)]
        gtile = [Tp(f"gtile{i}", [128, 512], F32) for i in range(2)]
        ps9 = [Pp(f"ps9_{i}", [128, 512], F32) for i in range(8)]

        NCG = (NGD + 3) // 4
        groups = []
        for ns in range(8):
            for cg in range(NCG):
                c2s = list(range(4 * cg, min(4 * cg + 4, NGD)))
                groups.append((ns, cg, c2s))

        vdeq9 = {}
        pe9 = {}
        a_g = {}
        st_g = {}

        def emit_loads_deq9(gidx):
            ns, cg, c2s = groups[gidx]
            nb = ns * 512
            buf = gidx % 2
            wld = []
            if gidx >= 2:
                wld.append((semV, vdeq9[gidx - 2]))
                wld.append((semPE, pe9[gidx - 2]))
            else:
                wld += [(sHST[0], sHST[0].v), (sHST[1], sHST[1].v),
                        (sHC[0], sHC[0].v), (sHC[1], sHC[1].v)]
            first = True
            for i, c2 in enumerate(c2s):
                for (p0, cnt, j, k8) in _down_segments(c2):
                    E("sync", (lambda e, buf=buf, i=i, p0=p0, cnt=cnt, k8=k8, nb=nb: e.dma_start(
                        out=qwd4[buf][p0:p0 + cnt, i, :],
                        in_=d_qw[k8:k8 + cnt, nb:nb + 512])),
                        waits=(wld if first else ()), inc=sQD[buf], inc_n=16)
                    first = False
                    E("sync", (lambda e, buf=buf, i=i, p0=p0, cnt=cnt, j=j, k8=k8: e.dma_start(
                        out=htc4[buf][p0:p0 + cnt, i, :],
                        in_=ht_d.ap()[8 * k8 + j: 8 * (k8 + cnt - 1) + j + 1: 8, :])),
                        inc=sQD[buf], inc_n=16)
            LD_C = sQD[buf].v
            firstg = True
            for i, c2 in enumerate(c2s):
                for (p0, cnt, j, k8) in _down_segments(c2):
                    g0 = (8 * k8 + j) // 128
                    ngr = cnt // 16
                    E("gpsimd", (lambda e, buf=buf, i=i, p0=p0, g0=g0, ngr=ngr, nb=nb: e.dma_start(
                        out=sed4[buf][p0:p0 + 16 * ngr, i, :],
                        in_=d_sc[g0:g0 + ngr, nb:nb + 512].unsqueeze(1).broadcast_to([ngr, 16, 512]))),
                        waits=(wld if firstg else ()), inc=sSD[buf], inc_n=16)
                    firstg = False
            LD_SED = sSD[buf].v

            wdq = [(sQD[buf], LD_C), (sSD[buf], LD_SED)]
            if gidx >= 2:
                wdq.append((semPE, pe9[gidx - 2]))
            js = [set(seg[2] for seg in _down_segments(c2)) for c2 in c2s]
            if all(len(s) == 1 for s in js) and len(set.union(*js)) == 1:
                j = js[0].pop()
                gn = len(c2s)
                E("vector", (lambda e, buf=buf, j=j, gn=gn: e.tensor_scalar(
                    w4d4[buf][:, 0:gn, :], qwd4[buf][:, 0:gn, :], int(4 * j), 15,
                    op0=ALU.logical_shift_right, op1=ALU.bitwise_and)),
                    waits=wdq, inc=semV)
            else:
                firstd = True
                for i, c2 in enumerate(c2s):
                    segs = _down_segments(c2)
                    if len(segs) == 1:
                        j = segs[0][2]
                        E("vector", (lambda e, buf=buf, i=i, j=j: e.tensor_scalar(
                            w4d4[buf][:, i, :], qwd4[buf][:, i, :], int(4 * j), 15,
                            op0=ALU.logical_shift_right, op1=ALU.bitwise_and)),
                            waits=(wdq if firstd else ()), inc=semV)
                    else:
                        pstar, _, j2, _ = segs[1]
                        j1 = segs[0][2]
                        E("vector", (lambda e, buf=buf, i=i, j2=j2: e.tensor_scalar(
                            w4d4[buf][64:128, i, :], qwd4[buf][64:128, i, :], int(4 * j2), 15,
                            op0=ALU.logical_shift_right, op1=ALU.bitwise_and)),
                            waits=(wdq if firstd else ()), inc=semV)
                        if pstar == 32:
                            E("vector", (lambda e, buf=buf, i=i, j2=j2: e.tensor_scalar(
                                w4d4[buf][32:64, i, :], qwd4[buf][32:64, i, :], int(4 * j2), 15,
                                op0=ALU.logical_shift_right, op1=ALU.bitwise_and)),
                                inc=semV)
                        E("vector", (lambda e, buf=buf, i=i, pstar=pstar, j1=j1: e.tensor_scalar(
                            w4d4[buf][0:pstar, i, :], qwd4[buf][0:pstar, i, :], int(4 * j1), 15,
                            op0=ALU.logical_shift_right, op1=ALU.bitwise_and)),
                            inc=semV)
                    firstd = False
            gn = len(c2s)
            E("vector", (lambda e, buf=buf, gn=gn: e.tensor_mul(
                wd4[buf][:, 0:gn, :], w4d4[buf][:, 0:gn, :], sed4[buf][:, 0:gn, :])),
                inc=semV)
            vdeq9[gidx] = semV.v

        def emit_mms9(gidx):
            ns, cg, c2s = groups[gidx]
            buf = gidx % 2
            for i, c2 in enumerate(c2s):
                for mt in range(8):
                    wmm = []
                    if i == 0 and mt == 0:
                        wmm.append((semV, vdeq9[gidx]))
                        if cg == 0 and ns >= 1:
                            wmm.append((semA, a_g[ns - 1]))
                    E("tensor", (lambda e, buf=buf, i=i, mt=mt, c2=c2: e.matmul(
                        ps9[mt][:], htc4[buf][:, i, mt * 128:(mt + 1) * 128], wd4[buf][:, i, :],
                        start=(c2 == 0), stop=False)), waits=wmm)
            em.mark_last("tensor", semPE)
            pe9[gidx] = semPE.v

        def emit_ns_tail(ns):
            nb = ns * 512
            for mt in range(8):
                E("tensor", (lambda e, mt=mt, nb=nb: e.matmul(
                    ps9[mt][:], xgd_sb[:, mt * 128:(mt + 1) * 128],
                    bdown_sb[:, nb:nb + 512], start=False, stop=True)))
            em.mark_last("tensor", semPE)
            PE_NS = semPE.v
            for mt in range(8):
                gb = mt % 2
                wsl = [(semPE, PE_NS)]
                if mt >= 2:
                    wsl.append((sGST[mt % 2], st_g[(ns, mt - 2)]))
                elif ns >= 1:
                    wsl.append((sGST[mt % 2], st_g[(ns - 1, 6 + mt)]))
                E("scalar", (lambda e, mt=mt, gb=gb: e.activation(
                    gtile[gb][:], ps9[mt][:], AF.Silu)), waits=wsl, inc=semA)
                a_g[ns] = semA.v
                A_G = semA.v
                E("sync", (lambda e, mt=mt, gb=gb, nb=nb: e.dma_start(
                    out=g_out[mt * 128:(mt + 1) * 128, nb:nb + 512], in_=gtile[gb][:])),
                    waits=[(semA, A_G)], inc=sGST[gb], inc_n=16)
                st_g[(ns, mt)] = sGST[gb].v

        emit_loads_deq9(0)
        for gidx in range(len(groups)):
            if gidx + 1 < len(groups):
                emit_loads_deq9(gidx + 1)
            emit_mms9(gidx)
            if groups[gidx][1] == NCG - 1:
                emit_ns_tail(groups[gidx][0])
        em.barrier()

    _finish(nc, em, store_sems)
    glob.close()
    return nc


def _finish(nc, em, store_sems):
    if not isinstance(store_sems, (list, tuple)):
        store_sems = [store_sems]
    with nc.Block() as block:
        @block.sync
        def _(sync):
            em.replay("sync", sync)
            for s in store_sems:
                if s.v > 0:
                    sync.wait_ge(s.h, s.v)

        @block.gpsimd
        def _(gpsimd):
            em.replay("gpsimd", gpsimd)

        @block.vector
        def _(vector):
            em.replay("vector", vector)

        @block.scalar
        def _(scalar):
            em.replay("scalar", scalar)

        @block.tensor
        def _(tensor):
            em.replay("tensor", tensor)



def build_launch_b():
    nc = bass.Bass("TRN2", target_bir_lowering=False, debug=False)
    o_in = nc.dram_tensor("o_in", [TOK, D], BF16, kind="ExternalInput").ap()
    g_in = nc.dram_tensor("g_in", [TOK, D], F32, kind="ExternalInput").ap()
    x_in = nc.dram_tensor("x_in", [TOK, D], F32, kind="ExternalInput").ap()
    wo = nc.dram_tensor("wo", [D, D], F32, kind="ExternalInput").ap()
    ident = nc.dram_tensor("ident", [128, 128], BF16, kind="ExternalInput").ap()
    out_c = nc.dram_tensor("out_c", [TOK, D], F32, kind="ExternalOutput").ap()

    st = contextlib.ExitStack()
    T = lambda n, s, d: st.enter_context(nc.sbuf_tensor(n, s, d))
    P = lambda n, s, d: st.enter_context(nc.psum_tensor(n, s, d))

    ident_sb = T("ident_sb", [128, 128], BF16)
    ot_sb = T("ot_sb", [128, NCH, 1024], BF16)
    otile = [T(f"otile{i}", [128, D], BF16) for i in range(2)]
    wo_sb = [T(f"wo_sb{i}", [128, 8, 512], BF16) for i in range(2)]
    gt_sb = [T(f"gt{i}", [128, 512], F32) for i in range(2)]
    xt_sb = [T(f"xtb{i}", [128, 512], F32) for i in range(2)]
    res_sb = [T(f"res{i}", [128, 512], F32) for i in range(2)]
    tmp_sb = T("tmp_sb", [128, 512], F32)

    _allsems = []
    def mksem(n):
        s = SemCnt(st.enter_context(nc.semaphore(n)))
        _allsems.append(s)
        return s
    semPEb_ = None
    semPE = mksem("semPEb")
    semV = mksem("semVb")
    semA = mksem("semAb")
    em = Emitter(own_sems={"vector": semV, "scalar": semA})
    em.all_sems = _allsems
    sIDB = mksem("sIDB")
    sOB = [mksem("sOB0"), mksem("sOB1")]
    sWOB = [mksem("sWOB0"), mksem("sWOB1")]
    sGXB = [mksem("sGXB0"), mksem("sGXB1")]
    sOUTB = [mksem("sOUTB0"), mksem("sOUTB1")]
    E = em.emit

    E("sync", lambda e: e.dma_start(out=ident_sb[:], in_=ident), inc=sIDB, inc_n=16)
    S_ID = sIDB.v

    with nc.psum_tensor("psb_t", [128, 1024], BF16) as ps_t:
        v_tr = {}
        for m in range(8):
            buf = m % 2
            lw = [(sIDB, S_ID)]
            if m >= 2:
                lw.append((semV, v_tr[(m - 2, 3)]))
            E("sync", (lambda e, m=m, buf=buf: e.dma_start(
                out=otile[buf][:], in_=o_in[m * 128:(m + 1) * 128, :])),
                waits=lw, inc=sOB[buf], inc_n=16)
            LD_O = sOB[buf].v
            for rnd in range(4):
                wt = [(sOB[buf], LD_O)]
                if m == 0 and rnd == 0:
                    wt.append((sIDB, S_ID))
                if (m, rnd - 1) in v_tr:
                    wt.append((semV, v_tr[(m, rnd - 1)]))
                elif m >= 1 and rnd == 0:
                    wt.append((semV, v_tr[(m - 1, 3)]))
                for k in range(8):
                    ci = rnd * 8 + k
                    E("tensor", (lambda e, buf=buf, ci=ci, k=k: e.transpose(
                        ps_t[:, k * 128:(k + 1) * 128],
                        otile[buf][:, ci * 128:(ci + 1) * 128], ident_sb[:])),
                        waits=(wt if k == 0 else ()))
                em.mark_last("tensor", semPE)
                PE_T = semPE.v
                E("vector", (lambda e, rnd=rnd, m=m: e.tensor_copy(
                    ot_sb[:, rnd * 8:(rnd + 1) * 8, m * 128:(m + 1) * 128],
                    ps_t.rearrange("p (c n) -> p c n", c=8))),
                    waits=[(semPE, PE_T)], inc=semV)
                v_tr[(m, rnd)] = semV.v
        em.barrier()

    psb = [P(f"psb{i}", [128, 512], F32) for i in range(8)]
    v_done = {}
    st_out = {}
    pe_grp = {}
    for ns in range(8):
        nb = ns * 512
        for cg in range(4):
            buf = cg % 2
            lw = []
            if (ns, cg - 2) in pe_grp:
                lw.append((semPE, pe_grp[(ns, cg - 2)]))
            elif ns >= 1 and cg < 2:
                lw.append((semPE, pe_grp[(ns - 1, 2 + cg)]))
            E("gpsimd", (lambda e, buf=buf, cg=cg, nb=nb: e.dma_start(
                out=wo_sb[buf][:],
                in_=wo[cg * 1024:(cg + 1) * 1024, nb:nb + 512].rearrange("(c p) n -> p c n", p=128))),
                waits=lw, inc=sWOB[buf], inc_n=16)
            LD_WO = sWOB[buf].v
            for ci in range(8):
                for mt in range(8):
                    w0 = []
                    if ci == 0 and mt == 0:
                        w0 = [(sWOB[buf], LD_WO)]
                        if cg == 0 and ns >= 1:
                            w0.append((semV, v_done[ns - 1]))
                    E("tensor", (lambda e, buf=buf, cg=cg, ci=ci, mt=mt: e.matmul(
                        psb[mt][:], ot_sb[:, cg * 8 + ci, mt * 128:(mt + 1) * 128],
                        wo_sb[buf][:, ci, :],
                        start=(cg == 0 and ci == 0), stop=(cg == 3 and ci == 7))),
                        waits=w0)
            em.mark_last("tensor", semPE)
            pe_grp[(ns, cg)] = semPE.v
        PE_NS = semPE.v

        for mt in range(8):
            gb = mt % 2
            wld = [(semPE, PE_NS)]
            if mt >= 2:
                wld.append((sOUTB[gb], st_out[(ns, mt - 2)]))
            elif ns >= 1:
                wld.append((sOUTB[gb], st_out[(ns - 1, 6 + mt)]))
            E("sync", (lambda e, mt=mt, gb=gb, nb=nb: e.dma_start(
                out=gt_sb[gb][:], in_=g_in[mt * 128:(mt + 1) * 128, nb:nb + 512])),
                waits=wld, inc=sGXB[gb], inc_n=16)
            E("sync", (lambda e, mt=mt, gb=gb, nb=nb: e.dma_start(
                out=xt_sb[gb][:], in_=x_in[mt * 128:(mt + 1) * 128, nb:nb + 512])),
                inc=sGXB[gb], inc_n=16)
            LD_GX = sGXB[gb].v
            E("vector", (lambda e, mt=mt, gb=gb: e.tensor_mul(tmp_sb[:], psb[mt][:], gt_sb[gb][:])),
              waits=[(sGXB[gb], LD_GX)], inc=semV)
            E("vector", (lambda e, gb=gb: e.tensor_add(res_sb[gb][:], tmp_sb[:], xt_sb[gb][:])),
              inc=semV)
            v_done[ns] = semV.v
            V_R = semV.v
            E("sync", (lambda e, mt=mt, gb=gb, nb=nb: e.dma_start(
                out=out_c[mt * 128:(mt + 1) * 128, nb:nb + 512], in_=res_sb[gb][:])),
                waits=[(semV, V_R)], inc=sOUTB[gb], inc_n=16)
            st_out[(ns, mt)] = sOUTB[gb].v
    em.barrier()

    _finish(nc, em, sOUTB)
    st.close()
    return nc



def _host_consts():
    bf = ml_dtypes.bfloat16
    ident = np.eye(128, dtype=np.float32).astype(bf)
    identf = np.eye(128, dtype=np.float32)

    iidx = np.arange(128)[:, None]
    uidx = np.arange(896)[None, :]
    maskbin = (iidx <= uidx - 384).astype(np.float32).astype(bf)

    inv = 1.0 / (10000.0 ** (np.arange(0, ROT, 2) / ROT))
    t = np.arange(S, dtype=np.float64)
    sinus = np.outer(t, inv)
    cos64 = np.repeat(np.cos(sinus), 2, axis=1).astype(np.float32)
    sin64 = np.repeat(np.sin(sinus), 2, axis=1).astype(np.float32)
    rope_cos = np.ones((S, HPC * HD), np.float32)
    rope_sin = np.zeros((S, HPC * HD), np.float32)
    for h in range(HPC):
        rope_cos[:, h * HD:h * HD + ROT] = cos64
        rope_sin[:, h * HD:h * HD + ROT] = sin64

    e_gate = np.zeros((NCH, 128, NG), dtype=np.float32)
    for c2 in range(NCH):
        j, r = c2 // 4, c2 % 4
        k8 = r * 128 + np.arange(128)
        G = (8 * k8 + j) // 128
        e_gate[c2, np.arange(128), G] = -1.0
    e_gate = e_gate.reshape(NCH * 128, NG).astype(bf)

    e_down = np.zeros((NGD, 128, NGD), dtype=np.float32)
    dshift = np.zeros((128, NGD), dtype=np.int32)
    for c2 in range(NGD):
        kpp = 128 * c2 + np.arange(128)
        j = kpp // KP8D
        k8 = kpp % KP8D
        G = (8 * k8 + j) // 128
        e_down[c2, np.arange(128), G] = 1.0
        dshift[:, c2] = 4 * j
    e_down = e_down.reshape(NGD * 128, NGD).astype(bf)

    e2b = np.zeros((2, 256), dtype=np.float32)
    e2b[0, 0:128] = 1.0
    e2b[1, 128:256] = 1.0
    return dict(ident=ident, identf=identf, maskbin=maskbin, rope_cos=rope_cos,
                rope_sin=rope_sin, e_gate=e_gate, e_down=e_down, dshift=dshift,
                e2b=e2b)



def kernel(L0_attn_hidden_states, L0_attn_Wq, L0_attn_Wk, L0_attn_Wv, L0_attn_Wo,
           L0_ffn_gate_qweight, L0_ffn_gate_scales, L0_ffn_gate_qzeros, L0_ffn_gate_g_idx,
           L0_ffn_up_qweight, L0_ffn_up_scales, L0_ffn_up_qzeros, L0_ffn_up_g_idx,
           L0_ffn_down_qweight, L0_ffn_down_scales, L0_ffn_down_qzeros, L0_ffn_down_g_idx):
    x = np.ascontiguousarray(np.asarray(L0_attn_hidden_states, dtype=np.float32).reshape(S, D))
    xT = np.ascontiguousarray(x.T)
    consts = _host_consts()

    nc_a = build_launch_a()
    in_maps_a = []
    for c in range(NCORE):
        hs = c * HPC * HD
        m = dict(
            xT=xT,
            x_c=np.ascontiguousarray(x[c * TOK:(c + 1) * TOK, :]),
            wq_s=np.ascontiguousarray(np.asarray(L0_attn_Wq, np.float32)[:, hs:hs + HPC * HD]),
            wk_s=np.ascontiguousarray(np.asarray(L0_attn_Wk, np.float32)[:, hs:hs + HPC * HD]),
            wv_s=np.ascontiguousarray(np.asarray(L0_attn_Wv, np.float32)[:, hs:hs + HPC * HD]),
            rope_cos=consts["rope_cos"], rope_sin=consts["rope_sin"],
            g_qw=np.ascontiguousarray(np.asarray(L0_ffn_gate_qweight, np.int32)),
            u_qw=np.ascontiguousarray(np.asarray(L0_ffn_up_qweight, np.int32)),
            d_qw=np.ascontiguousarray(np.asarray(L0_ffn_down_qweight, np.int32)),
            g_sc=np.ascontiguousarray(np.asarray(L0_ffn_gate_scales, np.float32)),
            u_sc=np.ascontiguousarray(np.asarray(L0_ffn_up_scales, np.float32)),
            d_sc=np.ascontiguousarray(np.asarray(L0_ffn_down_scales, np.float32)),
            g_qz=np.ascontiguousarray(np.asarray(L0_ffn_gate_qzeros, np.int32)),
            u_qz=np.ascontiguousarray(np.asarray(L0_ffn_up_qzeros, np.int32)),
            d_qz=np.ascontiguousarray(np.asarray(L0_ffn_down_qzeros, np.int32)),
            ident=consts["ident"], identf=consts["identf"], maskbin=consts["maskbin"],
            e_gate=consts["e_gate"], e_down=consts["e_down"], dshift=consts["dshift"],
            e2b=consts["e2b"],
        )
        in_maps_a.append(m)

    res_a = run_bass_kernel_spmd(nc_a, in_maps_a, core_ids=list(range(NCORE)))
    o_full = np.concatenate([np.asarray(res_a.results[c]["o_out"]) for c in range(NCORE)], axis=1)
    g_full = [np.asarray(res_a.results[c]["g_out"]) for c in range(NCORE)]

    nc_b = build_launch_b()
    in_maps_b = []
    wo = np.ascontiguousarray(np.asarray(L0_attn_Wo, np.float32))
    for c in range(NCORE):
        in_maps_b.append(dict(
            o_in=np.ascontiguousarray(o_full[c * TOK:(c + 1) * TOK, :]),
            g_in=np.ascontiguousarray(g_full[c]),
            x_in=np.ascontiguousarray(x[c * TOK:(c + 1) * TOK, :]),
            wo=wo, ident=consts["ident"],
        ))
    res_b = run_bass_kernel_spmd(nc_b, in_maps_b, core_ids=list(range(NCORE)))
    out = np.concatenate([np.asarray(res_b.results[c]["out_c"]) for c in range(NCORE)], axis=0)
    return out.reshape(1, S, D).astype(np.float32)


# version 06
# speedup vs baseline: 1.6592x; 1.0922x over previous
import contextlib
import numpy as np
import ml_dtypes

import concourse.bass as bass
import concourse.mybir as mybir
from concourse.bass_utils import run_bass_kernel_spmd

F32 = mybir.dt.float32
BF16 = mybir.dt.bfloat16
I32 = mybir.dt.int32
AF = mybir.ActivationFunctionType
ALU = mybir.AluOpType
AX = mybir.AxisListType

S = 8192
D = 4096
H = 16
HD = 256
ROT = 64
INTER = 11008
NCORE = 8
HPC = H // NCORE
TOK = S // NCORE
NCH = D // 128
NG = D // 128
NGD = INTER // 128
KP8 = D // 8
KP8D = INTER // 8
EPS = 1e-5


class SemCnt:
    def __init__(self, handle):
        self.h = handle
        self.v = 0


class Emitter:
    ENGINES = ("sync", "gpsimd", "vector", "scalar", "tensor")

    def __init__(self, own_sems=None):
        self.streams = {e: [] for e in self.ENGINES}
        self.last = {e: None for e in self.ENGINES}
        self.own = own_sems or {}
        self.all_sems = []

    def emit(self, eng, fn, waits=(), inc=None, inc_n=1):
        rec = [(s.h, v) for (s, v) in waits if v > 0]
        own = self.own.get(eng)
        if own is not None and own.v > 0 and fn is not None:
            rec.append((own.h, own.v))
        inc_h = None
        if inc is not None:
            inc.v += inc_n
            inc_h = inc.h
            self.last[eng] = (inc, inc.v)
        self.streams[eng].append((fn, rec, inc_h, inc_n))

    def mark_last(self, eng, inc, inc_n=1):
        fn, waits, inc_h, n = self.streams[eng][-1]
        assert inc_h is None and fn is not None
        inc.v += inc_n
        self.last[eng] = (inc, inc.v)
        self.streams[eng][-1] = (fn, waits, inc.h, inc_n)

    def barrier(self):
        marks = [(s, s.v) for s in self.all_sems if s.v > 0]
        for eng in self.ENGINES:
            if marks:
                self.emit(eng, None, waits=marks)

    def replay(self, eng_name, eng_obj):
        for fn, waits, inc_h, inc_n in self.streams[eng_name]:
            for (h, v) in waits:
                eng_obj.wait_ge(h, v)
            ins = eng_obj.nop() if fn is None else fn(eng_obj)
            if inc_h is not None:
                ins.then_inc(inc_h, inc_n)


def _rearr_cpt(ap, p=128):
    return ap.rearrange("(c p) t -> p c t", p=p)


def _down_segments(c2):
    segs = []
    p = 0
    while p < 128:
        kpp = 128 * c2 + p
        j = kpp // KP8D
        k8 = kpp % KP8D
        cnt = min(128 - p, KP8D - k8)
        segs.append((p, cnt, j, k8))
        p += cnt
    return segs



def build_launch_a(upto=99):
    nc = bass.Bass("TRN2", target_bir_lowering=False, debug=False)

    xT = nc.dram_tensor("xT", [D, S], F32, kind="ExternalInput").ap()
    x_c = nc.dram_tensor("x_c", [TOK, D], F32, kind="ExternalInput").ap()
    wq_s = nc.dram_tensor("wq_s", [D, HPC * HD], F32, kind="ExternalInput").ap()
    wk_s = nc.dram_tensor("wk_s", [D, HPC * HD], F32, kind="ExternalInput").ap()
    wv_s = nc.dram_tensor("wv_s", [D, HPC * HD], F32, kind="ExternalInput").ap()
    rope_cos = nc.dram_tensor("rope_cos", [S, HPC * HD], F32, kind="ExternalInput").ap()
    rope_sin = nc.dram_tensor("rope_sin", [S, HPC * HD], F32, kind="ExternalInput").ap()
    g_qw = nc.dram_tensor("g_qw", [KP8, INTER], I32, kind="ExternalInput").ap()
    u_qw = nc.dram_tensor("u_qw", [KP8, INTER], I32, kind="ExternalInput").ap()
    d_qw = nc.dram_tensor("d_qw", [KP8D, D], I32, kind="ExternalInput").ap()
    g_sc = nc.dram_tensor("g_sc", [NG, INTER], F32, kind="ExternalInput").ap()
    u_sc = nc.dram_tensor("u_sc", [NG, INTER], F32, kind="ExternalInput").ap()
    d_sc = nc.dram_tensor("d_sc", [NGD, D], F32, kind="ExternalInput").ap()
    g_qz = nc.dram_tensor("g_qz", [NG, INTER // 8], I32, kind="ExternalInput").ap()
    u_qz = nc.dram_tensor("u_qz", [NG, INTER // 8], I32, kind="ExternalInput").ap()
    d_qz = nc.dram_tensor("d_qz", [NGD, D // 8], I32, kind="ExternalInput").ap()
    ident = nc.dram_tensor("ident", [128, 128], BF16, kind="ExternalInput").ap()
    identf = nc.dram_tensor("identf", [128, 128], F32, kind="ExternalInput").ap()
    maskbin = nc.dram_tensor("maskbin", [128, 896], BF16, kind="ExternalInput").ap()
    e_gate = nc.dram_tensor("e_gate", [NCH * 128, NG], BF16, kind="ExternalInput").ap()
    e_down = nc.dram_tensor("e_down", [NGD * 128, NGD], BF16, kind="ExternalInput").ap()
    dshift = nc.dram_tensor("dshift", [128, NGD], I32, kind="ExternalInput").ap()
    e2b = nc.dram_tensor("e2b", [2, 256], F32, kind="ExternalInput").ap()

    o_out = nc.dram_tensor("o_out", [S, HPC * HD], BF16, kind="ExternalOutput").ap()
    g_out = nc.dram_tensor("g_out", [TOK, D], F32, kind="ExternalOutput").ap()

    qt_d = nc.dram_tensor("qt_d", [4 * 128, S], BF16, kind="Internal")
    kt_d = nc.dram_tensor("kt_d", [4 * 128, S], BF16, kind="Internal")
    v_d = nc.dram_tensor("v_d", [S, HPC * HD], BF16, kind="Internal")
    ht_d = nc.dram_tensor("ht_d", [INTER, TOK], BF16, kind="Internal")

    glob = contextlib.ExitStack()
    Tg = lambda n, s, d: glob.enter_context(nc.sbuf_tensor(n, s, d))

    ident_sb = Tg("ident_sb", [128, 128], BF16)
    identf_sb = Tg("identf_sb", [128, 128], F32)
    onescol_bf = Tg("onescol_bf", [128, 1], BF16)
    ones11_f = Tg("ones11_f", [1, 1], F32)
    eps_sb = Tg("eps_sb", [1, 1], F32)
    epscol_sb = Tg("epscol_sb", [128, 1], F32)
    negmu_sb = Tg("negmu_sb", [1, S], BF16)
    rcol_all = Tg("rcol_all", [128, 64], F32)
    cwq_sb = Tg("cwq_sb", [1, 512], BF16)
    cwk_sb = Tg("cwk_sb", [1, 512], BF16)
    cwv_sb = Tg("cwv_sb", [1, 512], BF16)
    negb_col = Tg("negb_col", [128, 2], F32)
    negb_min = Tg("negb_min", [128, 1], F32)
    onesrow_f = Tg("onesrow_f", [1, 128], F32)

    def mksem(n):
        s = SemCnt(glob.enter_context(nc.semaphore(n)))
        _allsems.append(s)
        return s
    _allsems = []
    semPE = mksem("semPE")
    semV = mksem("semV")
    semA = mksem("semA")
    em = Emitter(own_sems={"vector": semV, "scalar": semA})
    em.all_sems = _allsems
    sCONST = mksem("sCONST")
    sWGT = mksem("sWGT")
    sXT = [mksem("sXT0"), mksem("sXT1")]
    sCOS = mksem("sCOS")
    sQST = [mksem("sQST0"), mksem("sQST1")]
    sVST = [mksem("sVST0"), mksem("sVST1")]
    sQTL = mksem("sQTL")
    sKV = [mksem("sKV0"), mksem("sKV1")]
    sOST = [mksem("sOST0"), mksem("sOST1")]
    sXC = [mksem("sXC0"), mksem("sXC1")]
    sZL = mksem("sZL")
    sEG = mksem("sEG")
    sED = mksem("sED")
    sDS = mksem("sDS")
    sQW = [mksem("sQW0"), mksem("sQW1")]
    sSE = [mksem("sSE0"), mksem("sSE1")]
    sHST = [mksem("sHST0"), mksem("sHST1")]
    sHC = [mksem("sHC0"), mksem("sHC1")]
    sQD = [mksem("sQD0"), mksem("sQD1")]
    sSD = [mksem("sSD0"), mksem("sSD1")]
    sGST = [mksem("sGST0"), mksem("sGST1")]
    store_sems = sQST + sVST + sOST + sHST + sGST
    E = em.emit

    E("sync", lambda e: e.dma_start(out=ident_sb[:], in_=ident), inc=sCONST, inc_n=16)
    E("sync", lambda e: e.dma_start(out=identf_sb[:], in_=identf), inc=sCONST, inc_n=16)
    S_CONST = sCONST.v
    E("vector", lambda e: e.memset(onescol_bf[:], 1.0), inc=semV)
    E("vector", lambda e: e.memset(ones11_f[:], 1.0), inc=semV)
    E("vector", lambda e: e.memset(eps_sb[:], EPS), inc=semV)
    E("vector", lambda e: e.memset(onesrow_f[:], 1.0), inc=semV)
    E("vector", lambda e: e.memset(epscol_sb[:], EPS), inc=semV)
    V_CONST = semV.v

    with contextlib.ExitStack() as ph:
        Tp = lambda n, s, d: ph.enter_context(nc.sbuf_tensor(n, s, d))
        Pp = lambda n, s, d: ph.enter_context(nc.psum_tensor(n, s, d))

        xt_sb = [Tp(f"xt{i}", [128, NCH, 512], BF16) for i in range(2)]
        wq_sb = Tp("wq_sb", [128, NCH, 512], BF16)
        wk_sb = Tp("wk_sb", [128, NCH, 512], BF16)
        sq_sb = [Tp(f"sq{i}", [128, 4, 512], BF16) for i in range(2)]
        rowt = Tp("rowt", [1, 512], F32)
        rowt2 = Tp("rowt2", [1, 512], F32)
        rowt3 = Tp("rowt3", [1, 512], F32)
        rstd_row = Tp("rstd_row", [1, 512], F32)
        qk_f32 = Tp("qk_f32", [128, 512], F32)
        q_bf = [Tp(f"q_bf{i}", [128, 512], BF16) for i in range(2)]
        k_bf = [Tp(f"k_bf{i}", [128, 512], BF16) for i in range(2)]
        rt1 = Tp("rt1", [128, 512], F32)
        rt2 = Tp("rt2", [128, 512], F32)
        cos_sb = Tp("cos_sb", [128, HPC * HD], F32)
        sin_sb = Tp("sin_sb", [128, HPC * HD], F32)
        rcol_sb = Tp("rcol_sb", [128, 4], F32)
        qtw_sb = [Tp(f"qtw_sb{i}", [128, 4, 512], BF16) for i in range(2)]
        ktw_sb = [Tp(f"ktw_sb{i}", [128, 4, 512], BF16) for i in range(2)]

        ps_rows = Pp("ps_rows", [1, 1024], F32)
        ps_q = [Pp(f"ps_q{i}", [128, 512], F32) for i in range(2)]
        ps_k = [Pp(f"ps_k{i}", [128, 512], F32) for i in range(2)]
        ps_t = Pp("ps_t", [128, 1024], BF16)
        ps_rc = Pp("ps_rc", [128, 4], F32)


        E("gpsimd", lambda e: e.dma_start(out=wq_sb[:], in_=_rearr_cpt(wq_s)), inc=sWGT, inc_n=16)
        E("gpsimd", lambda e: e.dma_start(out=wk_sb[:], in_=_rearr_cpt(wk_s)), inc=sWGT, inc_n=16)
        LD_W = sWGT.v

        for wi, wsb in ((0, wq_sb), (1, wk_sb)):
            for ci in range(NCH):
                E("tensor", (lambda e, wsb=wsb, ci=ci, wi=wi: e.matmul(
                    ps_rows[0:1, 512 * wi:512 * wi + 512], onescol_bf[:], wsb[:, ci, :],
                    start=(ci == 0), stop=(ci == NCH - 1))),
                    waits=([(sWGT, LD_W), (semV, V_CONST)] if (ci == 0 and wi == 0) else ()))
        em.mark_last("tensor", semPE)
        PE_CW = semPE.v
        E("vector", lambda e: e.tensor_copy(cwq_sb[:], ps_rows[0:1, 0:512]),
          waits=[(semPE, PE_CW)], inc=semV)
        E("vector", lambda e: e.tensor_copy(cwk_sb[:], ps_rows[0:1, 512:1024]), inc=semV)
        v_rows_free = semV.v

        pe_trd = {}

        def emit_tr(mm):
            gg = mm // 4
            mmi = mm % 4
            wtr = [(semV, v_evac[mm])]
            if mm == 0:
                wtr.append((sCONST, S_CONST))
            if mm >= 1:
                wtr.append((semV, v_trev[mm - 1]))
            for proj, obf, half in ((0, q_bf[mm % 2], 0), (1, k_bf[mm % 2], 1)):
                for dc in range(4):
                    E("tensor", (lambda e, obf=obf, dc=dc, half=half: e.transpose(
                        ps_t[:, half * 512 + dc * 128:half * 512 + (dc + 1) * 128],
                        obf[:, dc * 128:(dc + 1) * 128], ident_sb[:])),
                        waits=(wtr if (dc == 0 and proj == 0) else ()))
            em.mark_last("tensor", semPE)
            pe_trd[mm] = semPE.v
            PE_TR = semPE.v
            wev = [(semPE, PE_TR)]
            if gg >= 2 and mmi == 0:
                wev.append((sQST[gg % 2], st_qk[gg - 2]))
            E("vector", (lambda e, mmi=mmi, gg=gg: e.tensor_copy(
                qtw_sb[gg % 2][:, :, mmi * 128:(mmi + 1) * 128],
                ps_t[:, 0:512].rearrange("p (c n) -> p c n", c=4))),
                waits=wev, inc=semV)
            E("vector", (lambda e, mmi=mmi, gg=gg: e.tensor_copy(
                ktw_sb[gg % 2][:, :, mmi * 128:(mmi + 1) * 128],
                ps_t[:, 512:1024].rearrange("p (c n) -> p c n", c=4))), inc=semV)
            v_trev[mm] = semV.v
            if mmi == 3:
                wst = [(semV, v_trev[mm])]
                E("sync", (lambda e, gg=gg: e.dma_start(
                    out=_rearr_cpt(qt_d.ap())[:, :, gg * 512:(gg + 1) * 512],
                    in_=qtw_sb[gg % 2][:])), waits=wst, inc=sQST[gg % 2], inc_n=16)
                E("sync", (lambda e, gg=gg: e.dma_start(
                    out=_rearr_cpt(kt_d.ap())[:, :, gg * 512:(gg + 1) * 512],
                    in_=ktw_sb[gg % 2][:])), inc=sQST[gg % 2], inc_n=16)
                st_qk[gg] = sQST[gg % 2].v

        pe_xt_done = {}
        v_evac = {}
        v_trev = {}
        st_qk = {}
        pe_sq = {}

        for g in range(S // 512):
            buf = g % 2
            lw = [(semPE, pe_xt_done[g - 2])] if g >= 2 else []
            E("gpsimd", (lambda e, buf=buf, g=g: e.dma_start(
                out=xt_sb[buf][:], in_=_rearr_cpt(xT)[:, :, g * 512:(g + 1) * 512])),
                waits=lw, inc=sXT[buf], inc_n=16)
            LD_G = sXT[buf].v

            for cg4 in range(NCH // 4):
                sb = cg4 % 2
                w = [(sXT[buf], LD_G)]
                if (g, cg4 - 2) in pe_sq:
                    w.append((semPE, pe_sq[(g, cg4 - 2)]))
                elif g > 0 and cg4 < 2:
                    w.append((semPE, pe_sq[(g - 1, NCH // 4 - 2 + cg4)]))
                E("vector", (lambda e, buf=buf, cg4=cg4, sb=sb: e.tensor_mul(
                    sq_sb[sb][:], xt_sb[buf][:, 4 * cg4:4 * cg4 + 4, :],
                    xt_sb[buf][:, 4 * cg4:4 * cg4 + 4, :])),
                    waits=w, inc=semV)
                V_SQ = semV.v
                for cr in range(4):
                    ci = 4 * cg4 + cr
                    wsum = [(sXT[buf], LD_G), (semV, v_rows_free)] if ci == 0 else ()
                    E("tensor", (lambda e, buf=buf, ci=ci: e.matmul(
                        ps_rows[0:1, 0:512], onescol_bf[:], xt_sb[buf][:, ci, :],
                        start=(ci == 0), stop=(ci == NCH - 1))), waits=wsum)
                    E("tensor", (lambda e, sb=sb, cr=cr, ci=ci: e.matmul(
                        ps_rows[0:1, 512:1024], onescol_bf[:], sq_sb[sb][:, cr, :],
                        start=(ci == 0), stop=(ci == NCH - 1))),
                        waits=([(semV, V_SQ)] if cr == 0 else ()))
                em.mark_last("tensor", semPE)
                pe_sq[(g, cg4)] = semPE.v
            PE_ROWS = semPE.v

            E("vector", lambda e: e.tensor_scalar(rowt[:], ps_rows[0:1, 0:512], 1.0 / D, None, op0=ALU.mult),
              waits=[(semPE, PE_ROWS)], inc=semV)
            E("vector", lambda e: e.tensor_scalar(rowt2[:], ps_rows[0:1, 512:1024], 1.0 / D, None, op0=ALU.mult), inc=semV)
            v_rows_free = semV.v
            E("vector", lambda e: e.tensor_mul(rowt3[:], rowt[:], rowt[:]), inc=semV)
            E("vector", lambda e: e.tensor_sub(rowt2[:], rowt2[:], rowt3[:]), inc=semV)
            V_VAR = semV.v
            E("scalar", lambda e: e.activation(rowt2[:], rowt2[:], AF.Ln, bias=eps_sb[0:1, 0:1]),
              waits=[(semV, V_VAR)], inc=semA)
            E("scalar", lambda e: e.activation(rstd_row[:], rowt2[:], AF.Exp, scale=-0.5), inc=semA)
            E("scalar", (lambda e, g=g: e.activation(
                negmu_sb[0:1, g * 512:(g + 1) * 512], rowt[:], AF.Copy, scale=-1.0)), inc=semA)
            A_ROWS = semA.v

            for mi in range(4):
                E("tensor", (lambda e, mi=mi: e.matmul(
                    ps_rc[:, mi:mi + 1], rstd_row[0:1, mi * 128:(mi + 1) * 128], ones11_f[:],
                    start=True, stop=True)),
                    waits=([(semA, A_ROWS)] if mi == 0 else ()))
            em.mark_last("tensor", semPE)
            PE_RC = semPE.v
            E("vector", lambda e: e.tensor_copy(rcol_sb[:], ps_rc[:]), waits=[(semPE, PE_RC)], inc=semV)
            E("vector", (lambda e, g=g: e.tensor_copy(rcol_all[:, g * 4:g * 4 + 4], rcol_sb[:])), inc=semV)
            V_RC = semV.v

            for mi in range(4):
                m = g * 4 + mi
                qb = m % 2
                E("sync", (lambda e, m=m: e.dma_start(out=cos_sb[:], in_=rope_cos[m * 128:(m + 1) * 128, :])),
                  waits=([(semV, v_evac[m - 1])] if m >= 1 else []), inc=sCOS, inc_n=16)
                E("sync", (lambda e, m=m: e.dma_start(out=sin_sb[:], in_=rope_sin[m * 128:(m + 1) * 128, :])),
                  inc=sCOS, inc_n=16)
                S_ROPE = sCOS.v

                for proj, psb, wsb, cwt in ((0, ps_q[qb], wq_sb, cwq_sb), (1, ps_k[qb], wk_sb, cwk_sb)):
                    w0 = [(semV, v_evac[m - 2])] if m >= 2 else []
                    for ci in range(NCH):
                        E("tensor", (lambda e, psb=psb, buf=buf, ci=ci, mi=mi, wsb=wsb: e.matmul(
                            psb[:], xt_sb[buf][:, ci, mi * 128:(mi + 1) * 128], wsb[:, ci, :],
                            start=(ci == 0), stop=False)),
                            waits=(w0 if ci == 0 else ()))
                    E("tensor", (lambda e, psb=psb, g=g, mi=mi, cwt=cwt: e.matmul(
                        psb[:], negmu_sb[0:1, g * 512 + mi * 128:g * 512 + (mi + 1) * 128],
                        cwt[:], start=False, stop=True)), inc=semPE)
                pe_xt_done[g] = semPE.v
                PE_QK = semPE.v

                for proj, psb, obf in ((0, ps_q[qb], q_bf[m % 2]), (1, ps_k[qb], k_bf[m % 2])):
                    wr0 = [(semPE, PE_QK), (semV, V_RC), (sCOS, S_ROPE)]
                    if m >= 2 and proj == 0:
                        wr0.append((semPE, pe_trd[m - 2]))
                    E("vector", (lambda e, psb=psb, mi=mi, wr0=wr0: e.tensor_scalar(
                        qk_f32[:], psb[:], rcol_sb[:, mi:mi + 1], None, op0=ALU.mult)),
                        waits=wr0, inc=semV)
                    E("vector", lambda e: e.tensor_mul(rt1[:], qk_f32[:], cos_sb[:]), inc=semV)
                    E("vector", lambda e: e.tensor_mul(
                        rt2[:, 0:512:2], qk_f32[:, 1:512:2], sin_sb[:, 0:512:2]), inc=semV)
                    E("vector", lambda e: e.tensor_mul(
                        rt2[:, 1:512:2], qk_f32[:, 0:512:2], sin_sb[:, 1:512:2]), inc=semV)
                    E("vector", (lambda e, obf=obf: e.tensor_add(obf[:], rt1[:], rt2[:])), inc=semV)
                v_evac[m] = semV.v
                V_EV = semV.v


                if m >= 1:
                    emit_tr(m - 1)

        emit_tr(S // 128 - 1)
        em.barrier()

    if upto <= 1:
        _finish(nc, em, store_sems)
        glob.close()
        return nc

    with contextlib.ExitStack() as ph:
        Tp = lambda n, s, d: ph.enter_context(nc.sbuf_tensor(n, s, d))
        Pp = lambda n, s, d: ph.enter_context(nc.psum_tensor(n, s, d))
        xtv_sb = [Tp(f"xtv{i}", [128, NCH, 512], BF16) for i in range(2)]
        wv_sb = Tp("wv_sb", [128, NCH, 512], BF16)
        v_bf = [Tp(f"v_bf{i}", [128, 512], BF16) for i in range(2)]
        ps_rowsv = Pp("ps_rowsvv", [1, 1024], F32)
        ps_v = [Pp(f"ps_v{i}", [128, 512], F32) for i in range(2)]

        E("gpsimd", lambda e: e.dma_start(out=wv_sb[:], in_=_rearr_cpt(wv_s)), inc=sWGT, inc_n=16)
        LD_WV = sWGT.v
        for ci in range(NCH):
            E("tensor", (lambda e, ci=ci: e.matmul(
                ps_rowsv[0:1, 0:512], onescol_bf[:], wv_sb[:, ci, :],
                start=(ci == 0), stop=(ci == NCH - 1))),
                waits=([(sWGT, LD_WV)] if ci == 0 else ()))
        em.mark_last("tensor", semPE)
        PE_CWV = semPE.v
        E("vector", lambda e: e.tensor_copy(cwv_sb[:], ps_rowsv[0:1, 0:512]),
          waits=[(semPE, PE_CWV)], inc=semV)
        V_CWV = semV.v

        v_ev = {}
        st_v = {}
        pe_g = {}
        for g in range(S // 512):
            buf = g % 2
            lw = [(semPE, pe_g[g - 2])] if g >= 2 else []
            E("gpsimd", (lambda e, buf=buf, g=g: e.dma_start(
                out=xtv_sb[buf][:], in_=_rearr_cpt(xT)[:, :, g * 512:(g + 1) * 512])),
                waits=lw, inc=sXT[buf], inc_n=16)
            LD_G = sXT[buf].v
            for mi in range(4):
                m = g * 4 + mi
                vb = m % 2
                w0 = [(sXT[buf], LD_G), (semV, V_CWV)]
                if m >= 2:
                    w0.append((semV, v_ev[m - 2]))
                for ci in range(NCH):
                    E("tensor", (lambda e, vb=vb, buf=buf, ci=ci, mi=mi: e.matmul(
                        ps_v[vb][:], xtv_sb[buf][:, ci, mi * 128:(mi + 1) * 128], wv_sb[:, ci, :],
                        start=(ci == 0), stop=False)),
                        waits=(w0 if ci == 0 else ()))
                E("tensor", (lambda e, vb=vb, g=g, mi=mi: e.matmul(
                    ps_v[vb][:], negmu_sb[0:1, g * 512 + mi * 128:g * 512 + (mi + 1) * 128],
                    cwv_sb[:], start=False, stop=True)), inc=semPE)
                pe_g[g] = semPE.v
                PE_V = semPE.v
                wv0 = [(semPE, PE_V)]
                if m >= 2:
                    wv0.append((sVST[m % 2], st_v[m - 2]))
                E("vector", (lambda e, vb=vb, m=m: e.tensor_scalar(
                    v_bf[vb][:], ps_v[vb][:], rcol_all[:, m:m + 1], None, op0=ALU.mult)),
                    waits=wv0, inc=semV)
                v_ev[m] = semV.v
                E("sync", (lambda e, m=m, vb=vb: e.dma_start(
                    out=v_d.ap()[m * 128:(m + 1) * 128, :], in_=v_bf[vb][:])),
                    waits=[(semV, v_ev[m])], inc=sVST[vb], inc_n=16)
                st_v[m] = sVST[vb].v
        em.barrier()

    with contextlib.ExitStack() as ph:
        Tp = lambda n, s, d: ph.enter_context(nc.sbuf_tensor(n, s, d))
        Pp = lambda n, s, d: ph.enter_context(nc.psum_tensor(n, s, d))
        qkt_sb = [Tp(f"qkt2b{i}", [128, S], BF16) for i in range(2)]
        sq2b = [Tp(f"sq2b{i}", [128, S], BF16) for i in range(2)]
        nrow = Tp("nrow", [1, S], F32)
        scal2 = Tp("scal2", [1, 8], F32)
        ps2b = Pp("ps2b", [1, 1024], F32)
        ps2c = Pp("ps2c", [128, 2], F32)

        for pi, tdram in ((0, qt_d), (1, kt_d)):
            for h in range(HPC):
                v_sq2 = []
                for dc in range(2):
                    ch = 2 * h + dc
                    wl = []
                    if pi + h + dc > 0:
                        wl.append((semV, semV.v))
                    E("sync", (lambda e, tdram=tdram, ch=ch, dc=dc: e.dma_start(
                        out=qkt_sb[dc][:], in_=tdram.ap()[128 * ch:128 * (ch + 1), :])),
                        waits=wl, inc=sQTL, inc_n=16)
                    LD2B = sQTL.v
                    E("vector", (lambda e, dc=dc: e.tensor_mul(
                        sq2b[dc][:], qkt_sb[dc][:], qkt_sb[dc][:])),
                        waits=[(sQTL, LD2B)], inc=semV)
                    v_sq2.append(semV.v)
                ev_done = 0
                for span in range(S // 512):
                    sl = (span % 2) * 512
                    wm = [(semV, v_sq2[1])]
                    if span >= 2:
                        wm.append((semV, ev_done))
                    E("tensor", (lambda e, span=span, sl=sl: e.matmul(
                        ps2b[0:1, sl:sl + 512], onescol_bf[:],
                        sq2b[0][:, span * 512:(span + 1) * 512], start=True, stop=False)),
                        waits=wm)
                    E("tensor", (lambda e, span=span, sl=sl: e.matmul(
                        ps2b[0:1, sl:sl + 512], onescol_bf[:],
                        sq2b[1][:, span * 512:(span + 1) * 512], start=False, stop=True)))
                    em.mark_last("tensor", semPE)
                    PE2B = semPE.v
                    E("vector", (lambda e, span=span, sl=sl: e.tensor_copy(
                        nrow[0:1, span * 512:(span + 1) * 512], ps2b[0:1, sl:sl + 512])),
                        waits=[(semPE, PE2B)], inc=semV)
                    ev_done = semV.v
                E("vector", (lambda e, pi=pi, h=h: e.reduce_max(
                    scal2[0:1, 2 * pi + h:2 * pi + h + 1], nrow[:], axis=AX.X)), inc=semV)
        V_NORMS = semV.v
        for h in range(HPC):
            E("scalar", (lambda e, h=h: e.activation(
                scal2[0:1, 4 + h:5 + h], scal2[0:1, h:h + 1], AF.Ln)),
                waits=[(semV, V_NORMS)], inc=semA)
            E("scalar", (lambda e, h=h: e.activation(
                scal2[0:1, 6 + h:7 + h], scal2[0:1, 2 + h:3 + h], AF.Ln)), inc=semA)
            A_LN2 = semA.v
            E("vector", (lambda e, h=h: e.tensor_add(
                scal2[0:1, 4 + h:5 + h], scal2[0:1, 4 + h:5 + h], scal2[0:1, 6 + h:7 + h])),
                waits=[(semA, A_LN2)], inc=semV)
            V_AD2 = semV.v
            E("scalar", (lambda e, h=h: e.activation(
                scal2[0:1, 4 + h:5 + h], scal2[0:1, 4 + h:5 + h], AF.Exp, scale=0.5)),
                waits=[(semV, V_AD2)], inc=semA)
            A_EX2 = semA.v
            E("vector", (lambda e, h=h: e.tensor_scalar(
                scal2[0:1, 4 + h:5 + h], scal2[0:1, 4 + h:5 + h], -1.0 / 16.0, None,
                op0=ALU.mult)), waits=[(semA, A_EX2)], inc=semV)
            V_NB = semV.v
            E("tensor", (lambda e, h=h: e.matmul(
                ps2c[:, h:h + 1], onesrow_f[:], scal2[0:1, 4 + h:5 + h],
                start=True, stop=True)), waits=[(semV, V_NB)])
            em.mark_last("tensor", semPE)
            PE2C = semPE.v
            E("vector", (lambda e, h=h: e.tensor_copy(negb_col[:, h:h + 1], ps2c[:, h:h + 1])),
              waits=[(semPE, PE2C)], inc=semV)
        E("vector", lambda e: e.tensor_tensor(
            negb_min[:], negb_col[:, 0:1], negb_col[:, 1:2], op=ALU.min), inc=semV)
        em.barrier()

    if upto <= 2:
        _finish(nc, em, store_sems)
        glob.close()
        return nc

    with contextlib.ExitStack() as ph:
        Tp = lambda n, s, d: ph.enter_context(nc.sbuf_tensor(n, s, d))
        Pp = lambda n, s, d: ph.enter_context(nc.psum_tensor(n, s, d))
        qt_sb = Tp("qt_sb", [128, 4, 512], BF16)
        kt_sb = [Tp(f"kt_s{i}", [128, 4, 512], BF16) for i in range(2)]
        v_sb = [Tp(f"v_s{i}", [128, 4, 512], BF16) for i in range(2)]
        pt_sb = [Tp(f"pt_s{i}", [128, 1024], BF16) for i in range(2)]
        mask_sb = Tp("mask_sb", [128, 896], BF16)
        oev_sb = Tp("oev_sb", [128, 2, 512], F32)
        lrow_pad = Tp("lrow_pad", [128, 512], F32)
        rl_sb = Tp("rl_sb", [128, 4], F32)
        o_sb = [Tp(f"o_sb{i}", [128, 4, 512], BF16) for i in range(2)]

        ps_s2 = Pp("ps_s2", [128, 1024], F32)
        ps_o = [Pp(f"ps_o{i}", [128, 512], F32) for i in range(4)]
        ps_lf = Pp("ps_lf", [1, 1024], F32)

        E("sync", lambda e: e.dma_start(out=mask_sb[:], in_=maskbin), inc=sCONST, inc_n=16)
        S_MASK = sCONST.v

        fin_done = 0
        st_o = {}
        for qc in range(16):
            nkc = 4 * (qc + 1)
            E("sync", (lambda e, qc=qc: e.dma_start(
                out=qt_sb[:], in_=_rearr_cpt(qt_d.ap())[:, :, qc * 512:(qc + 1) * 512])),
                waits=[(semV, fin_done), (sQST[0], sQST[0].v), (sQST[1], sQST[1].v)],
                inc=sQTL, inc_n=16)
            S_QT = sQTL.v

            ld_grp = {}
            kc_pe = {}
            kc_exp = {}
            for kc in range(nkc):
                gi = kc // 4
                gb = gi % 2
                kci = kc % 4
                ub = kc % 2
                if kci == 0:
                    w = [(semV, fin_done)]
                    if gi >= 2:
                        w.append((semPE, kc_pe[4 * (gi - 1) - 1]))
                    if qc == 0 and gi == 0:
                        w += [(sVST[0], sVST[0].v), (sVST[1], sVST[1].v)]
                    E("sync", (lambda e, gi=gi, gb=gb: e.dma_start(
                        out=kt_sb[gb][:],
                        in_=_rearr_cpt(kt_d.ap())[:, :, gi * 512:(gi + 1) * 512])),
                        waits=w, inc=sKV[gb], inc_n=16)
                    E("sync", (lambda e, gi=gi, gb=gb: e.dma_start(
                        out=v_sb[gb][:],
                        in_=v_d.ap()[gi * 512:(gi + 1) * 512, :].rearrange("(g p) d -> p g d", p=128))),
                        inc=sKV[gb], inc_n=16)
                    ld_grp[gi] = sKV[gb].v

                ws = [(sKV[gb], ld_grp[gi]), (sQTL, S_QT)]
                if kc >= 1:
                    ws.append((semA, kc_exp[kc - 1]))
                else:
                    ws.append((semV, fin_done))
                for h in range(HPC):
                    for dc in range(2):
                        E("tensor", (lambda e, gb=gb, h=h, dc=dc, kci=kci: e.matmul(
                            ps_s2[:, 512 * h:512 * h + 512],
                            kt_sb[gb][:, 2 * h + dc, kci * 128:(kci + 1) * 128],
                            qt_sb[:, 2 * h + dc, :], start=(dc == 0), stop=(dc == 1))),
                            waits=(ws if (h == 0 and dc == 0) else ()))
                em.mark_last("tensor", semPE)
                PE_S = semPE.v

                wexp = [(semPE, PE_S)]
                if kc >= 2:
                    wexp.append((semPE, kc_pe[kc - 2]))
                E("scalar", (lambda e, ub=ub: e.activation(
                    pt_sb[ub][:], ps_s2[:], AF.Exp, bias=negb_min[:, 0:1], scale=1.0 / 16.0)),
                    waits=wexp, inc=semA)
                kc_exp[kc] = semA.v
                pt_sem, pt_val = semA, semA.v

                kbase, qbase = kc * 128, qc * 512
                if kbase > qbase - 128:
                    off = 384 - (kbase - qbase)
                    E("vector", (lambda e, ub=ub, off=off: e.tensor_mul(
                        pt_sb[ub][:, 0:512], pt_sb[ub][:, 0:512], mask_sb[:, off:off + 512])),
                        waits=[(semA, pt_val), (sCONST, S_MASK)], inc=semV)
                    E("vector", (lambda e, ub=ub, off=off: e.tensor_mul(
                        pt_sb[ub][:, 512:1024], pt_sb[ub][:, 512:1024], mask_sb[:, off:off + 512])),
                        inc=semV)
                    pt_sem, pt_val = semV, semV.v

                for h in range(HPC):
                    E("tensor", (lambda e, gb=gb, h=h, kci=kci, ub=ub, kc=kc, nkc=nkc: e.matmul(
                        ps_o[2 * h][:], v_sb[gb][:, kci, h * HD:h * HD + 128],
                        pt_sb[ub][:, 512 * h:512 * h + 512],
                        start=(kc == 0), stop=(kc == nkc - 1))),
                        waits=([(pt_sem, pt_val)] if h == 0 else ()))
                    E("tensor", (lambda e, gb=gb, h=h, kci=kci, ub=ub, kc=kc, nkc=nkc: e.matmul(
                        ps_o[2 * h + 1][:], v_sb[gb][:, kci, h * HD + 128:h * HD + 256],
                        pt_sb[ub][:, 512 * h:512 * h + 512],
                        start=(kc == 0), stop=(kc == nkc - 1))))
                    E("tensor", (lambda e, h=h, ub=ub, kc=kc, nkc=nkc: e.matmul(
                        ps_lf[0:1, 512 * h:512 * h + 512], onescol_bf[:],
                        pt_sb[ub][:, 512 * h:512 * h + 512],
                        start=(kc == 0), stop=(kc == nkc - 1))))
                em.mark_last("tensor", semPE)
                kc_pe[kc] = semPE.v

            PE_UNITS = semPE.v
            A_LASTEXP = semA.v
            V_LASTMSK = semV.v

            for h in range(HPC):
                E("vector", (lambda e, h=h: e.tensor_copy(
                    lrow_pad[0:1, :], ps_lf[0:1, 512 * h:512 * h + 512])),
                    waits=[(semPE, PE_UNITS)], inc=semV)
                V_LR = semV.v
                for qs in range(4):
                    E("tensor", (lambda e, qs=qs: e.transpose(
                        ps_s2[:, qs * 128:(qs + 1) * 128],
                        lrow_pad[:, qs * 128:(qs + 1) * 128], identf_sb[:])),
                        waits=([(semV, V_LR), (semA, A_LASTEXP)] if qs == 0 else ()))
                em.mark_last("tensor", semPE)
                PE_LT = semPE.v
                E("vector", (lambda e: e.reciprocal(rl_sb[:], ps_s2[:, 0:512:128])),
                  waits=[(semPE, PE_LT)], inc=semV)
                V_RL = semV.v
                E("vector", (lambda e, h=h: e.tensor_copy(oev_sb[:, 0, :], ps_o[2 * h][:])), inc=semV)
                E("vector", (lambda e, h=h: e.tensor_copy(oev_sb[:, 1, :], ps_o[2 * h + 1][:])), inc=semV)
                V_OE = semV.v
                for pair in range(2):
                    bank = ps_s2[:, 512:1024] if pair == 0 else ps_s2[:, 0:512]
                    wtp = [(semV, V_OE)]
                    if pair == 0:
                        wtp.append((semV, V_RL))
                    else:
                        wtp.append((semV, semV.v))
                    for qi in range(2):
                        qs = pair * 2 + qi
                        for dc in range(2):
                            E("tensor", (lambda e, bank=bank, qi=qi, qs=qs, dc=dc: e.transpose(
                                bank[:, qi * 256 + dc * 128:qi * 256 + (dc + 1) * 128],
                                oev_sb[:, dc, qs * 128:(qs + 1) * 128], identf_sb[:])),
                                waits=(wtp if (qi == 0 and dc == 0) else ()))
                    em.mark_last("tensor", semPE)
                    PE_OT = semPE.v
                    for qi in range(2):
                        qs = pair * 2 + qi
                        wsc = [(semPE, PE_OT)] if qi == 0 else []
                        if h == 0 and pair == 0 and qi == 0 and qc >= 2:
                            wsc.append((sOST[qc % 2], st_o[qc - 2]))
                        E("vector", (lambda e, bank=bank, qi=qi, qs=qs, h=h, qc=qc: e.tensor_scalar(
                            o_sb[qc % 2][:, qs, h * HD:(h + 1) * HD], bank[:, qi * 256:(qi + 1) * 256],
                            rl_sb[:, qs:qs + 1], None, op0=ALU.mult)),
                            waits=wsc, inc=semV)
            fin_done = semV.v
            E("sync", (lambda e, qc=qc: e.dma_start(
                out=o_out[qc * 512:(qc + 1) * 512, :].rearrange("(q p) d -> p q d", p=128),
                in_=o_sb[qc % 2][:])), waits=[(semV, fin_done)], inc=sOST[qc % 2], inc_n=16)
            st_o[qc] = sOST[qc % 2].v
        em.barrier()

    if upto <= 3:
        _finish(nc, em, store_sems)
        glob.close()
        return nc

    xnt_sb = Tg("xnt_sb", [128, NCH, 1024], BF16)
    with contextlib.ExitStack() as ph:
        Tp = lambda n, s, d: ph.enter_context(nc.sbuf_tensor(n, s, d))
        Pp = lambda n, s, d: ph.enter_context(nc.psum_tensor(n, s, d))
        xtile = [Tp(f"xtile{i}", [128, D], F32) for i in range(2)]
        xsq = Tp("xsq", [128, D], F32)
        stat = Tp("stat", [128, 6], F32)
        xn_bf = Tp("xn_bf", [128, D], BF16)
        ps4 = Pp("ps4", [128, 1024], BF16)

        v_done = {}
        pe_tr = {}
        for m in range(8):
            buf = m % 2
            lw = [(semV, v_done[m - 2])] if m >= 2 else []
            E("sync", (lambda e, m=m, buf=buf: e.dma_start(
                out=xtile[buf][:], in_=x_c[m * 128:(m + 1) * 128, :])),
                waits=lw, inc=sXC[buf], inc_n=16)
            LD_X = sXC[buf].v
            E("vector", (lambda e, buf=buf: e.reduce_sum(stat[:, 0:1], xtile[buf][:], axis=AX.X)),
              waits=[(sXC[buf], LD_X)], inc=semV)
            E("vector", (lambda e, buf=buf: e.tensor_mul(xsq[:], xtile[buf][:], xtile[buf][:])), inc=semV)
            E("vector", lambda e: e.reduce_sum(stat[:, 1:2], xsq[:], axis=AX.X), inc=semV)
            E("vector", lambda e: e.tensor_scalar(stat[:, 2:3], stat[:, 0:1], 1.0 / D, None, op0=ALU.mult), inc=semV)
            E("vector", lambda e: e.tensor_scalar(stat[:, 3:4], stat[:, 1:2], 1.0 / D, None, op0=ALU.mult), inc=semV)
            E("vector", lambda e: e.tensor_mul(stat[:, 4:5], stat[:, 2:3], stat[:, 2:3]), inc=semV)
            E("vector", lambda e: e.tensor_sub(stat[:, 3:4], stat[:, 3:4], stat[:, 4:5]), inc=semV)
            V_ST = semV.v
            E("scalar", lambda e: e.activation(stat[:, 3:4], stat[:, 3:4], AF.Ln, bias=epscol_sb[:, 0:1]),
              waits=[(semV, V_ST)], inc=semA)
            E("scalar", lambda e: e.activation(stat[:, 5:6], stat[:, 3:4], AF.Exp, scale=-0.5), inc=semA)
            A_ST = semA.v
            E("vector", (lambda e, buf=buf: e.tensor_scalar(
                xn_bf[:], xtile[buf][:], stat[:, 2:3], stat[:, 5:6],
                op0=ALU.subtract, op1=ALU.mult)),
                waits=[(semA, A_ST)], inc=semV)
            v_done[m] = semV.v
            V_XN = semV.v

            for rnd in range(4):
                wt = [(semV, V_XN)]
                if (m, rnd - 1) in pe_tr:
                    wt.append((semV, pe_tr[(m, rnd - 1)]))
                elif m >= 1 and rnd == 0:
                    wt.append((semV, pe_tr[(m - 1, 3)]))
                for k in range(8):
                    c2 = rnd * 8 + k
                    j = c2 // 4
                    r = c2 % 4
                    E("tensor", (lambda e, j=j, r=r, k=k: e.transpose(
                        ps4[:, k * 128:(k + 1) * 128],
                        xn_bf[:, j + 8 * (r * 128): j + 8 * (r * 128 + 127) + 1: 8],
                        ident_sb[:])),
                        waits=(wt if k == 0 else ()))
                em.mark_last("tensor", semPE)
                PE_T4 = semPE.v
                E("vector", (lambda e, rnd=rnd, m=m: e.tensor_copy(
                    xnt_sb[:, rnd * 8:(rnd + 1) * 8, m * 128:(m + 1) * 128],
                    ps4.rearrange("p (c n) -> p c n", c=8))),
                    waits=[(semPE, PE_T4)], inc=semV)
                pe_tr[(m, rnd)] = semV.v
        em.barrier()

    if upto <= 4:
        _finish(nc, em, store_sems)
        glob.close()
        return nc

    bgate_sb = Tg("bgate_sb", [NG, INTER], BF16)
    bup_sb = Tg("bup_sb", [NG, INTER], BF16)
    bdown_sb = Tg("bdown_sb", [NGD, D], BF16)
    xg_sb = Tg("xg_sb", [NG, 1024], BF16)

    with contextlib.ExitStack() as ph:
        Tp = lambda n, s, d: ph.enter_context(nc.sbuf_tensor(n, s, d))
        Pp = lambda n, s, d: ph.enter_context(nc.psum_tensor(n, s, d))
        qz_sb = Tp("qz_sb", [NGD, 688], I32)
        zi_sb = Tp("zi_sb", [NGD, 5504], I32)
        zf_sb = Tp("zf_sb", [NGD, 5504], F32)
        sc5_sb = Tp("sc5_sb", [NGD, 5504], F32)
        egate_sb = Tp("egate_sb", [128, NCH, NG], BF16)
        ps_xg = Pp("ps_xg", [NG, 1024], F32)

        jobs = []
        for sp in range(2):
            jobs.append((g_qz, g_sc, bgate_sb, NG, 1376, INTER, sp, 1.0))
            jobs.append((u_qz, u_sc, bup_sb, NG, 1376, INTER, sp, 1.0))
        jobs.append((d_qz, d_sc, bdown_sb, NGD, 512, D, 0, -1.0))

        v_prev = 0
        for (qz, sc, bsb, ngr, n8, ntot, sp, sgn) in jobs:
            half8 = n8 // 2 if ntot == INTER else n8
            half = half8 * 8
            E("sync", (lambda e, qz=qz, ngr=ngr, sp=sp, half8=half8: e.dma_start(
                out=qz_sb[0:ngr, 0:half8], in_=qz[:, sp * half8:(sp + 1) * half8])),
                waits=([(semV, v_prev)] if v_prev else []), inc=sZL, inc_n=16)
            E("sync", (lambda e, sc=sc, ngr=ngr, sp=sp, half=half: e.dma_start(
                out=sc5_sb[0:ngr, 0:half], in_=sc[:, sp * half:(sp + 1) * half])),
                inc=sZL, inc_n=16)
            LD_Z = sZL.v
            for jn in range(8):
                E("vector", (lambda e, jn=jn, ngr=ngr, half8=half8: e.tensor_scalar(
                    zi_sb[0:ngr, jn:half8 * 8:8], qz_sb[0:ngr, 0:half8],
                    int(4 * jn), 15, op0=ALU.logical_shift_right, op1=ALU.bitwise_and)),
                    waits=([(sZL, LD_Z)] if jn == 0 else ()), inc=semV)
            E("vector", (lambda e, ngr=ngr, half=half: e.tensor_scalar(
                zf_sb[0:ngr, 0:half], zi_sb[0:ngr, 0:half], 1.0, None, op0=ALU.add)),
                inc=semV)
            if sgn != 1.0:
                E("vector", (lambda e, ngr=ngr, half=half, sgn=sgn: e.tensor_scalar(
                    sc5_sb[0:ngr, 0:half], sc5_sb[0:ngr, 0:half], sgn, None, op0=ALU.mult)),
                    inc=semV)
            E("vector", (lambda e, bsb=bsb, ngr=ngr, sp=sp, half=half: e.tensor_mul(
                bsb[0:ngr, sp * half:(sp + 1) * half], zf_sb[0:ngr, 0:half],
                sc5_sb[0:ngr, 0:half])), inc=semV)
            v_prev = semV.v
        em.barrier()

        E("sync", lambda e: e.dma_start(
            out=egate_sb[:], in_=e_gate.rearrange("(c p) g -> p c g", p=128)), inc=sEG, inc_n=16)
        LD_EG = sEG.v
        for half in range(2):
            for c2 in range(NCH):
                E("tensor", (lambda e, half=half, c2=c2: e.matmul(
                    ps_xg[:, half * 512:(half + 1) * 512], egate_sb[:, c2, :],
                    xnt_sb[:, c2, half * 512:(half + 1) * 512],
                    start=(c2 == 0), stop=(c2 == NCH - 1))),
                    waits=([(sEG, LD_EG)] if (half == 0 and c2 == 0) else ()))
        em.mark_last("tensor", semPE)
        PE_XG = semPE.v
        E("vector", lambda e: e.tensor_copy(xg_sb[:], ps_xg[:]), waits=[(semPE, PE_XG)], inc=semV)
        em.barrier()

    if upto <= 6:
        _finish(nc, em, store_sems)
        glob.close()
        return nc

    with contextlib.ExitStack() as ph:
        Tp = lambda n, s, d: ph.enter_context(nc.sbuf_tensor(n, s, d))
        Pp = lambda n, s, d: ph.enter_context(nc.psum_tensor(n, s, d))
        qwg_sb = [Tp(f"qwg{i}", [128, 4, 128], I32) for i in range(2)]
        qwu_sb = [Tp(f"qwu{i}", [128, 4, 128], I32) for i in range(2)]
        seg_sb = [Tp(f"seg{i}", [128, 4, 128], BF16) for i in range(2)]
        seu_sb = [Tp(f"seu{i}", [128, 4, 128], BF16) for i in range(2)]
        wg_all = [Tp(f"wg_all{i}", [128, NCH, 128], BF16) for i in range(2)]
        wu_all = [Tp(f"wu_all{i}", [128, NCH, 128], BF16) for i in range(2)]
        w4g = Tp("w4g", [128, 4, 128], I32)
        w4u = Tp("w4u", [128, 4, 128], I32)
        silu_t = Tp("silu_t", [128, 512], F32)
        htw = [Tp(f"htw{i}", [128, 1024], BF16) for i in range(2)]
        ps7 = [Pp(f"ps7_{i}", [128, 512], F32) for i in range(8)]

        NSP = INTER // 128
        pe_mm = {}
        v_deq = {}
        v_ht = {}
        st_ht = {}
        a_si = {}

        def emit_loads_deq(sp):
            buf = sp % 2
            base = sp * 128
            lw = [(semV, v_deq[sp - 2])] if sp >= 2 else []
            E("sync", (lambda e, buf=buf, base=base: e.dma_start(
                out=qwg_sb[buf][:], in_=g_qw.rearrange("(r p) n -> p r n", p=128)[:, :, base:base + 128])),
                waits=lw, inc=sQW[buf], inc_n=16)
            E("sync", (lambda e, buf=buf, base=base: e.dma_start(
                out=qwu_sb[buf][:], in_=u_qw.rearrange("(r p) n -> p r n", p=128)[:, :, base:base + 128])),
                inc=sQW[buf], inc_n=16)
            LD_QW = sQW[buf].v
            for r in range(4):
                E("gpsimd", (lambda e, buf=buf, base=base, r=r: e.dma_start(
                    out=seg_sb[buf][:, r, :],
                    in_=g_sc[8 * r:8 * r + 8, base:base + 128].unsqueeze(1).broadcast_to([8, 16, 128]))),
                    waits=(lw if r == 0 else ()), inc=sSE[buf], inc_n=16)
                E("gpsimd", (lambda e, buf=buf, base=base, r=r: e.dma_start(
                    out=seu_sb[buf][:, r, :],
                    in_=u_sc[8 * r:8 * r + 8, base:base + 128].unsqueeze(1).broadcast_to([8, 16, 128]))),
                    inc=sSE[buf], inc_n=16)
            LD_SE = sSE[buf].v
            for j in range(8):
                wdq = [(sQW[buf], LD_QW), (sSE[buf], LD_SE)]
                if sp >= 2 and j == 0:
                    wdq.append((semPE, pe_mm[sp - 2]))
                E("vector", (lambda e, buf=buf, j=j: e.tensor_scalar(
                    w4g[:], qwg_sb[buf][:], int(4 * j), 15,
                    op0=ALU.logical_shift_right, op1=ALU.bitwise_and)),
                    waits=wdq, inc=semV)
                E("vector", (lambda e, buf=buf, j=j: e.tensor_mul(
                    wg_all[buf][:, 4 * j:4 * j + 4, :], w4g[:], seg_sb[buf][:])), inc=semV)
                E("vector", (lambda e, buf=buf, j=j: e.tensor_scalar(
                    w4u[:], qwu_sb[buf][:], int(4 * j), 15,
                    op0=ALU.logical_shift_right, op1=ALU.bitwise_and)), inc=semV)
                E("vector", (lambda e, buf=buf, j=j: e.tensor_mul(
                    wu_all[buf][:, 4 * j:4 * j + 4, :], w4u[:], seu_sb[buf][:])), inc=semV)
            v_deq[sp] = semV.v

        def emit_mms(sp):
            buf = sp % 2
            base = sp * 128
            st_ = (sp % 2) * 4
            wmm0 = [(semV, v_deq[sp])]
            if sp >= 2:
                wmm0.append((semV, v_ht[sp - 2]))
            for c2 in range(NCH):
                for half in range(2):
                    E("tensor", (lambda e, buf=buf, half=half, c2=c2, st_=st_: e.matmul(
                        ps7[st_ + half][:], wg_all[buf][:, c2, :],
                        xnt_sb[:, c2, half * 512:(half + 1) * 512],
                        start=(c2 == 0), stop=False)),
                        waits=(wmm0 if (c2 == 0 and half == 0) else ()))
                    E("tensor", (lambda e, buf=buf, half=half, c2=c2, st_=st_: e.matmul(
                        ps7[st_ + 2 + half][:], wu_all[buf][:, c2, :],
                        xnt_sb[:, c2, half * 512:(half + 1) * 512],
                        start=(c2 == 0), stop=False)))
            for half in range(2):
                E("tensor", (lambda e, half=half, base=base, st_=st_: e.matmul(
                    ps7[st_ + half][:], bgate_sb[:, base:base + 128],
                    xg_sb[:, half * 512:(half + 1) * 512], start=False, stop=True)))
                E("tensor", (lambda e, half=half, base=base, st_=st_: e.matmul(
                    ps7[st_ + 2 + half][:], bup_sb[:, base:base + 128],
                    xg_sb[:, half * 512:(half + 1) * 512], start=False, stop=True)))
            em.mark_last("tensor", semPE)
            pe_mm[sp] = semPE.v

        def emit_evac(sp):
            buf = sp % 2
            base = sp * 128
            st_ = (sp % 2) * 4
            for half in range(2):
                wsi = [(semPE, pe_mm[sp]), (semV, semV.v)]
                E("scalar", (lambda e, half=half, st_=st_: e.activation(
                    silu_t[:], ps7[st_ + half][:], AF.Silu)), waits=wsi, inc=semA)
                A_SI = semA.v
                wht = [(semA, A_SI)]
                if sp >= 2 and half == 0:
                    wht.append((sHST[sp % 2], st_ht[sp - 2]))
                E("vector", (lambda e, buf=buf, half=half, st_=st_: e.tensor_mul(
                    htw[buf][:, half * 512:(half + 1) * 512], silu_t[:],
                    ps7[st_ + 2 + half][:])), waits=wht, inc=semV)
            v_ht[sp] = semV.v
            E("sync", (lambda e, buf=buf, base=base: e.dma_start(
                out=ht_d.ap()[base:base + 128, :], in_=htw[buf][:])),
                waits=[(semV, v_ht[sp])], inc=sHST[buf], inc_n=16)
            st_ht[sp] = sHST[buf].v

        emit_loads_deq(0)
        for sp in range(NSP):
            if sp + 1 < NSP:
                emit_loads_deq(sp + 1)
            emit_mms(sp)
            emit_evac(sp)
        em.barrier()


    if upto <= 7:
        _finish(nc, em, store_sems)
        glob.close()
        return nc

    xgd_sb = Tg("xgd_sb", [NGD, 1024], BF16)
    with contextlib.ExitStack() as ph:
        Tp = lambda n, s, d: ph.enter_context(nc.sbuf_tensor(n, s, d))
        Pp = lambda n, s, d: ph.enter_context(nc.psum_tensor(n, s, d))
        edown_sb = Tp("edown_sb", [128, NGD, NGD], BF16)
        htc_sb = [Tp(f"htc{i}", [128, 1024], BF16) for i in range(2)]
        ps_xgd = Pp("ps_xgd", [NGD, 1024], F32)

        E("sync", lambda e: e.dma_start(
            out=edown_sb[:], in_=e_down.rearrange("(c p) g -> p c g", p=128)), inc=sED, inc_n=16)
        LD_ED = sED.v
        pe_c = {}
        for c2 in range(NGD):
            buf = c2 % 2
            segs = _down_segments(c2)
            w = [(sED, LD_ED)]
            if c2 >= 2:
                w.append((semPE, pe_c[c2 - 2]))
            else:
                w += [(sHST[0], sHST[0].v), (sHST[1], sHST[1].v)]
            for si, (p0, cnt, j, k8) in enumerate(segs):
                E("sync", (lambda e, buf=buf, p0=p0, cnt=cnt, j=j, k8=k8: e.dma_start(
                    out=htc_sb[buf][p0:p0 + cnt, :],
                    in_=ht_d.ap()[8 * k8 + j: 8 * (k8 + cnt - 1) + j + 1: 8, :])),
                    waits=(w if si == 0 else ()), inc=sHC[buf], inc_n=16)
            LD_HT = sHC[buf].v
            for half in range(2):
                E("tensor", (lambda e, buf=buf, c2=c2, half=half: e.matmul(
                    ps_xgd[:, half * 512:(half + 1) * 512], edown_sb[:, c2, :],
                    htc_sb[buf][:, half * 512:(half + 1) * 512],
                    start=(c2 == 0), stop=(c2 == NGD - 1))),
                    waits=([(sHC[buf], LD_HT)] if half == 0 else ()))
            em.mark_last("tensor", semPE)
            pe_c[c2] = semPE.v
        PE_XGD = semPE.v
        E("vector", lambda e: e.tensor_copy(xgd_sb[:], ps_xgd[:]), waits=[(semPE, PE_XGD)], inc=semV)
        em.barrier()

    if upto <= 8:
        _finish(nc, em, store_sems)
        glob.close()
        return nc

    with contextlib.ExitStack() as ph:
        Tp = lambda n, s, d: ph.enter_context(nc.sbuf_tensor(n, s, d))
        Pp = lambda n, s, d: ph.enter_context(nc.psum_tensor(n, s, d))
        qwd4 = [Tp(f"qwd4_{i}", [128, 4, 512], I32) for i in range(2)]
        sed4 = [Tp(f"sed4_{i}", [128, 4, 512], BF16) for i in range(2)]
        htc4 = [Tp(f"htc4_{i}", [128, 4, 1024], BF16) for i in range(2)]
        w4d4 = [Tp(f"w4d4_{i}", [128, 4, 512], I32) for i in range(2)]
        wd4 = [Tp(f"wd4_{i}", [128, 4, 512], BF16) for i in range(2)]
        gtile = [Tp(f"gtile{i}", [128, 512], F32) for i in range(2)]
        ps9 = [Pp(f"ps9_{i}", [128, 512], F32) for i in range(8)]

        NCG = (NGD + 3) // 4
        groups = []
        for ns in range(8):
            for cg in range(NCG):
                c2s = list(range(4 * cg, min(4 * cg + 4, NGD)))
                groups.append((ns, cg, c2s))

        vdeq9 = {}
        pe9 = {}
        a_g = {}
        st_g = {}

        def emit_loads_deq9(gidx):
            ns, cg, c2s = groups[gidx]
            nb = ns * 512
            buf = gidx % 2
            wld = []
            if gidx >= 2:
                wld.append((semV, vdeq9[gidx - 2]))
                wld.append((semPE, pe9[gidx - 2]))
            else:
                wld += [(sHST[0], sHST[0].v), (sHST[1], sHST[1].v),
                        (sHC[0], sHC[0].v), (sHC[1], sHC[1].v)]
            first = True
            firsth = True
            for i, c2 in enumerate(c2s):
                for (p0, cnt, j, k8) in _down_segments(c2):
                    E("sync", (lambda e, buf=buf, i=i, p0=p0, cnt=cnt, k8=k8, nb=nb: e.dma_start(
                        out=qwd4[buf][p0:p0 + cnt, i, :],
                        in_=d_qw[k8:k8 + cnt, nb:nb + 512])),
                        waits=(wld if first else ()), inc=sQD[buf], inc_n=16)
                    first = False
                    E("gpsimd", (lambda e, buf=buf, i=i, p0=p0, cnt=cnt, j=j, k8=k8: e.dma_start(
                        out=htc4[buf][p0:p0 + cnt, i, :],
                        in_=ht_d.ap()[8 * k8 + j: 8 * (k8 + cnt - 1) + j + 1: 8, :])),
                        waits=(wld if firsth else ()), inc=sHC[buf], inc_n=16)
                    firsth = False
            LD_C = sQD[buf].v
            LD_HT4 = sHC[buf].v
            firstg = True
            for i, c2 in enumerate(c2s):
                for (p0, cnt, j, k8) in _down_segments(c2):
                    g0 = (8 * k8 + j) // 128
                    ngr = cnt // 16
                    E("gpsimd", (lambda e, buf=buf, i=i, p0=p0, g0=g0, ngr=ngr, nb=nb: e.dma_start(
                        out=sed4[buf][p0:p0 + 16 * ngr, i, :],
                        in_=d_sc[g0:g0 + ngr, nb:nb + 512].unsqueeze(1).broadcast_to([ngr, 16, 512]))),
                        waits=(wld if firstg else ()), inc=sSD[buf], inc_n=16)
                    firstg = False
            LD_SED = sSD[buf].v

            wdq = [(sQD[buf], LD_C), (sSD[buf], LD_SED), (sHC[buf], LD_HT4)]
            if gidx >= 2:
                wdq.append((semPE, pe9[gidx - 2]))
            js = [set(seg[2] for seg in _down_segments(c2)) for c2 in c2s]
            if all(len(s) == 1 for s in js) and len(set.union(*js)) == 1:
                j = js[0].pop()
                gn = len(c2s)
                E("vector", (lambda e, buf=buf, j=j, gn=gn: e.tensor_scalar(
                    w4d4[buf][:, 0:gn, :], qwd4[buf][:, 0:gn, :], int(4 * j), 15,
                    op0=ALU.logical_shift_right, op1=ALU.bitwise_and)),
                    waits=wdq, inc=semV)
            else:
                firstd = True
                for i, c2 in enumerate(c2s):
                    segs = _down_segments(c2)
                    if len(segs) == 1:
                        j = segs[0][2]
                        E("vector", (lambda e, buf=buf, i=i, j=j: e.tensor_scalar(
                            w4d4[buf][:, i, :], qwd4[buf][:, i, :], int(4 * j), 15,
                            op0=ALU.logical_shift_right, op1=ALU.bitwise_and)),
                            waits=(wdq if firstd else ()), inc=semV)
                    else:
                        pstar, _, j2, _ = segs[1]
                        j1 = segs[0][2]
                        E("vector", (lambda e, buf=buf, i=i, j2=j2: e.tensor_scalar(
                            w4d4[buf][64:128, i, :], qwd4[buf][64:128, i, :], int(4 * j2), 15,
                            op0=ALU.logical_shift_right, op1=ALU.bitwise_and)),
                            waits=(wdq if firstd else ()), inc=semV)
                        if pstar == 32:
                            E("vector", (lambda e, buf=buf, i=i, j2=j2: e.tensor_scalar(
                                w4d4[buf][32:64, i, :], qwd4[buf][32:64, i, :], int(4 * j2), 15,
                                op0=ALU.logical_shift_right, op1=ALU.bitwise_and)),
                                inc=semV)
                        E("vector", (lambda e, buf=buf, i=i, pstar=pstar, j1=j1: e.tensor_scalar(
                            w4d4[buf][0:pstar, i, :], qwd4[buf][0:pstar, i, :], int(4 * j1), 15,
                            op0=ALU.logical_shift_right, op1=ALU.bitwise_and)),
                            inc=semV)
                    firstd = False
            gn = len(c2s)
            E("vector", (lambda e, buf=buf, gn=gn: e.tensor_mul(
                wd4[buf][:, 0:gn, :], w4d4[buf][:, 0:gn, :], sed4[buf][:, 0:gn, :])),
                inc=semV)
            vdeq9[gidx] = semV.v

        def emit_mms9(gidx):
            ns, cg, c2s = groups[gidx]
            buf = gidx % 2
            for i, c2 in enumerate(c2s):
                for mt in range(8):
                    wmm = []
                    if i == 0 and mt == 0:
                        wmm.append((semV, vdeq9[gidx]))
                        if cg == 0 and ns >= 1:
                            wmm.append((semA, a_g[ns - 1]))
                    E("tensor", (lambda e, buf=buf, i=i, mt=mt, c2=c2: e.matmul(
                        ps9[mt][:], htc4[buf][:, i, mt * 128:(mt + 1) * 128], wd4[buf][:, i, :],
                        start=(c2 == 0), stop=False)), waits=wmm)
            em.mark_last("tensor", semPE)
            pe9[gidx] = semPE.v

        def emit_ns_tail(ns):
            nb = ns * 512
            for mt in range(8):
                E("tensor", (lambda e, mt=mt, nb=nb: e.matmul(
                    ps9[mt][:], xgd_sb[:, mt * 128:(mt + 1) * 128],
                    bdown_sb[:, nb:nb + 512], start=False, stop=True)))
            em.mark_last("tensor", semPE)
            PE_NS = semPE.v
            for mt in range(8):
                gb = mt % 2
                wsl = [(semPE, PE_NS)]
                if mt >= 2:
                    wsl.append((sGST[mt % 2], st_g[(ns, mt - 2)]))
                elif ns >= 1:
                    wsl.append((sGST[mt % 2], st_g[(ns - 1, 6 + mt)]))
                E("scalar", (lambda e, mt=mt, gb=gb: e.activation(
                    gtile[gb][:], ps9[mt][:], AF.Silu)), waits=wsl, inc=semA)
                a_g[ns] = semA.v
                A_G = semA.v
                E("sync", (lambda e, mt=mt, gb=gb, nb=nb: e.dma_start(
                    out=g_out[mt * 128:(mt + 1) * 128, nb:nb + 512], in_=gtile[gb][:])),
                    waits=[(semA, A_G)], inc=sGST[gb], inc_n=16)
                st_g[(ns, mt)] = sGST[gb].v

        emit_loads_deq9(0)
        for gidx in range(len(groups)):
            if gidx + 1 < len(groups):
                emit_loads_deq9(gidx + 1)
            emit_mms9(gidx)
            if groups[gidx][1] == NCG - 1:
                emit_ns_tail(groups[gidx][0])
        em.barrier()

    _finish(nc, em, store_sems)
    glob.close()
    return nc


def _finish(nc, em, store_sems):
    if not isinstance(store_sems, (list, tuple)):
        store_sems = [store_sems]
    with nc.Block() as block:
        @block.sync
        def _(sync):
            em.replay("sync", sync)
            for s in store_sems:
                if s.v > 0:
                    sync.wait_ge(s.h, s.v)

        @block.gpsimd
        def _(gpsimd):
            em.replay("gpsimd", gpsimd)

        @block.vector
        def _(vector):
            em.replay("vector", vector)

        @block.scalar
        def _(scalar):
            em.replay("scalar", scalar)

        @block.tensor
        def _(tensor):
            em.replay("tensor", tensor)



def build_launch_b():
    nc = bass.Bass("TRN2", target_bir_lowering=False, debug=False)
    o_in = nc.dram_tensor("o_in", [TOK, D], BF16, kind="ExternalInput").ap()
    g_in = nc.dram_tensor("g_in", [TOK, D], F32, kind="ExternalInput").ap()
    x_in = nc.dram_tensor("x_in", [TOK, D], F32, kind="ExternalInput").ap()
    wo = nc.dram_tensor("wo", [D, D], F32, kind="ExternalInput").ap()
    ident = nc.dram_tensor("ident", [128, 128], BF16, kind="ExternalInput").ap()
    out_c = nc.dram_tensor("out_c", [TOK, D], F32, kind="ExternalOutput").ap()

    st = contextlib.ExitStack()
    T = lambda n, s, d: st.enter_context(nc.sbuf_tensor(n, s, d))
    P = lambda n, s, d: st.enter_context(nc.psum_tensor(n, s, d))

    ident_sb = T("ident_sb", [128, 128], BF16)
    ot_sb = T("ot_sb", [128, NCH, 1024], BF16)
    otile = [T(f"otile{i}", [128, D], BF16) for i in range(2)]
    wo_sb = [T(f"wo_sb{i}", [128, 8, 512], BF16) for i in range(2)]
    gt_sb = [T(f"gt{i}", [128, 512], F32) for i in range(2)]
    xt_sb = [T(f"xtb{i}", [128, 512], F32) for i in range(2)]
    res_sb = [T(f"res{i}", [128, 512], F32) for i in range(2)]
    tmp_sb = T("tmp_sb", [128, 512], F32)

    _allsems = []
    def mksem(n):
        s = SemCnt(st.enter_context(nc.semaphore(n)))
        _allsems.append(s)
        return s
    semPEb_ = None
    semPE = mksem("semPEb")
    semV = mksem("semVb")
    semA = mksem("semAb")
    em = Emitter(own_sems={"vector": semV, "scalar": semA})
    em.all_sems = _allsems
    sIDB = mksem("sIDB")
    sOB = [mksem("sOB0"), mksem("sOB1")]
    sWOB = [mksem("sWOB0"), mksem("sWOB1")]
    sGXB = [mksem("sGXB0"), mksem("sGXB1")]
    sOUTB = [mksem("sOUTB0"), mksem("sOUTB1")]
    E = em.emit

    E("sync", lambda e: e.dma_start(out=ident_sb[:], in_=ident), inc=sIDB, inc_n=16)
    S_ID = sIDB.v

    with nc.psum_tensor("psb_t", [128, 1024], BF16) as ps_t:
        v_tr = {}
        for m in range(8):
            buf = m % 2
            lw = [(sIDB, S_ID)]
            if m >= 2:
                lw.append((semV, v_tr[(m - 2, 3)]))
            E("sync", (lambda e, m=m, buf=buf: e.dma_start(
                out=otile[buf][:], in_=o_in[m * 128:(m + 1) * 128, :])),
                waits=lw, inc=sOB[buf], inc_n=16)
            LD_O = sOB[buf].v
            for rnd in range(4):
                wt = [(sOB[buf], LD_O)]
                if m == 0 and rnd == 0:
                    wt.append((sIDB, S_ID))
                if (m, rnd - 1) in v_tr:
                    wt.append((semV, v_tr[(m, rnd - 1)]))
                elif m >= 1 and rnd == 0:
                    wt.append((semV, v_tr[(m - 1, 3)]))
                for k in range(8):
                    ci = rnd * 8 + k
                    E("tensor", (lambda e, buf=buf, ci=ci, k=k: e.transpose(
                        ps_t[:, k * 128:(k + 1) * 128],
                        otile[buf][:, ci * 128:(ci + 1) * 128], ident_sb[:])),
                        waits=(wt if k == 0 else ()))
                em.mark_last("tensor", semPE)
                PE_T = semPE.v
                E("vector", (lambda e, rnd=rnd, m=m: e.tensor_copy(
                    ot_sb[:, rnd * 8:(rnd + 1) * 8, m * 128:(m + 1) * 128],
                    ps_t.rearrange("p (c n) -> p c n", c=8))),
                    waits=[(semPE, PE_T)], inc=semV)
                v_tr[(m, rnd)] = semV.v
        em.barrier()

    psb = [P(f"psb{i}", [128, 512], F32) for i in range(8)]
    v_done = {}
    st_out = {}
    pe_grp = {}
    for ns in range(8):
        nb = ns * 512
        for cg in range(4):
            buf = cg % 2
            lw = []
            if (ns, cg - 2) in pe_grp:
                lw.append((semPE, pe_grp[(ns, cg - 2)]))
            elif ns >= 1 and cg < 2:
                lw.append((semPE, pe_grp[(ns - 1, 2 + cg)]))
            E("gpsimd", (lambda e, buf=buf, cg=cg, nb=nb: e.dma_start(
                out=wo_sb[buf][:],
                in_=wo[cg * 1024:(cg + 1) * 1024, nb:nb + 512].rearrange("(c p) n -> p c n", p=128))),
                waits=lw, inc=sWOB[buf], inc_n=16)
            LD_WO = sWOB[buf].v
            for ci in range(8):
                for mt in range(8):
                    w0 = []
                    if ci == 0 and mt == 0:
                        w0 = [(sWOB[buf], LD_WO)]
                        if cg == 0 and ns >= 1:
                            w0.append((semV, v_done[ns - 1]))
                    E("tensor", (lambda e, buf=buf, cg=cg, ci=ci, mt=mt: e.matmul(
                        psb[mt][:], ot_sb[:, cg * 8 + ci, mt * 128:(mt + 1) * 128],
                        wo_sb[buf][:, ci, :],
                        start=(cg == 0 and ci == 0), stop=(cg == 3 and ci == 7))),
                        waits=w0)
            em.mark_last("tensor", semPE)
            pe_grp[(ns, cg)] = semPE.v
        PE_NS = semPE.v

        for mt in range(8):
            gb = mt % 2
            wld = [(semPE, PE_NS)]
            if mt >= 2:
                wld.append((sOUTB[gb], st_out[(ns, mt - 2)]))
            elif ns >= 1:
                wld.append((sOUTB[gb], st_out[(ns - 1, 6 + mt)]))
            E("sync", (lambda e, mt=mt, gb=gb, nb=nb: e.dma_start(
                out=gt_sb[gb][:], in_=g_in[mt * 128:(mt + 1) * 128, nb:nb + 512])),
                waits=wld, inc=sGXB[gb], inc_n=16)
            E("sync", (lambda e, mt=mt, gb=gb, nb=nb: e.dma_start(
                out=xt_sb[gb][:], in_=x_in[mt * 128:(mt + 1) * 128, nb:nb + 512])),
                inc=sGXB[gb], inc_n=16)
            LD_GX = sGXB[gb].v
            E("vector", (lambda e, mt=mt, gb=gb: e.tensor_mul(tmp_sb[:], psb[mt][:], gt_sb[gb][:])),
              waits=[(sGXB[gb], LD_GX)], inc=semV)
            E("vector", (lambda e, gb=gb: e.tensor_add(res_sb[gb][:], tmp_sb[:], xt_sb[gb][:])),
              inc=semV)
            v_done[ns] = semV.v
            V_R = semV.v
            E("sync", (lambda e, mt=mt, gb=gb, nb=nb: e.dma_start(
                out=out_c[mt * 128:(mt + 1) * 128, nb:nb + 512], in_=res_sb[gb][:])),
                waits=[(semV, V_R)], inc=sOUTB[gb], inc_n=16)
            st_out[(ns, mt)] = sOUTB[gb].v
    em.barrier()

    _finish(nc, em, sOUTB)
    st.close()
    return nc



def _host_consts():
    bf = ml_dtypes.bfloat16
    ident = np.eye(128, dtype=np.float32).astype(bf)
    identf = np.eye(128, dtype=np.float32)

    iidx = np.arange(128)[:, None]
    uidx = np.arange(896)[None, :]
    maskbin = (iidx <= uidx - 384).astype(np.float32).astype(bf)

    inv = 1.0 / (10000.0 ** (np.arange(0, ROT, 2) / ROT))
    t = np.arange(S, dtype=np.float64)
    sinus = np.outer(t, inv)
    cos64 = np.repeat(np.cos(sinus), 2, axis=1).astype(np.float32)
    sin64 = np.repeat(np.sin(sinus), 2, axis=1).astype(np.float32)
    rope_cos = np.ones((S, HPC * HD), np.float32)
    rope_sin = np.zeros((S, HPC * HD), np.float32)
    for h in range(HPC):
        rope_cos[:, h * HD:h * HD + ROT] = cos64
        rope_sin[:, h * HD:h * HD + ROT] = sin64

    e_gate = np.zeros((NCH, 128, NG), dtype=np.float32)
    for c2 in range(NCH):
        j, r = c2 // 4, c2 % 4
        k8 = r * 128 + np.arange(128)
        G = (8 * k8 + j) // 128
        e_gate[c2, np.arange(128), G] = -1.0
    e_gate = e_gate.reshape(NCH * 128, NG).astype(bf)

    e_down = np.zeros((NGD, 128, NGD), dtype=np.float32)
    dshift = np.zeros((128, NGD), dtype=np.int32)
    for c2 in range(NGD):
        kpp = 128 * c2 + np.arange(128)
        j = kpp // KP8D
        k8 = kpp % KP8D
        G = (8 * k8 + j) // 128
        e_down[c2, np.arange(128), G] = 1.0
        dshift[:, c2] = 4 * j
    e_down = e_down.reshape(NGD * 128, NGD).astype(bf)

    e2b = np.zeros((2, 256), dtype=np.float32)
    e2b[0, 0:128] = 1.0
    e2b[1, 128:256] = 1.0
    return dict(ident=ident, identf=identf, maskbin=maskbin, rope_cos=rope_cos,
                rope_sin=rope_sin, e_gate=e_gate, e_down=e_down, dshift=dshift,
                e2b=e2b)



def kernel(L0_attn_hidden_states, L0_attn_Wq, L0_attn_Wk, L0_attn_Wv, L0_attn_Wo,
           L0_ffn_gate_qweight, L0_ffn_gate_scales, L0_ffn_gate_qzeros, L0_ffn_gate_g_idx,
           L0_ffn_up_qweight, L0_ffn_up_scales, L0_ffn_up_qzeros, L0_ffn_up_g_idx,
           L0_ffn_down_qweight, L0_ffn_down_scales, L0_ffn_down_qzeros, L0_ffn_down_g_idx):
    x = np.ascontiguousarray(np.asarray(L0_attn_hidden_states, dtype=np.float32).reshape(S, D))
    xT = np.ascontiguousarray(x.T)
    consts = _host_consts()

    nc_a = build_launch_a()
    in_maps_a = []
    for c in range(NCORE):
        hs = c * HPC * HD
        m = dict(
            xT=xT,
            x_c=np.ascontiguousarray(x[c * TOK:(c + 1) * TOK, :]),
            wq_s=np.ascontiguousarray(np.asarray(L0_attn_Wq, np.float32)[:, hs:hs + HPC * HD]),
            wk_s=np.ascontiguousarray(np.asarray(L0_attn_Wk, np.float32)[:, hs:hs + HPC * HD]),
            wv_s=np.ascontiguousarray(np.asarray(L0_attn_Wv, np.float32)[:, hs:hs + HPC * HD]),
            rope_cos=consts["rope_cos"], rope_sin=consts["rope_sin"],
            g_qw=np.ascontiguousarray(np.asarray(L0_ffn_gate_qweight, np.int32)),
            u_qw=np.ascontiguousarray(np.asarray(L0_ffn_up_qweight, np.int32)),
            d_qw=np.ascontiguousarray(np.asarray(L0_ffn_down_qweight, np.int32)),
            g_sc=np.ascontiguousarray(np.asarray(L0_ffn_gate_scales, np.float32)),
            u_sc=np.ascontiguousarray(np.asarray(L0_ffn_up_scales, np.float32)),
            d_sc=np.ascontiguousarray(np.asarray(L0_ffn_down_scales, np.float32)),
            g_qz=np.ascontiguousarray(np.asarray(L0_ffn_gate_qzeros, np.int32)),
            u_qz=np.ascontiguousarray(np.asarray(L0_ffn_up_qzeros, np.int32)),
            d_qz=np.ascontiguousarray(np.asarray(L0_ffn_down_qzeros, np.int32)),
            ident=consts["ident"], identf=consts["identf"], maskbin=consts["maskbin"],
            e_gate=consts["e_gate"], e_down=consts["e_down"], dshift=consts["dshift"],
            e2b=consts["e2b"],
        )
        in_maps_a.append(m)

    res_a = run_bass_kernel_spmd(nc_a, in_maps_a, core_ids=list(range(NCORE)))
    o_full = np.concatenate([np.asarray(res_a.results[c]["o_out"]) for c in range(NCORE)], axis=1)
    g_full = [np.asarray(res_a.results[c]["g_out"]) for c in range(NCORE)]

    nc_b = build_launch_b()
    in_maps_b = []
    wo = np.ascontiguousarray(np.asarray(L0_attn_Wo, np.float32))
    for c in range(NCORE):
        in_maps_b.append(dict(
            o_in=np.ascontiguousarray(o_full[c * TOK:(c + 1) * TOK, :]),
            g_in=np.ascontiguousarray(g_full[c]),
            x_in=np.ascontiguousarray(x[c * TOK:(c + 1) * TOK, :]),
            wo=wo, ident=consts["ident"],
        ))
    res_b = run_bass_kernel_spmd(nc_b, in_maps_b, core_ids=list(range(NCORE)))
    out = np.concatenate([np.asarray(res_b.results[c]["out_c"]) for c in range(NCORE)], axis=0)
    return out.reshape(1, S, D).astype(np.float32)
